# version 09
# baseline (speedup 1.0000x reference)
import contextlib
import numpy as np
import ml_dtypes
import concourse.bass as bass
import concourse.mybir as mybir
from concourse.bass_utils import run_bass_kernel_spmd
from concourse.bass import IndirectOffsetOnAxis

F32 = mybir.dt.float32
BF16 = mybir.dt.bfloat16
U32 = mybir.dt.uint32
AF = mybir.ActivationFunctionType
ALU = mybir.AluOpType

NCORE = 8
D = 1024
NB = 8
SEQ = 4096
BLK = 64
NT = 4096
NCH = 8
TILE = 512
NTILES = NT // TILE
E = 8
FF = 4096
NFF = FF // 128
CAP = 1152
NCAPCH = CAP // 128
KSEL = 8192
BIG = 16777216.0
N_BISECT = 28

bfd = ml_dtypes.bfloat16
_BASS_KW = {}
FAKE_COLL = False


def _mknc():
    return bass.Bass("TRN2", target_bir_lowering=False, **_BASS_KW)



def shard_x(x):
    xr = np.ascontiguousarray(x, dtype=np.float32).reshape(NB, 64, BLK, D)
    out = []
    for c in range(NCORE):
        xc = xr[:, c::8]
        xc = np.transpose(xc, (3, 0, 1, 2))
        out.append(np.ascontiguousarray(xc.reshape(D, NT)))
    return out


def unshard_x(parts):
    x = np.zeros((NB, 64, BLK, D), dtype=np.float32)
    for c in range(NCORE):
        xc = np.asarray(parts[c], np.float32).reshape(D, NB, 8, BLK)
        x[:, c::8] = np.transpose(xc, (1, 2, 3, 0))
    return x.reshape(NB, SEQ, D)


def vec_pp(v):
    return np.ascontiguousarray(np.asarray(v, np.float32).reshape(NCH, 128).T)


def host_constants(core):
    cst = {}
    cst["ident"] = np.eye(128, dtype=np.float32)
    cst["ltm"] = (np.arange(128)[:, None] < np.arange(128)[None, :]).astype(np.float32)
    c_ = np.arange(512)
    qglob = ((8 * (c_ // BLK) + core) * BLK + c_ % BLK).astype(np.float32)
    cst["qglob"] = np.ascontiguousarray(np.tile(qglob[None, :], (128, 1)))
    kg = np.zeros((128, 32), np.float32)
    for cc in range(32):
        r = cc // 4
        j = 128 * (cc % 4) + np.arange(128)
        kg[:, cc] = (8 * (j // BLK) + r) * BLK + j % BLK
    cst["kglob"] = kg
    cst["iota32"] = (np.arange(128)[:, None] + 128.0 * np.arange(32)[None, :]).astype(np.float32)
    pref = np.zeros((E * CAP, 2), np.float32)
    pref[:, 0] = NT + (np.arange(E * CAP) % 128)
    cst["exmeta0"] = pref
    m = np.zeros((128, 8), np.float32)
    m[:, 0] = -4.0
    m[:, 1] = 1e-5
    m[:, 2] = 1e-6
    m[:, 3] = 1.0
    m[:, 4] = 0.5
    cst["misc"] = m
    cst["sel2"] = (np.arange(128)[:, None] // 64 == np.arange(2)[None, :]).astype(np.float32)
    selB2 = np.zeros((128, 128), np.float32)
    selB2[0:2] = cst["sel2"].T
    selB2[64:66] = cst["sel2"].T
    cst["selB2"] = selB2
    em = np.zeros((128, 18), np.float32)
    r = np.arange(128)[:, None] * 9 + np.arange(9)[None, :]
    em[:, 0::2] = NT + (r % 128)
    cst["exmeta0s"] = em
    return cst


def host_weights(inputs):
    w = {}
    for nm in ("wf", "wi", "wc", "uf", "ui", "uc"):
        w[f"l0_{nm}T"] = np.ascontiguousarray(np.asarray(inputs[f"L0_ffn_{nm}"], np.float32).T)
    w["l0_woT"] = np.ascontiguousarray(np.asarray(inputs["L0_ffn_wo"], np.float32).T)
    w["l0_uoT"] = np.ascontiguousarray(np.asarray(inputs["L0_ffn_uo"], np.float32).T)
    w["l0_bf"] = vec_pp(inputs["L0_ffn_bf"])
    w["l0_bi"] = vec_pp(inputs["L0_ffn_bi"])
    w["l0_bc"] = vec_pp(inputs["L0_ffn_bc"])
    w["l0_bo"] = vec_pp(inputs["L0_ffn_bo"])
    w["l0_inc"] = vec_pp(inputs["L0_ffn_in_c"])
    w["l0_wq"] = np.ascontiguousarray(np.asarray(inputs["L0_attn_Wq"], np.float32))
    w["l0_wk"] = np.ascontiguousarray(np.asarray(inputs["L0_attn_Wk"], np.float32))
    w["l0_wv"] = np.ascontiguousarray(np.asarray(inputs["L0_attn_Wv"], np.float32))
    w["l1_wq"] = np.ascontiguousarray(np.asarray(inputs["L1_attn_Wq"], np.float32))
    w["l1_wk"] = np.ascontiguousarray(np.asarray(inputs["L1_attn_Wk"], np.float32))
    w["l1_wv"] = np.ascontiguousarray(np.asarray(inputs["L1_attn_Wv"], np.float32))
    w["l1_wo"] = np.ascontiguousarray(np.asarray(inputs["L1_attn_Wo"], np.float32))
    for nm in ("bq", "bk", "bv", "bo"):
        w[f"l1_{nm}"] = np.asarray(inputs[f"L1_attn_{nm}"], np.float32).reshape(1, D).copy()
    w["l1_qns"] = np.tile(np.asarray(inputs["L1_attn_qn_scale"], np.float32), 2).reshape(128, 1).copy()
    w["l1_qnb"] = np.tile(np.asarray(inputs["L1_attn_qn_bias"], np.float32), 2).reshape(128, 1).copy()
    w["l1_kns"] = np.tile(np.asarray(inputs["L1_attn_kn_scale"], np.float32), 2).reshape(128, 1).copy()
    w["l1_knb"] = np.tile(np.asarray(inputs["L1_attn_kn_bias"], np.float32), 2).reshape(128, 1).copy()
    w["l1_gamma"] = vec_pp(inputs["L1_attn_gamma"])
    w["l1_initb"] = np.full((128, 1), float(np.asarray(inputs["L1_attn_init_bias"]).reshape(())), np.float32)
    w["l1_wg"] = np.ascontiguousarray(np.asarray(inputs["L1_ffn_Wg"], np.float32))
    w["l1_w1"] = np.ascontiguousarray(np.asarray(inputs["L1_ffn_W1"], np.float32))
    w["l1_b1"] = np.ascontiguousarray(np.asarray(inputs["L1_ffn_b1"], np.float32))
    w["l1_w2"] = np.ascontiguousarray(np.asarray(inputs["L1_ffn_W2"], np.float32))
    w["l1_b2"] = np.ascontiguousarray(np.asarray(inputs["L1_ffn_b2"], np.float32))
    return w


WEIGHT_SPECS = {k: None for k in (
    "l0_wfT l0_wiT l0_wcT l0_ufT l0_uiT l0_ucT l0_woT l0_uoT l0_bf l0_bi l0_bc "
    "l0_bo l0_inc l0_wq l0_wk l0_wv l1_wq l1_wk l1_wv l1_wo l1_bq l1_bk l1_bv "
    "l1_bo l1_qns l1_qnb l1_kns l1_knb l1_gamma l1_initb l1_wg l1_w1 l1_b1 "
    "l1_w2 l1_b2").split()}



def _inst_kind(inst):
    ins = inst.ins
    op = str(ins.opcode)
    if "DMA" in op and "Pool" in str(ins.engine):
        return "sw"
    return "hw"


class Sem:

    def __init__(self, ctx, name, kind="hw"):
        self.h = ctx.es.enter_context(ctx.nc.semaphore(name))
        self.n = 0
        self.base = 0
        self.kind = kind
        self.audit = getattr(ctx, "fused", False)

    def rebase(self):
        self.base = self.n

    def inc(self, inst, k=1):
        if self.audit and self.kind != "coll":
            ik = _inst_kind(inst)
            assert ik == self.kind, \
                f"sem kind mismatch: {self.h} declared {self.kind} got {ik} ({inst.ins.opcode})"
        inst.then_inc(self.h, k)
        self.n += k
        return self.n

    def dma(self, inst):
        return self.inc(inst, 16)

    def wait(self, eng, val=None):
        eng.wait_ge(self.h, self.n if val is None else (self.base + val))

    def wait_abs(self, eng, val):
        eng.wait_ge(self.h, val)


def phase_sem(ctx, name, kind="hw"):
    if getattr(ctx, "fused", False):
        if not hasattr(ctx, "sem_pools"):
            ctx.sem_pools = {"hw": [], "sw": [], "coll": []}
        pool = ctx.sem_pools[kind]
        slot = ctx.sem_slots[kind]
        ctx.sem_slots[kind] += 1
        while len(pool) <= slot:
            pool.append(Sem(ctx, f"ps_{kind}{len(pool)}", kind=kind))
        sm = pool[slot]
        sm.rebase()
        return sm
    if not hasattr(ctx, "sem_cache"):
        ctx.sem_cache = {}
    if name not in ctx.sem_cache:
        ctx.sem_cache[name] = Sem(ctx, name)
    sm = ctx.sem_cache[name]
    sm.rebase()
    return sm


class DPipe:

    def __init__(self, ctx, name, depth=2):
        self.depth = depth
        self.sems = [Sem(ctx, f"{name}_{i}") for i in range(depth)]

    def rebase(self):
        for sm in self.sems:
            sm.rebase()

    def mark(self, i, inst):
        self.sems[i % self.depth].inc(inst, 16)

    def wait(self, eng, i):
        sm = self.sems[i % self.depth]
        eng.wait_ge(sm.h, sm.base + 16 * (i // self.depth + 1))


class _SlotPipe:
    def __init__(self, sems, depth):
        self.depth = depth
        self.sems = sems

    def rebase(self):
        for sm in self.sems:
            sm.rebase()

    def mark(self, i, inst):
        self.sems[i % self.depth].inc(inst, 16)

    def wait(self, eng, i):
        sm = self.sems[i % self.depth]
        eng.wait_ge(sm.h, sm.base + 16 * (i // self.depth + 1))


def phase_pipe(ctx, name, depth=2, kind="hw"):
    if getattr(ctx, "fused", False):
        sems = [phase_sem(ctx, f"{name}_{i}", kind=kind) for i in range(depth)]
        return _SlotPipe(sems, depth)
    if not hasattr(ctx, "pipe_cache"):
        ctx.pipe_cache = {}
    key = (name, depth)
    if key not in ctx.pipe_cache:
        ctx.pipe_cache[key] = DPipe(ctx, name, depth)
    pp = ctx.pipe_cache[key]
    pp.rebase()
    return pp


class Ctx:
    pass


_DT_BYTES = {F32: 4, BF16: 2, U32: 4}


def sbt(ctx, name, shape, dtype):
    if not hasattr(ctx, "tile_cache"):
        ctx.tile_cache = {}
    if getattr(ctx, "fused", False) and getattr(ctx, "phase_key", None) is not None:
        key = ("s", ctx.phase_key, name)
        if key not in ctx.tile_cache:
            nbytes = shape[1] * _DT_BYTES[dtype]
            off = (ctx.arena_off + 31) // 32 * 32
            assert off + nbytes <= ctx.arena_size, \
                f"arena overflow: {ctx.phase_key}/{name} needs {nbytes} at {off} of {ctx.arena_size}"
            ctx.arena_off = off + nbytes
            t = ctx.nc.alloc_sbuf_tensor_at(f"{ctx.phase_key}_{name}", shape, dtype,
                                            offset=ctx.arena_base + off)
            ctx.tile_cache[key] = t
        t = ctx.tile_cache[key]
        assert list(t.shape) == list(shape), f"sbt {name} shape mismatch"
        return t
    key = ("s", name)
    if key not in ctx.tile_cache:
        ctx.tile_cache[key] = ctx.es.enter_context(ctx.nc.sbuf_tensor(name, shape, dtype))
    t = ctx.tile_cache[key]
    assert list(t.shape) == list(shape), f"sbt {name} shape mismatch"
    return t


def pst(ctx, name, shape, dtype=F32):
    if not hasattr(ctx, "tile_cache"):
        ctx.tile_cache = {}
    key = ("p", name)
    if key not in ctx.tile_cache:
        ctx.tile_cache[key] = ctx.es.enter_context(ctx.nc.psum_tensor(name, shape, dtype))
    return ctx.tile_cache[key]


def fm3(ap2d):
    return ap2d.rearrange("(c p) n -> p c n", p=128)


def make_ctx(nc, es, block, shapes_weights, uid=""):
    ctx = Ctx()
    ctx.nc, ctx.es, ctx.block = nc, es, block
    ctx.uid = uid
    ctx.win = {}
    for name, arr in shapes_weights.items():
        ctx.win[name] = nc.dram_tensor(name, list(arr.shape), F32, kind="ExternalInput").ap()
    for name in ("ident", "ltm", "qglob", "kglob", "iota32", "exmeta0", "misc"):
        pass
    return ctx


def emit_const_loads(ctx, consts):
    nc, block = ctx.nc, ctx.block
    ctx.cin = {}
    for name, arr in consts.items():
        ctx.cin[name] = nc.dram_tensor(name, list(arr.shape), F32, kind="ExternalInput").ap()
    ctx.ones_bf = sbt(ctx, "c_ones_bf", [128, 128], BF16)
    ctx.ones_row = sbt(ctx, "c_ones_row", [1, TILE], BF16)
    ctx.ones_f32 = sbt(ctx, "c_ones_f32", [128, 128], F32)
    ctx.ident = sbt(ctx, "c_ident", [128, 128], F32)
    ctx.ltm = sbt(ctx, "c_ltm", [128, 128], F32)
    ctx.qglob = sbt(ctx, "c_qglob", [128, 512], F32)
    ctx.kglob = sbt(ctx, "c_kglob", [128, 32], F32)
    ctx.iota32 = sbt(ctx, "c_iota32", [128, 32], F32)
    ctx.misc = sbt(ctx, "c_misc", [128, 8], F32)
    ctx.ps512 = [pst(ctx, f"cps{i}", [128, TILE]) for i in range(6)]
    ctx.s_const = Sem(ctx, "s_const")

    @block.sync
    def _(sync):
        ctx.s_const.dma(sync.dma_start(out=ctx.ident[:, :], in_=ctx.cin["ident"][:, :]))
        ctx.s_const.dma(sync.dma_start(out=ctx.ltm[:, :], in_=ctx.cin["ltm"][:, :]))
        ctx.s_const.dma(sync.dma_start(out=ctx.qglob[:, :], in_=ctx.cin["qglob"][:, :]))
        ctx.s_const.dma(sync.dma_start(out=ctx.kglob[:, :], in_=ctx.cin["kglob"][:, :]))
        ctx.s_const.dma(sync.dma_start(out=ctx.iota32[:, :], in_=ctx.cin["iota32"][:, :]))
        ctx.s_const.dma(sync.dma_start(out=ctx.misc[:, :], in_=ctx.cin["misc"][:, :]))

    @block.vector
    def _(vector):
        vector.memset(ctx.ones_bf[:, :], 1.0)
        vector.memset(ctx.ones_row[:, :], 1.0)
        ctx.s_const.inc(vector.memset(ctx.ones_f32[:, :], 1.0))

    ctx.const_ready = ctx.s_const.n


def barrier(ctx):
    block = ctx.block
    if not hasattr(ctx, "bar_sem"):
        ctx.bar_sem = Sem(ctx, "bar_sem")
    s = ctx.bar_sem
    base = s.n
    engs = ["sync", "vector", "scalar", "tensor", "gpsimd"]

    for e in engs:
        def mk(e=e):
            def f(eng):
                s.inc(eng.engine_nop() if e in ("vector", "gpsimd") else eng.nop())
            return f
        getattr(block, e)(mk())
    for e in engs:
        def mk2(e=e):
            def f(eng):
                eng.wait_ge(s.h, base + 5)
            return f
        getattr(block, e)(mk2())


def begin_phase(ctx, key):
    ctx.phase_key = key
    ctx.sem_slots = {"hw": 0, "sw": 0, "coll": 0}
    ctx.arena_off = 0


def end_phase(ctx):
    sphase, n = ctx.phase_done
    block = ctx.block

    @block.sync
    def _(sync):
        sphase.wait_abs(sync, n)
    barrier(ctx)
    ctx.phase_key = None





LTILE = 256
LNT = NT // LTILE


def emit_ln(ctx, x_src, h_dst, gates=False):
    nc, block = ctx.nc, ctx.block
    u = ctx.uid
    p_lb = phase_pipe(ctx, "lnlb", kind="sw")
    s_sq = phase_sem(ctx, "lnsq")
    s_pe = phase_sem(ctx, "lnpe")
    s_vs = phase_sem(ctx, "lnvs")
    s_sd = phase_sem(ctx, "lnsd")
    s_ap = phase_sem(ctx, "lnap")
    p_st = phase_pipe(ctx, "lnst")
    s_ag2 = phase_sem(ctx, "lnag2")
    s_gw = phase_sem(ctx, "lngw", kind="sw")
    s_gp = phase_sem(ctx, "lngp")
    s_gv = phase_sem(ctx, "lngv")

    xb = sbt(ctx, "lnxb", [128, 2 * NCH * LTILE], BF16)
    xq = sbt(ctx, "lnxq", [128, NCH * LTILE], BF16)
    hst = sbt(ctx, "lnhs", [128, 4 * LTILE], F32)
    hout = sbt(ctx, "lnho", [128, 2 * NCH * LTILE], BF16)
    p_s = pst(ctx, "lnps", [128, LTILE])
    p_s2 = pst(ctx, "lnps2", [128, LTILE])
    xb4 = xb.ap().rearrange("p (b c n) -> p b c n", b=2, c=NCH)
    xq3 = xq.ap().rearrange("p (c n) -> p c n", c=NCH)
    ho4 = hout.ap().rearrange("p (b c n) -> p b c n", b=2, c=NCH)
    if gates:
        wg_s = sbt(ctx, "lnwg", [128, NCH * E], BF16)
        p_g = ctx.ps512[0][:, 0:E]
        ctx.gates_tm = sbt(ctx, "gatetm", [128, 32 * E], F32)
        ctx.gsig_tm = sbt(ctx, "gsigtm", [128, 32 * E], F32)

    if gates:
        @block.gpsimd
        def _(gpsimd):
            s_gw.dma(gpsimd.dma_start(out=wg_s.ap().rearrange("p (c e) -> p c e", c=NCH),
                                      in_=ctx.win["l1_wg"].rearrange("(c p) e -> p c e", p=128)))

    pieces = []
    if getattr(ctx, "m_prefill_pending", False):
        ctx.m_prefill_pending = False
        _, _, _, _, _, w1c, w2c, b12c = ctx.m_dram
        for e in range(E):
            for hh in range(2):
                pieces.append((w1c[e][hh * 512:(hh + 1) * 512, :], ctx.win["l1_w1"][e][hh * 512:(hh + 1) * 512, :]))
                pieces.append((w2c[e][hh * 2048:(hh + 1) * 2048, :], ctx.win["l1_w2"][e][hh * 2048:(hh + 1) * 2048, :]))
            pieces.append((b12c[e:e + 1, 0:FF], ctx.win["l1_b1"][e:e + 1, :]))
            pieces.append((b12c[e:e + 1, FF:FF + D], ctx.win["l1_b2"][e:e + 1, :]))

    @block.gpsimd
    def _(gpsimd):
        ctx.s_const.wait(gpsimd, ctx.const_ready)
        for t in range(LNT):
            b = t % 2
            if t >= 2:
                s_ap.wait(gpsimd, t - 1)
                p_st.wait(gpsimd, t - 2)
                if gates:
                    s_gp.wait(gpsimd, 2 * (t - 1))
            p_lb.mark(t, gpsimd.dma_start(out=xb4[:, b], in_=fm3(x_src)[:, :, t * LTILE:(t + 1) * LTILE]))
            if t >= 1:
                while pieces and len(pieces) > (LNT - 1 - t) * 4:
                    dst, src = pieces.pop(0)
                    ctx.s_mwc.dma(gpsimd.dma_start(out=dst, in_=src))

    @block.scalar
    def _(scalar):
        var = hst[:, 2 * LTILE:3 * LTILE]
        sd = hst[:, 3 * LTILE:4 * LTILE]
        for t in range(LNT):
            b = t % 2
            p_lb.wait(scalar, t)
            if t >= 1:
                s_pe.wait(scalar, t)
            s_sq.inc(scalar.activation(xq[:, :], xb4[:, b].rearrange("p c n -> p (c n)"), AF.Square))
            s_vs.wait(scalar, t + 1)
            s_sd.inc(scalar.activation(sd[:, :], var[:, :], AF.Sqrt, bias=ctx.misc[:, 1:2]))
            if gates:
                for tc in range(LTILE // 128):
                    a = t * (LTILE // 128) + tc
                    s_gp.wait(scalar, a + 1)
                    s_gv.inc(scalar.copy(ctx.gates_tm[:, a * E:(a + 1) * E], p_g))

    if gates:
        wg3 = wg_s.ap().rearrange("p (c e) -> p c e", c=NCH)

    @block.tensor
    def _(tensor):
        if gates:
            s_gw.wait(tensor, 16)
        for t in range(LNT):
            b = t % 2
            s_sq.wait(tensor, t + 1)
            if t >= 1:
                s_vs.wait(tensor, t)
            for c in range(NCH):
                tensor.matmul(p_s[:, :], ctx.ones_bf[:, :], xb4[:, b, c], start=(c == 0), stop=(c == NCH - 1))
            for c in range(NCH):
                i = tensor.matmul(p_s2[:, :], ctx.ones_bf[:, :], xq3[:, c], start=(c == 0), stop=(c == NCH - 1))
            s_pe.inc(i)
            if gates:
                s_ap.wait(tensor, t + 1)
                for tc in range(LTILE // 128):
                    a = t * (LTILE // 128) + tc
                    if a >= 1:
                        s_gv.wait(tensor, a)
                    for c in range(NCH):
                        i = tensor.matmul(p_g, ho4[:, b, c, tc * 128:(tc + 1) * 128],
                                          wg3[:, c], start=(c == 0), stop=(c == NCH - 1))
                    s_gp.inc(i)

    @block.vector
    def _(vector):
        mu = hst[:, 0:LTILE]
        r = hst[:, LTILE:2 * LTILE]
        var = hst[:, 2 * LTILE:3 * LTILE]
        sd = hst[:, 3 * LTILE:4 * LTILE]
        for t in range(LNT):
            b = t % 2
            s_pe.wait(vector, t + 1)
            vector.tensor_scalar_mul(mu[:, :], p_s[:, :], 1.0 / D)
            vector.tensor_scalar_mul(var[:, :], p_s2[:, :], 1.0 / D)
            vector.tensor_tensor(r[:, :], mu[:, :], mu[:, :], ALU.mult)
            s_vs.inc(vector.tensor_sub(var[:, :], var[:, :], r[:, :]))
            s_sd.wait(vector, t + 1)
            vector.reciprocal(r[:, :], sd[:, :])
            if t >= 2:
                p_st.wait(vector, t - 2)
                if gates:
                    s_gp.wait(vector, 2 * (t - 1))
            for c in range(NCH):
                vector.tensor_sub(var[:, :], xb4[:, b, c], mu[:, :])
                i = vector.tensor_tensor(ho4[:, b, c], var[:, :], r[:, :], ALU.mult)
            s_ap.inc(i)

    @block.sync
    def _(sync):
        ctx.ln_pipe = p_st
        for t in range(LNT):
            b = t % 2
            s_ap.wait(sync, t + 1)
            p_st.mark(t, sync.dma_start(out=fm3(h_dst)[:, :, t * LTILE:(t + 1) * LTILE], in_=ho4[:, b]))

    if gates:
        g3 = ctx.gates_tm.ap().rearrange("p (a e) -> p a e", e=E)

        @block.scalar
        def _(scalar):
            s_gv.wait(scalar, 32)
            s_gv.inc(scalar.activation(ctx.gsig_tm[:, :], ctx.gates_tm[:, :], AF.Sigmoid))

        @block.sync
        def _(sync):
            s_gv.wait(sync)
            s_ag2.dma(sync.dma_start(out=ctx.ag_gin.rearrange("(a p) e -> p a e", p=128), in_=g3))
            ctx.gates_mark = (s_ag2, s_ag2.n)


def ln_wait_cols(ctx, eng, hi_col):
    t = (hi_col + LTILE - 1) // LTILE - 1
    ctx.ln_pipe.wait(eng, t)
    if t >= 1:
        ctx.ln_pipe.wait(eng, t - 1)



def emit_l0ffn(ctx, h_src, x_src, x_dst, nsteps=NB):
    nc, block = ctx.nc, ctx.block
    u = ctx.uid
    s_w = phase_sem(ctx, "f0w", kind="sw")
    p_h = phase_pipe(ctx, "f0h")
    s_cb = phase_sem(ctx, "f0cb")
    s_pg = phase_sem(ctx, "f0pg")
    s_ag = phase_sem(ctx, "f0ag")
    s_cu = phase_sem(ctx, "f0cu")
    s_uo = phase_sem(ctx, "f0uo")
    s_x = phase_sem(ctx, "f0x")
    s_o = phase_sem(ctx, "f0o")
    s_oa = phase_sem(ctx, "f0oa")
    s_ra = phase_sem(ctx, "f0ra")
    s_r = phase_sem(ctx, "f0r", kind="sw")
    p_h2 = phase_pipe(ctx, "f0h2x")

    wnames = ["wfT", "ufT", "wiT", "uiT", "wcT", "ucT"]
    wt = {n: sbt(ctx, f"f0{n}", [128, NCH * D], BF16) for n in wnames}
    wt["woT"] = wt["wfT"]
    wt["uoT"] = wt["ufT"]
    bias = sbt(ctx, "f0bias", [128, 4 * NCH], F32)
    incs = sbt(ctx, "f0incs", [128, NCH], F32)
    cst = sbt(ctx, "f0c", [128, NCH * TILE], F32)
    cbf = sbt(ctx, "f0cbf", [128, NCH * TILE], BF16)
    hbuf = sbt(ctx, "f0h", [128, 2 * NCH * TILE], BF16)
    hbuf2 = hbuf
    gsc = sbt(ctx, "f0g", [128, 3 * TILE], F32)
    uout = sbt(ctx, "f0uo", [128, NCH * TILE], F32)
    oscr = sbt(ctx, "f0os", [128, 2 * TILE], F32)
    xbuf = sbt(ctx, "f0x", [128, NCH * TILE], F32)
    p_gA = ctx.ps512[0:3]
    p_gB = ctx.ps512[3:6]
    p_out = [p_gB[0], p_gB[1]]
    c3 = cst.ap().rearrange("p (c n) -> p c n", c=NCH)
    cb3 = cbf.ap().rearrange("p (c n) -> p c n", c=NCH)
    h4 = hbuf.ap().rearrange("p (b c n) -> p b c n", b=2, c=NCH)
    h4b = hbuf2.ap().rearrange("p (b c n) -> p b c n", b=2, c=NCH)
    uo3 = uout.ap().rearrange("p (c n) -> p c n", c=NCH)
    x3 = xbuf.ap().rearrange("p (c n) -> p c n", c=NCH)
    w3 = {n: wt[n].ap().rearrange("p (c m) -> p c m", c=NCH) for n in wt}

    s_w2 = phase_sem(ctx, "f0w2", kind="sw")

    @block.gpsimd
    def _(gpsimd):
        for n in wnames:
            s_w.dma(gpsimd.dma_start(out=w3[n], in_=ctx.win[f"l0_{n}"].rearrange("(c p) m -> p c m", p=128)))

    s_wb = phase_sem(ctx, "f0wb")
    s_ic = phase_sem(ctx, "f0ic")

    @block.sync
    def _(sync):
        s_wb.dma(sync.dma_start(out=bias[:, 0:NCH], in_=ctx.win["l0_bf"][:, :]))
        s_wb.dma(sync.dma_start(out=bias[:, NCH:2 * NCH], in_=ctx.win["l0_bi"][:, :]))
        s_wb.dma(sync.dma_start(out=bias[:, 2 * NCH:3 * NCH], in_=ctx.win["l0_bc"][:, :]))
        s_wb.dma(sync.dma_start(out=bias[:, 3 * NCH:4 * NCH], in_=ctx.win["l0_bo"][:, :]))
        s_ic.dma(sync.dma_start(out=incs[:, :], in_=ctx.win["l0_inc"][:, :]))
        for t in range(nsteps):
            b = t % 2
            ln_wait_cols(ctx, sync, (t + 1) * TILE)
            if t >= 2:
                s_pg.wait(sync, 3 * NCH * (t - 1))
            p_h.mark(t, sync.dma_start(out=h4[:, b], in_=fm3(h_src)[:, :, t * TILE:(t + 1) * TILE]))

    @block.vector
    def _(vector):
        s_ic.wait(vector, 16)
        for c in range(NCH):
            vector.memset(c3[:, c], 0.0)
        i = None
        for c in range(NCH):
            i = vector.tensor_scalar(c3[:, c], c3[:, c], incs[:, c:c + 1], None, ALU.add)
        s_cu.inc(i)

    ncu = 1
    npgW = 0

    for t in range(nsteps):
        bh = t % 2

        @block.vector
        def _(vector, ncu=ncu):
            s_cu.wait(vector, ncu)
            s_cb.inc(vector.tensor_copy(cbf[:, :], cst[:, :]))

        @block.tensor
        def _(tensor, t=t, bh=bh):
            s_w.wait(tensor, 16 * 6)
            p_h.wait(tensor, t)
            s_cb.wait(tensor, t + 1)
            for m in range(NCH):
                g = t * NCH + m
                pg = p_gA if m % 2 == 0 else p_gB
                if g >= 2:
                    s_ag.wait(tensor, 3 * (g - 2) + 3)
                for gi, (wn, un) in enumerate((("wfT", "ufT"), ("wiT", "uiT"), ("wcT", "ucT"))):
                    for c in range(NCH):
                        tensor.matmul(pg[gi][:, :], w3[wn][:, c, m * 128:(m + 1) * 128],
                                      h4[:, bh, c], start=(c == 0), stop=False)
                    for c in range(NCH):
                        i = tensor.matmul(pg[gi][:, :], w3[un][:, c, m * 128:(m + 1) * 128],
                                          cb3[:, c], start=False, stop=(c == NCH - 1))
                    s_pg.inc(i)

        @block.scalar
        def _(scalar, t=t):
            s_wb.wait(scalar, 64)
            for m in range(NCH):
                g = t * NCH + m
                pg = p_gA if m % 2 == 0 else p_gB
                s_pg.wait(scalar, 3 * g + 3)
                if g >= 1:
                    s_cu.wait(scalar, 1 + 3 * g)
                scalar.activation(gsc[:, 0:TILE], pg[0][:, :], AF.Sigmoid, bias=bias[:, m:m + 1])
                scalar.activation(gsc[:, TILE:2 * TILE], pg[1][:, :], AF.Sigmoid, bias=bias[:, NCH + m:NCH + m + 1])
                i = scalar.activation(gsc[:, 2 * TILE:3 * TILE], pg[2][:, :], AF.Tanh, bias=bias[:, 2 * NCH + m:2 * NCH + m + 1])
                s_ag.inc(i, 3)

        @block.vector
        def _(vector, t=t):
            for m in range(NCH):
                g = t * NCH + m
                s_ag.wait(vector, 3 * g + 3)
                vector.tensor_tensor(gsc[:, TILE:2 * TILE], gsc[:, TILE:2 * TILE], gsc[:, 2 * TILE:3 * TILE], ALU.mult)
                vector.tensor_tensor(c3[:, m], c3[:, m], gsc[:, 0:TILE], ALU.mult)
                i = vector.tensor_add(c3[:, m], c3[:, m], gsc[:, TILE:2 * TILE])
                s_cu.inc(i, 3)

        ncu += 3 * NCH
        npgW += 3 * NCH

    @block.gpsimd
    def _(gpsimd):
        s_pg.wait(gpsimd, npgW)
        s_w2.dma(gpsimd.dma_start(out=w3["woT"], in_=ctx.win["l0_woT"].rearrange("(c p) m -> p c m", p=128)))
        s_w2.dma(gpsimd.dma_start(out=w3["uoT"], in_=ctx.win["l0_uoT"].rearrange("(c p) m -> p c m", p=128)))

    @block.vector
    def _(vector):
        s_cu.wait(vector, ncu)
        s_cb.inc(vector.tensor_copy(cbf[:, :], cst[:, :]))

    @block.tensor
    def _(tensor):
        s_cb.wait(tensor, nsteps + 1)
        s_w2.wait(tensor, 32)
        for m in range(NCH):
            pox = p_gA[m % 3]
            if m >= 3:
                s_uo.wait(tensor, m - 2)
            for c in range(NCH):
                i = tensor.matmul(pox[:, :], w3["uoT"][:, c, m * 128:(m + 1) * 128],
                                  cb3[:, c], start=(c == 0), stop=(c == NCH - 1))
            s_pg.inc(i)

    @block.vector
    def _(vector):
        base = npgW
        for m in range(NCH):
            s_pg.wait(vector, base + m + 1)
            s_uo.inc(vector.tensor_copy(uo3[:, m], p_gA[m % 3][:, :]))

    @block.sync
    def _(sync):
        s_pg.wait(sync, npgW)
        for t in range(NB):
            b = t % 2
            if t >= 1:
                s_r.wait(sync, 16 * t)
            s_x.dma(sync.dma_start(out=x3, in_=fm3(x_src)[:, :, t * TILE:(t + 1) * TILE]))
            if t >= 2:
                s_o.wait(sync, NCH * (t - 1))
            p_h2.mark(t, sync.dma_start(out=h4b[:, b], in_=fm3(h_src)[:, :, t * TILE:(t + 1) * TILE]))

    npg2 = npgW + NCH

    @block.tensor
    def _(tensor):
        s_uo.wait(tensor, NCH)
        for t in range(NB):
            bh = t % 2
            p_h2.wait(tensor, t)
            for m in range(NCH):
                k = t * NCH + m
                po = p_out[k % 2]
                if k >= 2:
                    s_o.wait(tensor, k - 1)
                for c in range(NCH):
                    i = tensor.matmul(po[:, :], w3["woT"][:, c, m * 128:(m + 1) * 128],
                                      h4b[:, bh, c], start=(c == 0), stop=(c == NCH - 1))
                s_pg.inc(i)

    def _apply(vector, k):
        t, m = divmod(k, NCH)
        s_oa.wait(vector, k + 1)
        s_x.wait(vector, 16 * (t + 1))
        i = vector.scalar_tensor_tensor(x3[:, m],
                                        oscr[:, (k % 2) * TILE:(k % 2 + 1) * TILE],
                                        0.5, x3[:, m], ALU.mult, ALU.add)
        s_ra.inc(i)

    @block.vector
    def _(vector):
        for k in range(NB * NCH):
            m = k % NCH
            po = p_out[k % 2]
            if k >= 1:
                _apply(vector, k - 1)
            s_pg.wait(vector, npg2 + k + 1)
            s_o.inc(vector.tensor_add(oscr[:, (k % 2) * TILE:(k % 2 + 1) * TILE], po[:, :], uo3[:, m]))
        _apply(vector, NB * NCH - 1)

    @block.scalar
    def _(scalar):
        for t in range(NB):
            for m in range(NCH):
                k = t * NCH + m
                s_o.wait(scalar, k + 1)
                i = scalar.activation(oscr[:, (k % 2) * TILE:(k % 2 + 1) * TILE],
                                      oscr[:, (k % 2) * TILE:(k % 2 + 1) * TILE],
                                      AF.Tanh, bias=bias[:, 3 * NCH + m:3 * NCH + m + 1])
                s_oa.inc(i)

    @block.gpsimd
    def _(gpsimd):
        for t in range(NB):
            s_ra.wait(gpsimd, (t + 1) * NCH)
            s_r.dma(gpsimd.dma_start(out=fm3(x_dst)[:, :, t * TILE:(t + 1) * TILE], in_=x3))

    ctx.phase_done = (s_r, s_r.n)



def l0_chunklist(qt):
    out = []
    for r in range(NCORE):
        for jj in range(4):
            if 16 * jj + r >= 32 * (qt + 1):
                continue
            masked = not (16 * jj + r + 8 < 32 * qt)
            out.append((r * 4 + jj, masked))
    return out


def emit_l0attn(ctx, h_src, x_src, x_dst):
    nc, block = ctx.nc, ctx.block
    s_w = phase_sem(ctx, "a0w", kind="sw")
    p_h = phase_pipe(ctx, "a0h")
    s_qp = phase_sem(ctx, "a0qp")
    s_qe = phase_sem(ctx, "a0qe")
    p_kst = phase_pipe(ctx, "a0kst", 4, kind="sw")
    p_vst = phase_pipe(ctx, "a0vst", 2, kind="sw")
    s_cc = phase_sem(ctx, "a0cc", kind="coll")
    p_kld = phase_pipe(ctx, "a0kld")
    p_vld = phase_pipe(ctx, "a0vld")
    p_vld2 = phase_pipe(ctx, "a0vld2")
    s_sp = phase_sem(ctx, "a0sp")
    s_pb = phase_sem(ctx, "a0pb")
    s_pm = phase_sem(ctx, "a0pm")
    s_om = phase_sem(ctx, "a0om")
    s_on = phase_sem(ctx, "a0on")
    p_x = phase_pipe(ctx, "a0x")
    s_r = phase_sem(ctx, "a0r", kind="sw")

    wq = sbt(ctx, "a0wq", [128, NCH * D], BF16)
    wk = sbt(ctx, "a0wk", [128, NCH * D], BF16)
    wv = sbt(ctx, "a0wv", [128, NCH * D], BF16)
    hbuf = sbt(ctx, "a0hb", [128, 2 * NCH * TILE], BF16)
    qsb = sbt(ctx, "a0q", [128, NCH * NT], BF16)
    arena = sbt(ctx, "a0arena", [128, 8192], BF16)
    kbuf = sbt(ctx, "a0kb", [128, 3 * NCH * 128], BF16)
    vbuf = sbt(ctx, "a0vb", [128, 3 * D], BF16)
    mtile = sbt(ctx, "a0mt", [128, 256], F32)
    nsc = sbt(ctx, "a0ns", [128, 2 * 256], F32)
    xb = sbt(ctx, "a0xb", [128, NCH * 256], F32)

    w3 = {n: t.ap().rearrange("p (c m) -> p c m", c=NCH) for n, t in
          (("wq", wq), ("wk", wk), ("wv", wv))}
    h4 = hbuf.ap().rearrange("p (b c n) -> p b c n", b=2, c=NCH)
    q3 = qsb.ap().rearrange("p (c n) -> p c n", c=NCH)
    k3s = arena.ap()[:, 0:NCH * TILE].rearrange("p (c n) -> p c n", c=NCH)
    v3s = arena.ap()[:, NCH * TILE:NCH * TILE + 4 * D].rearrange("p (a d) -> p a d", a=4)
    kb4 = kbuf.ap().rearrange("p (b c n) -> p b c n", b=3, c=NCH)
    vb3 = vbuf.ap().rearrange("p (b d) -> p b d", b=3)
    x3 = xb.ap().rearrange("p (c n) -> p c n", c=NCH)

    if not hasattr(ctx, "a0_dram"):
        kv_k_in = nc.dram_tensor("a0kvk_in", [4 * NCH * 128 * NB, 128], BF16, kind="Internal").ap()
        kv_v_in = nc.dram_tensor("a0kvv_in", [NT, D], BF16, kind="Internal").ap()
        kv_k_out = nc.dram_tensor("a0kvk_out", [NCORE * 4 * NCH * 128 * NB, 128], BF16, kind="Internal").ap()
        kv_vout = nc.dram_tensor("a0kvvo", [NCORE * NT, D], BF16, kind="Internal").ap()
        ctx.a0_dram = (kv_k_in, kv_v_in, kv_k_out, kv_vout)
    kv_k_in, kv_v_in, kv_k_out, kv_vout = ctx.a0_dram

    pQ = [ctx.ps512[i] for i in range(2)]
    NG = 24

    @block.gpsimd
    def _(gpsimd):
        for nm, t3 in (("l0_wq", "wq"), ("l0_wk", "wk"), ("l0_wv", "wv")):
            s_w.dma(gpsimd.dma_start(out=w3[t3], in_=ctx.win[nm].rearrange("(c p) m -> p c m", p=128)))

    @block.sync
    def _(sync):
        for t in range(NTILES):
            b = t % 2
            ln_wait_cols(ctx, sync, (t + 1) * TILE)
            if t >= 2:
                s_qp.wait(sync, NG * (t - 1))
            p_h.mark(t, sync.dma_start(out=h4[:, b], in_=fm3(h_src)[:, :, t * TILE:(t + 1) * TILE]))

    @block.tensor
    def _(tensor):
        s_w.wait(tensor, 48)
        for t in range(NTILES):
            b = t % 2
            p_h.wait(tensor, t)
            for j in range(NG):
                g = NG * t + j
                pp = pQ[g % 2]
                if g >= 2:
                    s_qe.wait(tensor, g - 1)
                if j < 16:
                    wsel, co = ("wq", j) if j < 8 else ("wk", j - 8)
                    for c in range(NCH):
                        i = tensor.matmul(pp[:, :], w3[wsel][:, c, co * 128:(co + 1) * 128],
                                          h4[:, b, c], start=(c == 0), stop=(c == NCH - 1))
                else:
                    a, no = divmod(j - 16, 2)
                    for c in range(NCH):
                        i = tensor.matmul(pp[:, :], h4[:, b, c, a * 128:(a + 1) * 128],
                                          w3["wv"][:, c, no * TILE:(no + 1) * TILE],
                                          start=(c == 0), stop=(c == NCH - 1))
                s_qp.inc(i)

    @block.scalar
    def _(scalar):
        for t in range(NTILES):
            for j in range(NG):
                g = NG * t + j
                pp = pQ[g % 2]
                s_qp.wait(scalar, g + 1)
                if j < 8:
                    dst = q3[:, j, t * TILE:(t + 1) * TILE]
                elif j < 16:
                    if j == 8:
                        for jj in range(4):
                            if t >= 1:
                                p_kst.wait(scalar, 4 * (t - 1) + jj)
                    dst = k3s[:, j - 8]
                else:
                    a, no = divmod(j - 16, 2)
                    if j == 16 and t >= 1:
                        p_vst.wait(scalar, t - 1)
                    dst = v3s[:, a, no * TILE:(no + 1) * TILE]
                s_qe.inc(scalar.copy(dst, pp[:, :]))

    @block.gpsimd
    def _(gpsimd):
        for t in range(NTILES):
            s_qe.wait(gpsimd, NG * t + 16)
            for jj in range(4):
                p_kst.mark(4 * t + jj, gpsimd.dma_start(
                    out=kv_k_in[t * 4096 + jj * 1024:t * 4096 + (jj + 1) * 1024, :]
                    .rearrange("(c p) n -> p c n", p=128),
                    in_=k3s[:, :, jj * 128:(jj + 1) * 128]))
            s_qe.wait(gpsimd, NG * t + 24)
            p_vst.mark(t, gpsimd.dma_start(
                out=kv_v_in[t * TILE:(t + 1) * TILE, :].rearrange("(a p) d -> p a d", p=128),
                in_=v3s))

    @block.gpsimd
    def _(gpsimd):
        for jj in range(4):
            p_kst.wait(gpsimd, 4 * (NTILES - 1) + jj)
        p_vst.wait(gpsimd, NTILES - 1)
        if FAKE_COLL:
            for r in range(NCORE):
                gpsimd.dma_start(out=kv_k_out[r * 32768:(r + 1) * 32768, :], in_=kv_k_in[:, :]).then_inc(s_cc.h, 16)
                gpsimd.dma_start(out=kv_vout[r * 4096:(r + 1) * 4096, :], in_=kv_v_in[:, :]).then_inc(s_cc.h, 16)
            s_cc.n += 256
        else:
            gpsimd.collective_compute("AllGather", ALU.bypass,
                                      replica_groups=[list(range(NCORE))],
                                      ins=[kv_k_in.opt()], outs=[kv_k_out.opt()]).then_inc(s_cc.h, 1)
            s_cc.n += 1
            gpsimd.collective_compute("AllGather", ALU.bypass,
                                      replica_groups=[list(range(NCORE))],
                                      ins=[kv_v_in.opt()], outs=[kv_vout.opt()]).then_inc(s_cc.h, 1)
            s_cc.n += 1
    cc_done = s_cc.n

    groups = []
    for b in range(NB):
        for qt in range(2):
            groups.append((b, qt, l0_chunklist(qt)))

    pal3 = arena.ap().rearrange("p (a n) -> p a n", n=256)
    vbufB = sbt(ctx, "a0vbB", [128, 3 * D], BF16)
    vbB3 = vbufB.ap().rearrange("p (b d) -> p b d", b=3)

    pO = ctx.ps512[0:4]
    pSab = [ctx.ps512[4], ctx.ps512[5], pst(ctx, "lnps2", [128, LTILE])]
    pC = pst(ctx, "lnps", [128, LTILE])

    sched = []
    si = 0
    om = 0
    vA = 0
    vB = 0
    for gi, (b, qt, lst) in enumerate(groups):
        g = {"b": b, "qt": qt, "lst": lst, "si0": si, "vA0": vA, "vB0": vB,
             "om0": om, "gi": gi}
        si += len(lst)
        vA += len(lst)
        g["omA"] = om = om + len(lst)
        g["omB"] = om = om + len(lst)
        vB += len(lst)
        sched.append(g)

    @block.sync
    def _(sync):
        s_cc.wait(sync, cc_done)
        for gi, g in enumerate(sched):
            b, qt, lst = g["b"], g["qt"], g["lst"]
            if gi >= 1:
                s_r.wait(sync, 16 * gi)
            p_x.mark(gi, sync.dma_start(
                out=x3, in_=fm3(x_src)[:, :, b * TILE + qt * 256:b * TILE + qt * 256 + 256]))
            for ci, (cc, masked) in enumerate(lst):
                r, jj = divmod(cc, 4)
                ksi = g["si0"] + ci
                kvi = g["vA0"] + ci
                if ksi >= 3:
                    s_sp.wait(sync, ksi - 2)
                p_kld.mark(ksi, sync.dma_start(
                    out=kb4[:, ksi % 3],
                    in_=kv_k_out[r * 32768 + b * 4096 + jj * 1024:
                                 r * 32768 + b * 4096 + (jj + 1) * 1024, :]
                    .rearrange("(c p) n -> p c n", p=128)))
                if ci >= 3:
                    s_om.wait(sync, g["om0"] + ci - 2)
                p_vld.mark(kvi, sync.dma_start(
                    out=vb3[:, kvi % 3],
                    in_=kv_vout[r * 4096 + b * 512 + jj * 128:
                                 r * 4096 + b * 512 + (jj + 1) * 128, :]))
            for ci, (cc, masked) in enumerate(lst):
                r, jj = divmod(cc, 4)
                kvi = g["vB0"] + ci
                if ci >= 3:
                    s_om.wait(sync, g["omA"] + ci - 2)
                p_vld2.mark(kvi, sync.dma_start(
                    out=vbB3[:, kvi % 3],
                    in_=kv_vout[r * 4096 + b * 512 + jj * 128:
                                 r * 4096 + b * 512 + (jj + 1) * 128, :]))

    @block.tensor
    def _(tensor):
        s_qe.wait(tensor, NG * NTILES)

        def _oA(g, ci):
            nA = len(g["lst"])
            ksi = g["si0"] + ci
            s_pm.wait(tensor, ksi + 1)
            p_vld.wait(tensor, g["vA0"] + ci)
            if ci == 0:
                s_on.wait(tensor, 2 * g["gi"])
            for c in range(4):
                tensor.matmul(pO[c][:, 0:256], vb3[:, (g["vA0"] + ci) % 3, c * 128:(c + 1) * 128],
                              pal3[:, ci], start=(ci == 0), stop=(ci == nA - 1))
            i = tensor.matmul(pC[:, 0:256], ctx.ones_bf[:, :], pal3[:, ci],
                              start=(ci == 0), stop=(ci == nA - 1))
            s_om.inc(i)

        def _oB(g, ci):
            nA = len(g["lst"])
            p_vld2.wait(tensor, g["vB0"] + ci)
            if ci == 0:
                s_on.wait(tensor, 2 * g["gi"] + 1)
            for c in range(4):
                i = tensor.matmul(pO[c][:, 256:512], vbB3[:, (g["vB0"] + ci) % 3, (4 + c) * 128:(5 + c) * 128],
                                  pal3[:, ci], start=(ci == 0), stop=(ci == nA - 1))
            s_om.inc(i)

        for gi, g in enumerate(sched):
            b, qt, lst = g["b"], g["qt"], g["lst"]
            for ci, (cc, masked) in enumerate(lst):
                ksi = g["si0"] + ci
                p_kld.wait(tensor, ksi)
                if ksi >= 3:
                    s_pb.wait(tensor, ksi - 2)
                for c in range(NCH):
                    i = tensor.matmul(pSab[ksi % 3][:, 0:256],
                                      kb4[:, ksi % 3, c],
                                      q3[:, c, b * TILE + qt * 256:b * TILE + qt * 256 + 256],
                                      start=(c == 0), stop=(c == NCH - 1))
                s_sp.inc(i)
                if ci >= 1:
                    _oA(g, ci - 1)
            _oA(g, len(lst) - 1)
            for ci in range(len(lst)):
                _oB(g, ci)

    @block.scalar
    def _(scalar):
        for gi, g in enumerate(sched):
            for ci, (cc, masked) in enumerate(g["lst"]):
                ksi = g["si0"] + ci
                s_sp.wait(scalar, ksi + 1)
                if gi >= 1:
                    s_om.wait(scalar, sched[gi - 1]["omB"])
                s_pb.inc(scalar.activation(pal3[:, ci], pSab[ksi % 3][:, 0:256],
                                           AF.Exp, bias=ctx.misc[:, 0:1], scale=1.0 / 32.0))

    @block.vector
    def _(vector):
        for gi, g in enumerate(sched):
            b, qt, lst = g["b"], g["qt"], g["lst"]
            for ci, (cc, masked) in enumerate(lst):
                ksi = g["si0"] + ci
                s_pb.wait(vector, ksi + 1)
                if masked:
                    vector.tensor_scalar(mtile[:, :], ctx.qglob[:, qt * 256:(qt + 1) * 256],
                                         ctx.kglob[:, cc:cc + 1], None, ALU.is_ge)
                    i = vector.tensor_tensor(pal3[:, ci], pal3[:, ci], mtile[:, :], ALU.mult)
                else:
                    i = vector.engine_nop()
                s_pm.inc(i)
            s_om.wait(vector, g["omA"])
            p_x.wait(vector, gi)
            vector.reciprocal(nsc[:, 0:256], pC[:, 0:256])
            for c in range(4):
                vector.tensor_tensor(nsc[:, 256:512], pO[c][:, 0:256], nsc[:, 0:256], ALU.mult)
                i = vector.tensor_add(x3[:, c], x3[:, c], nsc[:, 256:512])
            s_on.inc(i)
            s_om.wait(vector, g["omB"])
            for c in range(4):
                vector.tensor_tensor(nsc[:, 256:512], pO[c][:, 256:512], nsc[:, 0:256], ALU.mult)
                i = vector.tensor_add(x3[:, 4 + c], x3[:, 4 + c], nsc[:, 256:512])
            s_on.inc(i)

    @block.gpsimd
    def _(gpsimd):
        for gi, g in enumerate(sched):
            b, qt = g["b"], g["qt"]
            s_on.wait(gpsimd, 2 * (gi + 1))
            s_r.dma(gpsimd.dma_start(
                out=fm3(x_dst)[:, :, b * TILE + qt * 256:b * TILE + qt * 256 + 256], in_=x3))

    ctx.phase_done = (s_r, s_r.n)



def emit_l1attn(ctx, h_src, x_src, x_dst):
    nc, block = ctx.nc, ctx.block
    s_w = phase_sem(ctx, "a1w", kind="sw")
    s_w2 = phase_sem(ctx, "a1w2", kind="sw")
    p_h = phase_pipe(ctx, "a1h")
    s_qp = phase_sem(ctx, "a1qp")
    s_qe = phase_sem(ctx, "a1qe")
    s_qs = phase_sem(ctx, "a1qs")
    s_qv = phase_sem(ctx, "a1qv")
    s_qb = phase_sem(ctx, "a1qb")
    s_vv = phase_sem(ctx, "a1vv")
    s_sd = phase_sem(ctx, "a1sd")
    s_ql = phase_sem(ctx, "a1ql")
    s_vp = phase_sem(ctx, "a1vp")
    s_ve = phase_sem(ctx, "a1ve")
    p_qst = phase_pipe(ctx, "a1qst", 2, kind="sw")
    p_kst = phase_pipe(ctx, "a1kst", 4, kind="sw")
    p_vst = phase_pipe(ctx, "a1vst", 4, kind="sw")
    s_cc = phase_sem(ctx, "a1cc", kind="coll")
    p_qld = phase_pipe(ctx, "a1qld")
    p_kld = phase_pipe(ctx, "a1kld")
    p_vld = phase_pipe(ctx, "a1vld")
    s_sp = phase_sem(ctx, "a1sp")
    s_pb = phase_sem(ctx, "a1pb")
    s_om = phase_sem(ctx, "a1om")
    s_oe = phase_sem(ctx, "a1oe")
    s_wp = phase_sem(ctx, "a1wp")
    s_wv = phase_sem(ctx, "a1wv")
    p_x = phase_pipe(ctx, "a1x")
    s_r = phase_sem(ctx, "a1r", kind="sw")

    wq = sbt(ctx, "a1wq", [128, NCH * D], BF16)
    wk = sbt(ctx, "a1wk", [128, NCH * D], BF16)
    wv = sbt(ctx, "a1wv", [128, NCH * D], BF16)
    bqr = sbt(ctx, "a1bq", [1, 4 * D], BF16)
    selc = sbt(ctx, "a1sel", [128, 2], BF16)
    selb = sbt(ctx, "a1selb", [128, 128], F32)
    qkp = sbt(ctx, "a1qkp", [128, 4], F32)
    inib = sbt(ctx, "a1inib", [128, 1], F32)
    gam = sbt(ctx, "a1gam", [128, NCH], F32)
    hbuf = sbt(ctx, "a1hb", [128, NCH * TILE], BF16)
    qkarena = sbt(ctx, "a1qka", [128, 2 * 2 * NCH * TILE], BF16)
    vst_t = sbt(ctx, "a1vst", [128, 4 * D], BF16)
    stage = sbt(ctx, "a1stg", [128, 4 * 1024], BF16)
    srow = sbt(ctx, "a1srow", [128, 2 * 512], F32)
    vsc = sbt(ctx, "a1vsc", [128, 2 * TILE], F32)
    sdr = sbt(ctx, "a1sdr", [128, TILE], F32)
    qb_t = sbt(ctx, "a1qb2", [128, 2 * NCH * TILE], BF16)
    kbuf = sbt(ctx, "a1kb", [128, 4 * 128], BF16)
    vbuf = sbt(ctx, "a1vb", [128, 4 * 128], BF16)
    pbuf = sbt(ctx, "a1pbf", [128, 4 * TILE], BF16)
    osb = sbt(ctx, "a1osb", [128, 2 * NCH * TILE], BF16)
    xb = sbt(ctx, "a1xb", [128, NCH * TILE], F32)

    w3 = {"wq": wq.ap().rearrange("p (c m) -> p c m", c=NCH),
          "wk": wk.ap().rearrange("p (c m) -> p c m", c=NCH),
          "wv": wv.ap().rearrange("p (c m) -> p c m", c=NCH)}
    w3["wo"] = w3["wq"]
    h4 = hbuf.ap().rearrange("p (o c n) -> p o c n", o=1, c=NCH)
    qk5 = qkarena.ap().rearrange("p (s x c n) -> p s x c n", s=2, x=2, c=NCH)
    v3s = vst_t.ap().rearrange("p (a d) -> p a d", a=4)
    st4 = stage.ap().rearrange("p (r n) -> p r n", r=4)
    sr3 = srow.ap().rearrange("p (r n) -> p r n", r=2)
    qb4 = qb_t.ap().rearrange("p (s c n) -> p s c n", s=2, c=NCH)
    kb3 = kbuf.ap().rearrange("p (b n) -> p b n", b=4)
    vb3 = vbuf.ap().rearrange("p (b n) -> p b n", b=4)
    pb3 = pbuf.ap().rearrange("p (b n) -> p b n", b=4)
    ob4 = osb.ap().rearrange("p (b c n) -> p b c n", b=2, c=NCH)
    x3 = xb.ap().rearrange("p (c n) -> p c n", c=NCH)

    if not hasattr(ctx, "a1_dram"):
        k_in = nc.dram_tensor("a1kvk_in", [NB * 4 * NCH * 128, 128], BF16, kind="Internal").ap()
        v_in = nc.dram_tensor("a1kvv_in", [NCH * NT, 128], BF16, kind="Internal").ap()
        k_out = nc.dram_tensor("a1kvk_out", [NCORE * NB * 4 * NCH * 128, 128], BF16, kind="Internal").ap()
        vout = nc.dram_tensor("a1kvvo", [NCORE * NCH * NT, 128], BF16, kind="Internal").ap()
        q_dram = nc.dram_tensor("a1q_dram", [D, NT], BF16, kind="Internal").ap()
        ctx.a1_dram = (k_in, v_in, k_out, vout, q_dram)
    k_in, v_in, k_out, vout, q_dram = ctx.a1_dram

    pQ = ctx.ps512[0:2]
    pstat = ctx.ps512[2]
    pbc = ctx.ps512[3:5]
    pV = ctx.ps512[5]
    NU = 16

    @block.gpsimd
    def _(gpsimd):
        for nm, t3 in (("l1_wq", "wq"), ("l1_wk", "wk"), ("l1_wv", "wv")):
            s_w.dma(gpsimd.dma_start(out=w3[t3], in_=ctx.win[nm].rearrange("(c p) m -> p c m", p=128)))
        for i, nm in enumerate(("l1_bq", "l1_bk", "l1_bv", "l1_bo")):
            s_w.dma(gpsimd.dma_start(out=bqr[0:1, i * D:(i + 1) * D], in_=ctx.win[nm][:, :]))
        s_w.dma(gpsimd.dma_start(out=selc[:, :], in_=ctx.cin["sel2"][:, :]))

    s_wc = phase_sem(ctx, "a1wc")

    @block.sync
    def _(sync):
        s_wc.dma(sync.dma_start(out=selb[:, :], in_=ctx.cin["selB2"][:, :]))
        s_wc.dma(sync.dma_start(out=qkp[:, 0:1], in_=ctx.win["l1_qns"][:, :]))
        s_wc.dma(sync.dma_start(out=qkp[:, 1:2], in_=ctx.win["l1_qnb"][:, :]))
        s_wc.dma(sync.dma_start(out=qkp[:, 2:3], in_=ctx.win["l1_kns"][:, :]))
        s_wc.dma(sync.dma_start(out=qkp[:, 3:4], in_=ctx.win["l1_knb"][:, :]))
        s_wc.dma(sync.dma_start(out=inib[:, :], in_=ctx.win["l1_initb"][:, :]))
        s_wc.dma(sync.dma_start(out=gam[:, :], in_=ctx.win["l1_gamma"][:, :]))
        for t in range(NTILES):
            ln_wait_cols(ctx, sync, (t + 1) * TILE)
            if t >= 1:
                s_qp.wait(sync, NU * t)
                s_vp.wait(sync, 8 * t)
            p_h.mark(t, sync.dma_start(out=h4[:, 0], in_=fm3(h_src)[:, :, t * TILE:(t + 1) * TILE]))

    WALL = 16 * 8

    def _stats(tensor, U):
        s_qe.wait(tensor, U + 1)
        if U >= 1:
            s_qv.wait(tensor, U)
        tensor.matmul(pstat[0:2, 0:512], selc[:, :], st4[:, U % 4, 0:512], start=True, stop=True)
        i = tensor.matmul(pstat[64:66, 0:512], selc[:, :], st4[:, U % 4, 512:1024], start=True, stop=True)
        s_qs.inc(i)

    def _bcast(tensor, U):
        s_qv.wait(tensor, U + 1)
        if U >= 1:
            s_vv.wait(tensor, U)
        tensor.matmul(pbc[0][:, :], selb[0:2, :], sr3[0:2, U % 2], start=True, stop=True)
        i = tensor.matmul(pbc[1][:, :], selb[64:66, :], sr3[64:66, U % 2], start=True, stop=True)
        s_qb.inc(i)

    @block.tensor
    def _(tensor):
        s_w.wait(tensor, WALL)
        s_wc.wait(tensor, 16 * 7)
        for t in range(NTILES):
            bh = t % 2
            p_h.wait(tensor, t)
            for u in range(NU):
                U = NU * t + u
                X, co = u % 2, u // 2
                pp = pQ[U % 2]
                if U >= 2:
                    s_qe.wait(tensor, U - 1)
                for c in range(NCH):
                    tensor.matmul(pp[:, :], w3["wq" if X == 0 else "wk"][:, c, co * 128:(co + 1) * 128],
                                  h4[:, 0, c], start=(c == 0), stop=False)
                i = tensor.matmul(pp[:, :], bqr[0:1, X * D + co * 128:X * D + (co + 1) * 128],
                                  ctx.ones_row[0:1, :], start=False, stop=True)
                s_qp.inc(i)
                if U >= 1:
                    _stats(tensor, U - 1)
                if U >= 2:
                    _bcast(tensor, U - 2)
            for vg in range(8):
                a, no = divmod(vg, 2)
                G = 8 * t + vg
                if G >= 1:
                    s_ve.wait(tensor, G)
                for c in range(NCH):
                    tensor.matmul(pV[:, :], h4[:, 0, c, a * 128:(a + 1) * 128],
                                  w3["wv"][:, c, no * 512:(no + 1) * 512],
                                  start=(c == 0), stop=False)
                i = tensor.matmul(pV[:, :], ctx.ones_row[0:1, 0:128],
                                  bqr[0:1, 2 * D + no * 512:2 * D + (no + 1) * 512],
                                  start=False, stop=True)
                s_vp.inc(i)
        NUU = NU * NTILES
        _stats(tensor, NUU - 1)
        _bcast(tensor, NUU - 2)
        _bcast(tensor, NUU - 1)

    @block.scalar
    def _(scalar):
        for t in range(NTILES):
            for u in range(NU):
                U = NU * t + u
                s_qp.wait(scalar, U + 1)
                if U >= 3:
                    s_ql.wait(scalar, U - 3)
                scalar.copy(st4[:, U % 4, 0:512], pQ[U % 2][:, :])
                s_qe.inc(scalar.activation(st4[:, U % 4, 512:1024], pQ[U % 2][:, :], AF.Square))
                if U >= 1:
                    s_qs.wait(scalar, U)
                    scalar.copy(sr3[0:2, (U - 1) % 2], pstat[0:2, 0:512])
                    s_qv.inc(scalar.copy(sr3[64:66, (U - 1) % 2], pstat[64:66, 0:512]))
                if U >= 2:
                    s_vv.wait(scalar, U - 1)
                    s_sd.inc(scalar.activation(sdr[:, :], vsc[:, TILE:2 * TILE], AF.Sqrt, bias=ctx.misc[:, 2:3]))
            for vg in range(8):
                a, no = divmod(vg, 2)
                G = 8 * t + vg
                s_vp.wait(scalar, G + 1)
                if t >= 1:
                    for cslab in range(8):
                        p_vst.wait(scalar, 8 * (t - 1) + cslab)
                s_ve.inc(scalar.copy(v3s[:, a, no * 512:(no + 1) * 512], pV[:, :]))
        NUU = NU * NTILES
        for U in (NUU - 1,):
            s_qs.wait(scalar, U + 1)
            scalar.copy(sr3[0:2, U % 2], pstat[0:2, 0:512])
            s_qv.inc(scalar.copy(sr3[64:66, U % 2], pstat[64:66, 0:512]))
        for U in (NUU - 2, NUU - 1):
            s_vv.wait(scalar, U + 1)
            s_sd.inc(scalar.activation(sdr[:, :], vsc[:, TILE:2 * TILE], AF.Sqrt, bias=ctx.misc[:, 2:3]))

    @block.vector
    def _(vector):
        s_wc.wait(vector, 16 * 7)
        for U in range(NU * NTILES):
            t, u = divmod(U, NU)
            X, co = u % 2, u // 2
            s_qb.wait(vector, U + 1)
            m = vsc[:, 0:TILE]
            var = vsc[:, TILE:2 * TILE]
            vector.tensor_scalar_mul(m[:, :], pbc[0][:, :], 1.0 / 64.0)
            vector.tensor_scalar_mul(var[:, :], pbc[1][:, :], 1.0 / 64.0)
            vector.tensor_tensor(sdr[:, :], m[:, :], m[:, :], ALU.mult)
            s_vv.inc(vector.tensor_sub(var[:, :], var[:, :], sdr[:, :]))
            s_sd.wait(vector, U + 1)
            vector.reciprocal(var[:, :], sdr[:, :])
            if X == 1 and t >= 2:
                for jj in range(4):
                    p_kst.wait(vector, 4 * (t - 2) + jj)
            if X == 0 and t >= 2:
                p_qst.wait(vector, t - 2)
            dst = qk5[:, t % 2, X, co]
            vector.tensor_sub(sdr[:, :], st4[:, U % 4, 0:512], m[:, :])
            vector.tensor_tensor(sdr[:, :], sdr[:, :], var[:, :], ALU.mult)
            i = vector.tensor_scalar(dst, sdr[:, :], qkp[:, 2 * X:2 * X + 1], qkp[:, 2 * X + 1:2 * X + 2],
                                     ALU.mult, ALU.add)
            s_ql.inc(i)

    @block.gpsimd
    def _(gpsimd):
        for t in range(NTILES):
            s_ql.wait(gpsimd, NU * (t + 1))
            p_qst.mark(t, gpsimd.dma_start(
                out=fm3(q_dram)[:, :, t * TILE:(t + 1) * TILE], in_=qk5[:, t % 2, 0]))
            for jj in range(4):
                p_kst.mark(4 * t + jj, gpsimd.dma_start(
                    out=k_in[t * 4096 + jj * 1024:t * 4096 + (jj + 1) * 1024, :]
                    .rearrange("(c p) n -> p c n", p=128),
                    in_=qk5[:, t % 2, 1, :, jj * 128:(jj + 1) * 128]))
            s_ve.wait(gpsimd, 8 * (t + 1))
            for cslab in range(8):
                p_vst.mark(8 * t + cslab, gpsimd.dma_start(
                    out=v_in[cslab * NT + t * TILE:cslab * NT + (t + 1) * TILE, :]
                    .rearrange("(a p) n -> p a n", p=128),
                    in_=v3s[:, :, cslab * 128:(cslab + 1) * 128]))
        if FAKE_COLL:
            for r in range(NCORE):
                gpsimd.dma_start(out=k_out[r * 32768:(r + 1) * 32768, :], in_=k_in[:, :]).then_inc(s_cc.h, 16)
                gpsimd.dma_start(out=vout[r * 32768:(r + 1) * 32768, :], in_=v_in[:, :]).then_inc(s_cc.h, 16)
            s_cc.n += 256
        else:
            gpsimd.collective_compute("AllGather", ALU.bypass,
                                      replica_groups=[list(range(NCORE))],
                                      ins=[k_in.opt()], outs=[k_out.opt()]).then_inc(s_cc.h, 1)
            s_cc.n += 1
            gpsimd.collective_compute("AllGather", ALU.bypass,
                                      replica_groups=[list(range(NCORE))],
                                      ins=[v_in.opt()], outs=[vout.opt()]).then_inc(s_cc.h, 1)
            s_cc.n += 1
        s_qp.wait(gpsimd, NU * NTILES)
        s_w2.dma(gpsimd.dma_start(out=w3["wo"], in_=ctx.win["l1_wo"].rearrange("(c p) m -> p c m", p=128)))
    cc_done = s_cc.n - s_cc.base

    pO = ctx.ps512[0:2]
    pS2 = ctx.ps512[2:6]
    pW = ctx.ps512[0:2]

    @block.sync
    def _(sync):
        s_cc.wait(sync, cc_done)
        for b in range(NB):
            if b >= 1:
                s_r.wait(sync, 16 * b)
            p_x.mark(b, sync.dma_start(out=x3, in_=fm3(x_src)[:, :, b * TILE:(b + 1) * TILE]))
            if b >= 2:
                s_oe.wait(sync, 16 * (b - 1))
            p_qld.mark(b, sync.dma_start(out=qb4[:, b % 2], in_=fm3(q_dram)[:, :, b * TILE:(b + 1) * TILE]))
            for cp in range(NCH):
                for kc in range(32):
                    r, jj = divmod(kc, 4)
                    LI = (b * NCH + cp) * 32 + kc
                    if LI >= 4:
                        s_sp.wait(sync, 2 * (LI - 3))
                        s_om.wait(sync, 2 * (LI - 3))
                    p_kld.mark(LI, sync.dma_start(
                        out=kb3[:, LI % 4],
                        in_=k_out[r * 32768 + b * 4096 + jj * 1024 + cp * 128:
                                  r * 32768 + b * 4096 + jj * 1024 + (cp + 1) * 128, :]))
                    p_vld.mark(LI, sync.dma_start(
                        out=vb3[:, LI % 4],
                        in_=vout[r * 32768 + cp * 4096 + b * 512 + jj * 128:
                                  r * 32768 + cp * 4096 + b * 512 + (jj + 1) * 128, :]))

    @block.tensor
    def _(tensor):
        s_qe.wait(tensor, NU * NTILES)
        s_ve.wait(tensor, 8 * NTILES)
        s_w2.wait(tensor, 16)

        def _o_unit(SU):
            LI, h2 = divmod(SU, 2)
            b = LI // (NCH * 32)
            cp = (LI // 32) % NCH
            kc = LI % 32
            bank = pO[cp % 2]
            s_pb.wait(tensor, SU + 1)
            p_vld.wait(tensor, LI)
            if kc == 0 and b >= 1:
                s_wv.wait(tensor, 8 * b)
            i = tensor.matmul(bank[64 * h2:64 * h2 + 64, :],
                              vb3[:, LI % 4, 64 * h2:64 * h2 + 64],
                              pb3[:, SU % 4], start=(kc == 0), stop=(kc == 31))
            s_om.inc(i)

        def _wo(b):
            s_oe.wait(tensor, 16 * (b + 1))
            for dc in range(NCH):
                pp = pW[dc % 2]
                if b > 0 or dc >= 2:
                    s_wv.wait(tensor, 8 * b + dc - 1)
                for c in range(NCH):
                    tensor.matmul(pp[:, :], w3["wo"][:, c, dc * 128:(dc + 1) * 128],
                                  ob4[:, b % 2, c], start=(c == 0), stop=False)
                i = tensor.matmul(pp[:, :], bqr[0:1, 3 * D + dc * 128:3 * D + (dc + 1) * 128],
                                  ctx.ones_row[0:1, :], start=False, stop=True)
                s_wp.inc(i)

        for b in range(NB):
            if b >= 1:
                _wo(b - 1)
            p_qld.wait(tensor, b)
            SU0 = 2 * b * NCH * 32
            for cp in range(NCH):
                for kc in range(32):
                    LI = (b * NCH + cp) * 32 + kc
                    p_kld.wait(tensor, LI)
                    for h2 in range(2):
                        SU = 2 * LI + h2
                        if SU >= 4:
                            s_pb.wait(tensor, SU - 3)
                        i = tensor.matmul(pS2[SU % 4][:, :],
                                          kb3[64 * h2:64 * h2 + 64, LI % 4],
                                          qb4[64 * h2:64 * h2 + 64, b % 2, cp],
                                          start=True, stop=True)
                        s_sp.inc(i)
                        if SU > SU0:
                            _o_unit(SU - 1)
            _o_unit(2 * (b + 1) * NCH * 32 - 1)
        _wo(NB - 1)

    @block.scalar
    def _(scalar):
        for b in range(NB):
            for cp in range(NCH):
                for kc in range(32):
                    LI = (b * NCH + cp) * 32 + kc
                    for h2 in range(2):
                        SU = 2 * LI + h2
                        s_sp.wait(scalar, SU + 1)
                        if SU >= 4:
                            s_om.wait(scalar, SU - 3)
                        s_pb.inc(scalar.activation(pb3[:, SU % 4], pS2[SU % 4][:, :],
                                                   AF.Sigmoid, bias=inib[:, 0:1]))
                SUlast = 2 * ((b * NCH + cp) * 32 + 31) + 1
                s_om.wait(scalar, SUlast + 1)
                if b >= 2:
                    s_wp.wait(scalar, 8 * (b - 1))
                s_oe.inc(scalar.copy(ob4[:, b % 2, cp], pO[cp % 2][:, :]))
                s_oe.inc(scalar.nop())

    @block.vector
    def _(vector):
        for b in range(NB):
            p_x.wait(vector, b)
            for dc in range(NCH):
                s_wp.wait(vector, 8 * b + dc + 1)
                vector.tensor_scalar(sdr[:, :], pW[dc % 2][:, :], gam[:, dc:dc + 1], None, ALU.mult)
                i = vector.tensor_add(x3[:, dc], x3[:, dc], sdr[:, :])
                s_wv.inc(i)

    @block.gpsimd
    def _(gpsimd):
        for b in range(NB):
            s_wv.wait(gpsimd, 8 * (b + 1))
            s_r.dma(gpsimd.dma_start(out=fm3(x_dst)[:, :, b * TILE:(b + 1) * TILE], in_=x3))

    ctx.phase_done = (s_r, s_r.n)



NT3 = [(0, 512), (512, 1024), (1024, CAP)]
NT3_G = [2 * ((c1 - c0 + 127) // 128) for c0, c1 in NT3]


def emit_moe(ctx, h_src, x_src, x_dst):
    nc, block = ctx.nc, ctx.block
    s_ag = phase_sem(ctx, "m_ag", kind="coll")
    s_sl = phase_sem(ctx, "m_sl")
    s_bi = phase_sem(ctx, "m_bi")
    s_bp = phase_sem(ctx, "m_bp")
    s_ms = phase_sem(ctx, "m_ms")
    s_of = phase_sem(ctx, "m_of")
    p_ht = phase_pipe(ctx, "m_ht", 4)
    s_pf = phase_sem(ctx, "m_pf", kind="sw")
    s_em = phase_sem(ctx, "m_em")
    s_ez = phase_sem(ctx, "m_ez")
    s_sc = phase_sem(ctx, "m_sc", kind="sw")
    s_sm = phase_sem(ctx, "m_sm", kind="sw")
    p_ex = phase_pipe(ctx, "m_ex", 2)
    s_wc = phase_sem(ctx, "m_wcc", kind="sw")
    s_w2 = phase_sem(ctx, "m_w2")
    p_w1 = phase_pipe(ctx, "m_w1", 2)
    s_b12 = phase_sem(ctx, "m_b12")
    s_h1 = phase_sem(ctx, "m_h1")
    s_ge = phase_sem(ctx, "m_ge")
    p_hs = phase_pipe(ctx, "m_hs", 2)
    p_hl = phase_pipe(ctx, "m_hl", 2)
    p_mt = phase_pipe(ctx, "m_mt", 2)
    s_mc = phase_sem(ctx, "m_mc")
    s_y = phase_sem(ctx, "m_y")
    s_ye = phase_sem(ctx, "m_ye")
    s_ya = phase_sem(ctx, "m_ya", kind="sw")
    s_z = phase_sem(ctx, "m_z")
    p_yl = phase_pipe(ctx, "m_yl", 2)
    s_tr = phase_sem(ctx, "m_tr")
    s_rx = phase_sem(ctx, "m_rx")
    p_x = phase_pipe(ctx, "m_x")
    s_r = phase_sem(ctx, "m_r", kind="sw")

    S_all = sbt(ctx, "m_sall", [128, 256 * E], F32)
    bis = sbt(ctx, "m_bis", [128, 4 * E], F32)
    cnt = sbt(ctx, "m_cnt", [128, E], F32)
    cscr = sbt(ctx, "m_cscr", [128, 256], F32)
    mask = sbt(ctx, "m_mask", [128, 32], F32)
    pref = sbt(ctx, "m_pref", [128, 32], F32)
    cntc = sbt(ctx, "m_cntc", [128, 1], F32)
    offs = sbt(ctx, "m_offs", [128, 1], F32)
    dest = sbt(ctx, "m_dest", [128, 32], F32)
    destu = sbt(ctx, "m_destu", [128, 2 * 32], U32)
    pairs = sbt(ctx, "m_pairs", [128, 2 * 64], F32)
    htm = sbt(ctx, "m_htm", [128, 4 * D], BF16)
    em0 = sbt(ctx, "m_em0", [128, 18], F32)
    w2s = sbt(ctx, "m_w2s", [128, 32 * D], BF16)
    w1s = sbt(ctx, "m_w1s", [128, 2 * NCH * 128], BF16)
    b1e = sbt(ctx, "m_b1e", [1, FF], BF16)
    b2e = sbt(ctx, "m_b2e", [1, D], BF16)
    exf = sbt(ctx, "m_exf", [128, NCH * CAP], BF16)
    hstage = sbt(ctx, "m_hst", [128, 2 * CAP], BF16)
    hnt = sbt(ctx, "m_hnt", [128, 32 * 512], BF16)
    meta = sbt(ctx, "m_meta", [128, 2 * 2], F32)
    sctu = sbt(ctx, "m_sctu", [128, 9], U32)
    gbuf = sbt(ctx, "m_gbuf", [128, 9], F32)
    ysb = sbt(ctx, "m_ysb", [128, 2 * D], F32)
    zeros = sbt(ctx, "m_zero", [128, D], F32)
    zerosb = sbt(ctx, "m_zerob", [128, D], BF16)
    ytl = sbt(ctx, "m_ytl", [128, D], F32)
    xb = sbt(ctx, "m_xb", [128, NCH * 128], F32)

    sall3 = S_all.ap().rearrange("p (a e) -> p a e", e=E)
    thr = bis.ap()[:, 0:E]
    lo = bis.ap()[:, E:2 * E]
    hi = bis.ap()[:, 2 * E:3 * E]
    bsc = bis.ap()[:, 3 * E:4 * E]
    du3 = destu.ap().rearrange("p (s a) -> p s a", s=2)
    pr3 = pairs.ap().rearrange("p (s a t) -> p s a t", s=2, t=2)
    htm3 = htm.ap().rearrange("p (s d) -> p s d", s=4)
    w2v = w2s.ap().rearrange("p (f m) -> p f m", f=32)
    w1v = w1s.ap().rearrange("p (s c m) -> p s c m", s=2, c=NCH)
    exv = exf.ap().rearrange("p (c n) -> p c n", c=NCH)
    hsv = hstage.ap().rearrange("p (s n) -> p s n", s=2)
    hnv = hnt.ap().rearrange("p (f n) -> p f n", f=32)
    mv = meta.ap().rearrange("p (s t) -> p s t", s=2)
    yv = ysb.ap().rearrange("p (s d) -> p s d", s=2)
    ylv = ytl.ap().rearrange("p (s d) -> p s d", s=1)
    x3 = xb.ap().rearrange("p (c n) -> p c n", c=NCH)
    gloc = ctx.gates_tm.ap().rearrange("p (a e) -> p e a", e=E)
    gsg = ctx.gsig_tm.ap().rearrange("p (a e) -> p e a", e=E)

    if not hasattr(ctx, "m_dram"):
        ag_gout = nc.dram_tensor("m_aggout", [NCORE * NT, E], F32, kind="Internal").ap()
        ex_all = nc.dram_tensor("m_exall", [E * CAP, D], BF16, kind="Internal").ap()
        exmeta = nc.dram_tensor("m_exmeta", [E * CAP, 2], F32, kind="Internal").ap()
        h_big = nc.dram_tensor("m_hbig", [FF, CAP], BF16, kind="Internal").ap()
        y_moe = nc.dram_tensor("m_ymoe", [NT + 128, D], F32, kind="Internal").ap()
        w1c = nc.dram_tensor("m_w1c", [E, D, FF], BF16, kind="Internal").ap()
        w2c = nc.dram_tensor("m_w2c", [E, FF, D], BF16, kind="Internal").ap()
        b12c = nc.dram_tensor("m_b12c", [E, FF + D], BF16, kind="Internal").ap()
        ctx.m_dram = (ag_gout, ex_all, exmeta, h_big, y_moe, w1c, w2c, b12c)
        ctx.m_wcached = False
    ag_gout, ex_all, exmeta, h_big, y_moe, w1c, w2c, b12c = ctx.m_dram
    first_fill = not ctx.m_wcached

    pW = ctx.ps512[0:2]
    pTot = ctx.ps512[2]
    pOf = ctx.ps512[3]
    pTr = ctx.ps512[4:6]

    gm_sem, gm_n = ctx.gates_mark

    @block.gpsimd
    def _(gpsimd):
        if first_fill:
            for e in range(E):
                s_wc.dma(gpsimd.dma_start(out=w1c[e], in_=ctx.win["l1_w1"][e]))
                s_wc.dma(gpsimd.dma_start(out=w2c[e], in_=ctx.win["l1_w2"][e]))
                s_wc.dma(gpsimd.dma_start(out=b12c[e:e + 1, 0:FF], in_=ctx.win["l1_b1"][e:e + 1, :]))
                s_wc.dma(gpsimd.dma_start(out=b12c[e:e + 1, FF:FF + D], in_=ctx.win["l1_b2"][e:e + 1, :]))
        gm_sem.wait_abs(gpsimd, gm_n)
        if FAKE_COLL:
            for r in range(NCORE):
                gpsimd.dma_start(out=ag_gout[r * NT:(r + 1) * NT, :], in_=ctx.ag_gin[:, :]).then_inc(s_ag.h, 16)
            s_ag.n += 128
        else:
            gpsimd.collective_compute("AllGather", ALU.bypass,
                                      replica_groups=[list(range(NCORE))],
                                      ins=[ctx.ag_gin.opt()], outs=[ag_gout.opt()]).then_inc(s_ag.h, 1)
            s_ag.n += 1
    ctx.m_wcached = True

    @block.vector
    def _(vector):
        vector.memset(zeros[:, :], 0.0)
        vector.memset(zerosb[:, :], 0.0)
        vector.memset(lo, -64.0)
        vector.memset(hi, 64.0)
        s_bi.inc(vector.memset(thr, 0.0))

    @block.sync
    def _(sync):
        s_ag.wait(sync)
        s_sl.dma(sync.dma_start(out=sall3, in_=ag_gout.rearrange("(p a) e -> p a e", p=128)))
        s_bi.wait(sync, 1)
        for zc in range(33):
            s_z.dma(sync.dma_start(out=y_moe[zc * 128:(zc + 1) * 128, :], in_=zeros[:, :]))
        for zc in range(E * CAP // 128):
            s_ez.dma(sync.dma_start(out=ex_all[zc * 128:(zc + 1) * 128, :], in_=zerosb[:, :]))
        s_em.dma(sync.dma_start(out=em0[:, :], in_=ctx.cin["exmeta0s"][:, :]))

    for it in range(N_BISECT):
        @block.vector
        def _(vector, it=it):
            if it == 0:
                s_sl.wait(vector, 16)
            vector.wait_ge(s_bi.h, s_bi.n)
            i = None
            for e in range(E):
                i = vector.tensor_scalar(cscr[:, :], sall3[:, :, e], thr[:, e:e + 1], None,
                                         ALU.is_ge, ALU.add, accum_out=cnt[:, e:e + 1])
            s_bi.inc(i)

        @block.tensor
        def _(tensor, it=it):
            s_bi.wait(tensor)
            i = tensor.matmul(pTot[:, 0:E], ctx.ones_f32[:, :], cnt[:, :], start=True, stop=True)
            s_bp.inc(i)

        @block.vector
        def _(vector, it=it):
            s_bp.wait(vector, it + 1)
            w = 64.0 / (2.0 ** it)

            def step(i):
                s_bi.inc(i)
                vector.wait_ge(s_bi.h, s_bi.n)
            step(vector.tensor_scalar(bsc, pTot[:, 0:E], float(KSEL), None, ALU.is_ge))
            step(vector.tensor_scalar(bsc, bsc, 2.0 * w, -w, ALU.mult, ALU.add))
            step(vector.tensor_add(thr, thr, bsc))

    for e in range(E):
        @block.vector
        def _(vector, e=e):
            if e >= 2:
                s_sc.wait(vector, 16 * 32 * (e - 1))
                s_sm.wait(vector, 16 * 32 * (e - 1))

            def step(i):
                s_ms.inc(i)
                vector.wait_ge(s_ms.h, s_ms.n)
            step(vector.tensor_scalar(mask[:, :], gloc[:, e], thr[:, e:e + 1], None, ALU.is_ge))
            step(vector.tensor_tensor_scan(pref[:, :], mask[:, :], mask[:, :], 0.0, ALU.add, ALU.bypass))
            step(vector.tensor_copy(cntc[:, 0:1], pref[:, 31:32]))

        @block.tensor
        def _(tensor, e=e):
            s_ms.wait(tensor)
            i = tensor.matmul(pOf[:, 0:1], ctx.ltm[:, :], cntc[:, 0:1], start=True, stop=True)
            s_of.inc(i)

        @block.vector
        def _(vector, e=e):
            s_of.wait(vector, e + 1)

            def step(i):
                s_ms.inc(i)
                vector.wait_ge(s_ms.h, s_ms.n)
            step(vector.tensor_copy(offs[:, :], pOf[:, 0:1]))
            step(vector.tensor_scalar(dest[:, :], pref[:, :], offs[:, 0:1], -1.0 + e * CAP, ALU.add, ALU.add))
            step(vector.scalar_tensor_tensor(dest[:, :], dest[:, :], -BIG, mask[:, :], ALU.add, ALU.mult))
            step(vector.tensor_scalar(dest[:, :], dest[:, :], BIG, None, ALU.add))
            step(vector.tensor_copy(du3[:, e % 2], dest[:, :]))
            step(vector.tensor_copy(pr3[:, e % 2, :, 0], ctx.iota32[:, :]))
            step(vector.tensor_copy(pr3[:, e % 2, :, 1], gsg[:, e]))

        @block.scalar
        def _(scalar, e=e):
            for a in range(32):
                ai = 32 * e + a
                if ai >= 4:
                    s_sc.wait(scalar, 16 * (ai - 3))
                p_ht.mark(ai, scalar.dma_start(out=htm3[:, ai % 4],
                                               in_=h_src[:, a * 128:(a + 1) * 128], transpose=True))

        @block.gpsimd
        def _(gpsimd, e=e):
            if not hasattr(ctx, "m_bcreg"):
                ctx.m_bcreg = nc.alloc_register(name="m_bcreg", engine=mybir.EngineType.Pool)
                gpsimd.reg_mov(ctx.m_bcreg, E * CAP - 1)
                ctx.m_bcreg2 = nc.alloc_register(name="m_bcreg2", engine=mybir.EngineType.Pool)
                gpsimd.reg_mov(ctx.m_bcreg2, NT + 128 - 1)
            s_em.wait(gpsimd, 16)
            gpsimd.dma_start(out=exmeta[e * CAP:(e + 1) * CAP, :]
                             .rearrange("(p f) t -> p f t", p=128),
                             in_=em0.ap().rearrange("p (f t) -> p f t", t=2)).then_inc(s_pf.h, 16)
            s_pf.n += 16
            s_ms.wait(gpsimd)
            s_pf.wait(gpsimd)
            s_ez.wait(gpsimd, 16 * (E * CAP // 128))
            for a in range(32):
                ai = 32 * e + a
                p_ht.wait(gpsimd, ai)
                s_sc.dma(gpsimd.indirect_dma_start(
                    out=ex_all[:, :],
                    out_offset=IndirectOffsetOnAxis(ap=destu[:, (e % 2) * 32 + a:(e % 2) * 32 + a + 1], axis=0),
                    in_=htm3[:, ai % 4], in_offset=None,
                    bounds_check=ctx.m_bcreg, oob_is_err=False))
                s_sm.dma(gpsimd.indirect_dma_start(
                    out=exmeta[:, :],
                    out_offset=IndirectOffsetOnAxis(ap=destu[:, (e % 2) * 32 + a:(e % 2) * 32 + a + 1], axis=0),
                    in_=pairs[:, (e % 2) * 64 + 2 * a:(e % 2) * 64 + 2 * a + 2], in_offset=None,
                    bounds_check=ctx.m_bcreg, oob_is_err=False))

    ny = 0
    nya = 0
    for e in range(E):
        @block.scalar
        def _(scalar, e=e):
            s_sc.wait(scalar, 16 * 32 * (e + 1))
            if e >= 1:
                s_h1.wait(scalar, 96 * e)
            for c in range(NCH):
                p_ex.mark(NCH * e + c, scalar.dma_start(
                    out=exv[:, c], in_=ex_all[e * CAP:(e + 1) * CAP, c * 128:(c + 1) * 128],
                    transpose=True))

        @block.sync
        def _(sync, e=e, ny=ny, nge=96 * e):
            if hasattr(ctx, "s_mwc"):
                ctx.s_mwc.wait_abs(sync, ctx.s_mwc.n)
            else:
                s_wc.wait(sync)
            if e >= 1:
                s_y.wait(sync, ny)
            s_w2.dma(sync.dma_start(out=w2v, in_=w2c[e].rearrange("(f p) m -> p f m", p=128)))
            s_b12.dma(sync.dma_start(out=b1e[0:1, :], in_=b12c[e:e + 1, 0:FF]))
            s_b12.dma(sync.dma_start(out=b2e[0:1, :], in_=b12c[e:e + 1, FF:FF + D]))
            for f in range(32):
                fi = 32 * e + f
                if fi >= 2:
                    s_ge.wait(sync, 3 * (fi - 1))
                p_w1.mark(fi, sync.dma_start(
                    out=w1v[:, fi % 2],
                    in_=w1c[e, :, f * 128:(f + 1) * 128].rearrange("(c p) m -> p c m", p=128)))
                if f >= 1:
                    s_ge.wait(sync, nge + 3 * f)
                    p_hs.mark(32 * e + f - 1, sync.dma_start(
                        out=h_big[(f - 1) * 128:f * 128, :], in_=hsv[:, (f - 1) % 2]))
            s_ge.wait(sync, nge + 96)
            p_hs.mark(32 * e + 31, sync.dma_start(
                out=h_big[31 * 128:32 * 128, :], in_=hsv[:, 31 % 2]))

        @block.tensor
        def _(tensor, e=e, nh1=96 * e):
            for c in range(NCH):
                p_ex.wait(tensor, NCH * e + c)
            s_b12.wait(tensor)
            for f in range(32):
                fi = 32 * e + f
                p_w1.wait(tensor, fi)
                for nt, (c0, c1) in enumerate(NT3):
                    g = nh1 + 3 * f + nt
                    pp = pW[g % 2]
                    if g >= 2:
                        s_ge.wait(tensor, g - 1)
                    for c in range(NCH):
                        tensor.matmul(pp[:, 0:c1 - c0], w1v[:, fi % 2, c],
                                      exv[:, c, c0:c1], start=(c == 0), stop=False)
                    i = tensor.matmul(pp[:, 0:c1 - c0],
                                      b1e[0:1, f * 128:(f + 1) * 128],
                                      ctx.ones_row[0:1, 0:c1 - c0], start=False, stop=True)
                    s_h1.inc(i)

        @block.scalar
        def _(scalar, e=e, nh1=96 * e):
            for f in range(32):
                if f >= 2:
                    p_hs.wait(scalar, 32 * e + f - 2)
                for nt, (c0, c1) in enumerate(NT3):
                    g = nh1 + 3 * f + nt
                    s_h1.wait(scalar, g + 1)
                    i = scalar.activation(hsv[:, f % 2, c0:c1], pW[g % 2][:, 0:c1 - c0], AF.Gelu)
                    s_ge.inc(i)

        @block.sync
        def _(sync, e=e):
            s_sm.wait(sync, 16 * 32 * (e + 1))
            for q in range(9):
                qi = 9 * e + q
                if qi >= 2:
                    s_mc.wait(sync, qi - 1)
                p_mt.mark(qi, sync.dma_start(out=mv[:, qi % 2],
                                             in_=exmeta[e * CAP + q * 128:e * CAP + (q + 1) * 128, :]))

        @block.vector
        def _(vector, e=e):
            for q in range(9):
                qi = 9 * e + q
                p_mt.wait(vector, qi)
                if e >= 1:
                    s_ya.wait(vector, 16 * (9 * (e - 1) + q + 1))
                vector.tensor_copy(sctu[:, q:q + 1], mv[:, qi % 2, 0:1])
                i = vector.tensor_copy(gbuf[:, q:q + 1], mv[:, qi % 2, 1:2])
                s_mc.inc(i)

        @block.sync
        def _(sync, e=e, ny=ny):
            gcum = ny
            for nt, (c0, c1) in enumerate(NT3):
                li = 3 * e + nt
                if li >= 1:
                    s_y.wait(sync, gcum)
                p_hl.mark(li, sync.dma_start(
                    out=hnv[:, :, 0:c1 - c0],
                    in_=h_big[:, c0:c1].rearrange("(f p) n -> p f n", p=128)))
                gcum += NT3_G[nt]

        @block.tensor
        def _(tensor, e=e, ny=ny):
            s_w2.wait(tensor, 16 * (e + 1))
            k = ny
            for nt, (c0, c1) in enumerate(NT3):
                li = 3 * e + nt
                p_hl.wait(tensor, li)
                for cc in range((c1 - c0 + 127) // 128):
                    for dh in range(2):
                        pp = pW[k % 2]
                        if k >= 2:
                            s_ye.wait(tensor, k - 1)
                        for f in range(32):
                            tensor.matmul(pp[:, :], hnv[:, f, cc * 128:(cc + 1) * 128],
                                          w2v[:, f, dh * 512:(dh + 1) * 512],
                                          start=(f == 0), stop=False)
                        i = tensor.matmul(pp[:, :], ctx.ones_row[0:1, 0:128],
                                          b2e[0:1, dh * 512:(dh + 1) * 512],
                                          start=False, stop=True)
                        s_y.inc(i)
                        k += 1

        @block.vector
        def _(vector, e=e, ny=ny, nya=nya):
            k = ny
            kk = 0
            for nt, (c0, c1) in enumerate(NT3):
                for cc in range((c1 - c0 + 127) // 128):
                    q = c0 // 128 + cc
                    for dh in range(2):
                        s_y.wait(vector, k + 1)
                        s_mc.wait(vector, 9 * e + q + 1)
                        if kk >= 4:
                            s_ya.wait(vector, nya + 16 * (kk // 2 - 1))
                        i = vector.tensor_scalar(yv[:, (k // 2) % 2, dh * 512:(dh + 1) * 512],
                                                 pW[k % 2][:, :], gbuf[:, q:q + 1], None, ALU.mult)
                        s_ye.inc(i)
                        k += 1
                        kk += 1

        @block.gpsimd
        def _(gpsimd, e=e, ny=ny, nya=nya):
            if e == 0:
                s_z.wait(gpsimd, 16 * 33)
            else:
                s_ya.wait(gpsimd, nya)
            k = ny
            for nt, (c0, c1) in enumerate(NT3):
                for cc in range((c1 - c0 + 127) // 128):
                    q = c0 // 128 + cc
                    s_ye.wait(gpsimd, k + 2)
                    s_mc.wait(gpsimd, 9 * e + q + 1)
                    s_ya.dma(gpsimd.indirect_dma_start(
                        out=y_moe[:, :],
                        out_offset=IndirectOffsetOnAxis(ap=sctu[:, q:q + 1], axis=0),
                        in_=yv[:, (k // 2) % 2], in_offset=None,
                        compute_op=ALU.add, oob_is_err=True, bounds_check=ctx.m_bcreg2))
                    k += 2
        ny += 18
        nya += 16 * 9

    @block.sync
    def _(sync):
        s_ya.wait(sync, nya)
        for rt in range(32):
            if rt >= 1:
                s_tr.wait(sync, 8 * rt)
            p_yl.mark(rt, sync.dma_start(out=ylv[:, 0], in_=y_moe[rt * 128:(rt + 1) * 128, :]))
            if rt >= 1:
                s_r.wait(sync, 16 * rt)
            p_x.mark(rt, sync.dma_start(out=x3, in_=fm3(x_src)[:, :, rt * 128:(rt + 1) * 128]))

    @block.tensor
    def _(tensor):
        for rt in range(32):
            p_yl.wait(tensor, rt)
            for c in range(NCH):
                g = 8 * rt + c
                if g >= 2:
                    s_rx.wait(tensor, g - 1)
                i = tensor.transpose(pTr[g % 2][:, 0:128], ylv[:, 0, c * 128:(c + 1) * 128],
                                     ctx.ident[:, :])
                s_tr.inc(i)

    @block.vector
    def _(vector):
        for rt in range(32):
            p_x.wait(vector, rt)
            for c in range(NCH):
                g = 8 * rt + c
                s_tr.wait(vector, g + 1)
                i = vector.scalar_tensor_tensor(x3[:, c], pTr[g % 2][:, 0:128], 0.5,
                                                x3[:, c], ALU.mult, ALU.add)
                s_rx.inc(i)

    @block.gpsimd
    def _(gpsimd):
        for rt in range(32):
            s_rx.wait(gpsimd, 8 * (rt + 1))
            s_r.dma(gpsimd.dma_start(out=fm3(x_dst)[:, :, rt * 128:(rt + 1) * 128], in_=x3))

    ctx.phase_done = (s_r, s_r.n)



def finish(ctx):
    s, n = ctx.phase_done
    block = ctx.block

    @block.sync
    def _(sync):
        s.wait_abs(sync, n)


def build_ffn0(weights, consts, gates=False, ln_only=False, nsteps=NB):
    nc = _mknc()
    x_in = nc.dram_tensor("x_in", [D, NT], F32, kind="ExternalInput").ap()
    if ln_only:
        h_dram = nc.dram_tensor("h_out", [D, NT], BF16, kind="ExternalOutput").ap()
    else:
        x_out = nc.dram_tensor("x_out", [D, NT], F32, kind="ExternalOutput").ap()
        h_dram = nc.dram_tensor("h_dram", [D, NT], BF16, kind="Internal").ap()
    es = contextlib.ExitStack()
    with es:
        block = es.enter_context(nc.Block())
        ctx = Ctx()
        ctx.nc, ctx.es, ctx.block, ctx.uid, ctx.nbar = nc, es, block, "", 0
        ctx.win = {}
        wneed = [] if ln_only else [k for k in weights if k.startswith("l0_") and not k.endswith(("wq", "wk", "wv"))]
        for name in wneed:
            ctx.win[name] = nc.dram_tensor(name, list(weights[name].shape), F32, kind="ExternalInput").ap()
        emit_const_loads(ctx, consts)
        emit_ln(ctx, x_in, h_dram, gates=False)
        if not ln_only:
            emit_l0ffn(ctx, h_dram, x_in, x_out, nsteps=nsteps)
            finish(ctx)
        else:
            @block.sync
            def _(sync):
                ln_wait_cols(ctx, sync, NT)
    return nc, wneed


def build_attn0(weights, consts):
    nc = _mknc()
    x_in = nc.dram_tensor("x_in", [D, NT], F32, kind="ExternalInput").ap()
    x_out = nc.dram_tensor("x_out", [D, NT], F32, kind="ExternalOutput").ap()
    h_dram = nc.dram_tensor("h_dram", [D, NT], BF16, kind="Internal").ap()
    es = contextlib.ExitStack()
    with es:
        block = es.enter_context(nc.Block())
        ctx = Ctx()
        ctx.nc, ctx.es, ctx.block, ctx.uid, ctx.nbar = nc, es, block, "", 0
        ctx.win = {}
        wneed = ["l0_wq", "l0_wk", "l0_wv"]
        for name in wneed:
            ctx.win[name] = nc.dram_tensor(name, list(weights[name].shape), F32, kind="ExternalInput").ap()
        emit_const_loads(ctx, consts)
        emit_ln(ctx, x_in, h_dram, gates=False)
        emit_l0attn(ctx, h_dram, x_in, x_out)
        finish(ctx)
    return nc, wneed


def build_attn1(weights, consts):
    nc = _mknc()
    x_in = nc.dram_tensor("x_in", [D, NT], F32, kind="ExternalInput").ap()
    x_out = nc.dram_tensor("x_out", [D, NT], F32, kind="ExternalOutput").ap()
    h_dram = nc.dram_tensor("h_dram", [D, NT], BF16, kind="Internal").ap()
    es = contextlib.ExitStack()
    with es:
        block = es.enter_context(nc.Block())
        ctx = Ctx()
        ctx.nc, ctx.es, ctx.block, ctx.uid, ctx.nbar = nc, es, block, "", 0
        ctx.win = {}
        wneed = ["l1_wq", "l1_wk", "l1_wv", "l1_wo", "l1_bq", "l1_bk", "l1_bv", "l1_bo",
                 "l1_qns", "l1_qnb", "l1_kns", "l1_knb", "l1_initb", "l1_gamma"]
        for name in wneed:
            ctx.win[name] = nc.dram_tensor(name, list(weights[name].shape), F32, kind="ExternalInput").ap()
        emit_const_loads(ctx, consts)
        emit_ln(ctx, x_in, h_dram, gates=False)
        emit_l1attn(ctx, h_dram, x_in, x_out)
        finish(ctx)
    return nc, wneed


def build_moe(weights, consts):
    nc = _mknc()
    x_in = nc.dram_tensor("x_in", [D, NT], F32, kind="ExternalInput").ap()
    x_out = nc.dram_tensor("x_out", [D, NT], F32, kind="ExternalOutput").ap()
    h_dram = nc.dram_tensor("h_dram", [D, NT], BF16, kind="Internal").ap()
    es = contextlib.ExitStack()
    with es:
        block = es.enter_context(nc.Block())
        ctx = Ctx()
        ctx.nc, ctx.es, ctx.block, ctx.uid, ctx.nbar = nc, es, block, "", 0
        ctx.win = {}
        wneed = ["l1_wg", "l1_w1", "l1_b1", "l1_w2", "l1_b2"]
        for name in wneed:
            ctx.win[name] = nc.dram_tensor(name, list(weights[name].shape), F32, kind="ExternalInput").ap()
        ctx.ag_gin = nc.dram_tensor("m_aggin", [NT, E], F32, kind="Internal").ap()
        emit_const_loads(ctx, consts)
        emit_ln(ctx, x_in, h_dram, gates=True)
        emit_moe(ctx, h_dram, x_in, x_out)
        finish(ctx)
    return nc, wneed



_PROG_CACHE = {}


def _get_prog(kind, weights, consts0):
    if kind in _PROG_CACHE:
        return _PROG_CACHE[kind]
    if kind == "ffn0":
        nc, wneed = build_ffn0(weights, consts0)
    elif kind == "attn0":
        nc, wneed = build_attn0(weights, consts0)
    elif kind == "attn1":
        nc, wneed = build_attn1(weights, consts0)
    elif kind == "moe":
        nc, wneed = build_moe(weights, consts0)
    _PROG_CACHE[kind] = (nc, wneed)
    return _PROG_CACHE[kind]


def kernel_unfused(**inputs):
    w = host_weights(inputs)
    consts = [host_constants(c) for c in range(NCORE)]
    xs = shard_x(np.asarray(inputs["L0_attn_x"], np.float32))
    for kind in ("ffn0", "attn0", "ffn0", "moe", "attn1", "moe"):
        nc, wneed = _get_prog(kind, w, consts[0])
        in_maps = []
        for c in range(NCORE):
            m = {"x_in": xs[c]}
            for n in wneed:
                m[n] = w[n]
            m.update(consts[c])
            in_maps.append(m)
        res = run_bass_kernel_spmd(nc, in_maps, core_ids=list(range(NCORE)))
        xs = [np.asarray(res.results[c]["x_out"], np.float32) for c in range(NCORE)]
    return unshard_x(xs).astype(np.float32)


def kernel(**inputs):
    w = host_weights(inputs)
    consts = [host_constants(c) for c in range(NCORE)]
    xs = shard_x(np.asarray(inputs["L0_attn_x"], np.float32))
    if "fused" not in _PROG_CACHE:
        _PROG_CACHE["fused"] = build_fused(w, consts[0])
    nc = _PROG_CACHE["fused"]
    in_maps = []
    for c in range(NCORE):
        m = {"x_in": xs[c]}
        for n in w:
            m[n] = w[n]
        m.update(consts[c])
        in_maps.append(m)
    res = run_bass_kernel_spmd(nc, in_maps, core_ids=list(range(NCORE)))
    xs = [np.asarray(res.results[c]["x_out"], np.float32) for c in range(NCORE)]
    return unshard_x(xs).astype(np.float32)



def emit_moe_prefill(ctx):
    nc, block = ctx.nc, ctx.block
    ag_gout = nc.dram_tensor("m_aggout", [NCORE * NT, E], F32, kind="Internal").ap()
    ex_all = nc.dram_tensor("m_exall", [E * CAP, D], BF16, kind="Internal").ap()
    exmeta = nc.dram_tensor("m_exmeta", [E * CAP, 2], F32, kind="Internal").ap()
    h_big = nc.dram_tensor("m_hbig", [FF, CAP], BF16, kind="Internal").ap()
    y_moe = nc.dram_tensor("m_ymoe", [NT + 128, D], F32, kind="Internal").ap()
    w1c = nc.dram_tensor("m_w1c", [E, D, FF], BF16, kind="Internal").ap()
    w2c = nc.dram_tensor("m_w2c", [E, FF, D], BF16, kind="Internal").ap()
    b12c = nc.dram_tensor("m_b12c", [E, FF + D], BF16, kind="Internal").ap()
    ctx.m_dram = (ag_gout, ex_all, exmeta, h_big, y_moe, w1c, w2c, b12c)
    ctx.m_wcached = True
    ctx.s_mwc = Sem(ctx, "s_mwc", kind="sw")
    ctx.m_prefill_pending = True


def build_fused(weights, consts):
    nc = _mknc()
    x_in = nc.dram_tensor("x_in", [D, NT], F32, kind="ExternalInput").ap()
    x_out = nc.dram_tensor("x_out", [D, NT], F32, kind="ExternalOutput").ap()
    xa = nc.dram_tensor("xa", [D, NT], F32, kind="Internal").ap()
    xbd = nc.dram_tensor("xbd", [D, NT], F32, kind="Internal").ap()
    h_dram = nc.dram_tensor("h_dram", [D, NT], BF16, kind="Internal").ap()
    es = contextlib.ExitStack()
    with es:
        block = es.enter_context(nc.Block())
        ctx = Ctx()
        ctx.nc, ctx.es, ctx.block, ctx.uid, ctx.nbar = nc, es, block, "", 0
        ctx.fused = True
        ctx.phase_key = None
        ctx.win = {}
        for name in weights:
            ctx.win[name] = nc.dram_tensor(name, list(weights[name].shape), F32, kind="ExternalInput").ap()
        ctx.ag_gin = nc.dram_tensor("m_aggin", [NT, E], F32, kind="Internal").ap()
        emit_const_loads(ctx, consts)
        emit_moe_prefill(ctx)
        rem = nc.sbuf_bytes_remaining
        ctx.arena_size = rem - 512
        arena_t = ctx.es.enter_context(nc.sbuf_tensor("ARENA", [128, ctx.arena_size], mybir.dt.uint8))
        mloc = nc.lookup_mloc(arena_t)
        ctx.arena_base = mloc.addr

        begin_phase(ctx, "ffn0")
        emit_ln(ctx, x_in, h_dram, gates=False)
        emit_l0ffn(ctx, h_dram, x_in, xa)
        end_phase(ctx)

        begin_phase(ctx, "attn0")
        emit_ln(ctx, xa, h_dram, gates=False)
        emit_l0attn(ctx, h_dram, xa, xbd)
        end_phase(ctx)

        begin_phase(ctx, "ffn0")
        emit_ln(ctx, xbd, h_dram, gates=False)
        emit_l0ffn(ctx, h_dram, xbd, xa)
        end_phase(ctx)

        begin_phase(ctx, "moe")
        emit_ln(ctx, xa, h_dram, gates=True)
        emit_moe(ctx, h_dram, xa, xbd)
        end_phase(ctx)

        begin_phase(ctx, "attn1")
        emit_ln(ctx, xbd, h_dram, gates=False)
        emit_l1attn(ctx, h_dram, xbd, xa)
        end_phase(ctx)

        begin_phase(ctx, "moe")
        emit_ln(ctx, xa, h_dram, gates=True)
        emit_moe(ctx, h_dram, xa, x_out)
        end_phase(ctx)
    return nc


# version 10 (final)
# speedup vs baseline: 1.0301x; 1.0301x over previous
import contextlib
import numpy as np
import ml_dtypes
import concourse.bass as bass
import concourse.mybir as mybir
from concourse.bass_utils import run_bass_kernel_spmd
from concourse.bass import IndirectOffsetOnAxis

F32 = mybir.dt.float32
BF16 = mybir.dt.bfloat16
U32 = mybir.dt.uint32
AF = mybir.ActivationFunctionType
ALU = mybir.AluOpType

NCORE = 8
D = 1024
NB = 8
SEQ = 4096
BLK = 64
NT = 4096
NCH = 8
TILE = 512
NTILES = NT // TILE
E = 8
FF = 4096
NFF = FF // 128
CAP = 1152
NCAPCH = CAP // 128
KSEL = 8192
BIG = 16777216.0
N_BISECT = 28

bfd = ml_dtypes.bfloat16
_BASS_KW = {}
FAKE_COLL = False


def _mknc():
    return bass.Bass("TRN2", target_bir_lowering=False, **_BASS_KW)



def shard_x(x):
    xr = np.ascontiguousarray(x, dtype=np.float32).reshape(NB, 64, BLK, D)
    out = []
    for c in range(NCORE):
        xc = xr[:, c::8]
        xc = np.transpose(xc, (3, 0, 1, 2))
        out.append(np.ascontiguousarray(xc.reshape(D, NT)))
    return out


def unshard_x(parts):
    x = np.zeros((NB, 64, BLK, D), dtype=np.float32)
    for c in range(NCORE):
        xc = np.asarray(parts[c], np.float32).reshape(D, NB, 8, BLK)
        x[:, c::8] = np.transpose(xc, (1, 2, 3, 0))
    return x.reshape(NB, SEQ, D)


def vec_pp(v):
    return np.ascontiguousarray(np.asarray(v, np.float32).reshape(NCH, 128).T)


def host_constants(core):
    cst = {}
    cst["ident"] = np.eye(128, dtype=np.float32)
    cst["ltm"] = (np.arange(128)[:, None] < np.arange(128)[None, :]).astype(np.float32)
    c_ = np.arange(512)
    qglob = ((8 * (c_ // BLK) + core) * BLK + c_ % BLK).astype(np.float32)
    cst["qglob"] = np.ascontiguousarray(np.tile(qglob[None, :], (128, 1)))
    kg = np.zeros((128, 32), np.float32)
    for cc in range(32):
        r = cc // 4
        j = 128 * (cc % 4) + np.arange(128)
        kg[:, cc] = (8 * (j // BLK) + r) * BLK + j % BLK
    cst["kglob"] = kg
    cst["iota32"] = (np.arange(128)[:, None] + 128.0 * np.arange(32)[None, :]).astype(np.float32)
    pref = np.zeros((E * CAP, 2), np.float32)
    pref[:, 0] = NT + (np.arange(E * CAP) % 128)
    cst["exmeta0"] = pref
    m = np.zeros((128, 8), np.float32)
    m[:, 0] = -4.0
    m[:, 1] = 1e-5
    m[:, 2] = 1e-6
    m[:, 3] = 1.0
    m[:, 4] = 0.5
    cst["misc"] = m
    cst["sel2"] = (np.arange(128)[:, None] // 64 == np.arange(2)[None, :]).astype(np.float32)
    selB2 = np.zeros((128, 128), np.float32)
    selB2[0:2] = cst["sel2"].T
    selB2[64:66] = cst["sel2"].T
    cst["selB2"] = selB2
    em = np.zeros((128, 18), np.float32)
    r = np.arange(128)[:, None] * 9 + np.arange(9)[None, :]
    em[:, 0::2] = NT + (r % 128)
    cst["exmeta0s"] = em
    return cst


def host_weights(inputs):
    w = {}
    for nm in ("wf", "wi", "wc", "uf", "ui", "uc"):
        w[f"l0_{nm}T"] = np.ascontiguousarray(np.asarray(inputs[f"L0_ffn_{nm}"], np.float32).T)
    w["l0_woT"] = np.ascontiguousarray(np.asarray(inputs["L0_ffn_wo"], np.float32).T)
    w["l0_uoT"] = np.ascontiguousarray(np.asarray(inputs["L0_ffn_uo"], np.float32).T)
    w["l0_bf"] = vec_pp(inputs["L0_ffn_bf"])
    w["l0_bi"] = vec_pp(inputs["L0_ffn_bi"])
    w["l0_bc"] = vec_pp(inputs["L0_ffn_bc"])
    w["l0_bo"] = vec_pp(inputs["L0_ffn_bo"])
    w["l0_inc"] = vec_pp(inputs["L0_ffn_in_c"])
    w["l0_wq"] = np.ascontiguousarray(np.asarray(inputs["L0_attn_Wq"], np.float32))
    w["l0_wk"] = np.ascontiguousarray(np.asarray(inputs["L0_attn_Wk"], np.float32))
    w["l0_wv"] = np.ascontiguousarray(np.asarray(inputs["L0_attn_Wv"], np.float32))
    w["l1_wq"] = np.ascontiguousarray(np.asarray(inputs["L1_attn_Wq"], np.float32))
    w["l1_wk"] = np.ascontiguousarray(np.asarray(inputs["L1_attn_Wk"], np.float32))
    w["l1_wv"] = np.ascontiguousarray(np.asarray(inputs["L1_attn_Wv"], np.float32))
    w["l1_wo"] = np.ascontiguousarray(np.asarray(inputs["L1_attn_Wo"], np.float32))
    for nm in ("bq", "bk", "bv", "bo"):
        w[f"l1_{nm}"] = np.asarray(inputs[f"L1_attn_{nm}"], np.float32).reshape(1, D).copy()
    w["l1_qns"] = np.tile(np.asarray(inputs["L1_attn_qn_scale"], np.float32), 2).reshape(128, 1).copy()
    w["l1_qnb"] = np.tile(np.asarray(inputs["L1_attn_qn_bias"], np.float32), 2).reshape(128, 1).copy()
    w["l1_kns"] = np.tile(np.asarray(inputs["L1_attn_kn_scale"], np.float32), 2).reshape(128, 1).copy()
    w["l1_knb"] = np.tile(np.asarray(inputs["L1_attn_kn_bias"], np.float32), 2).reshape(128, 1).copy()
    w["l1_gamma"] = vec_pp(inputs["L1_attn_gamma"])
    w["l1_initb"] = np.full((128, 1), float(np.asarray(inputs["L1_attn_init_bias"]).reshape(())), np.float32)
    w["l1_wg"] = np.ascontiguousarray(np.asarray(inputs["L1_ffn_Wg"], np.float32))
    w["l1_w1"] = np.ascontiguousarray(np.asarray(inputs["L1_ffn_W1"], np.float32))
    w["l1_b1"] = np.ascontiguousarray(np.asarray(inputs["L1_ffn_b1"], np.float32))
    w["l1_w2"] = np.ascontiguousarray(np.asarray(inputs["L1_ffn_W2"], np.float32))
    w["l1_b2"] = np.ascontiguousarray(np.asarray(inputs["L1_ffn_b2"], np.float32))
    return w


WEIGHT_SPECS = {k: None for k in (
    "l0_wfT l0_wiT l0_wcT l0_ufT l0_uiT l0_ucT l0_woT l0_uoT l0_bf l0_bi l0_bc "
    "l0_bo l0_inc l0_wq l0_wk l0_wv l1_wq l1_wk l1_wv l1_wo l1_bq l1_bk l1_bv "
    "l1_bo l1_qns l1_qnb l1_kns l1_knb l1_gamma l1_initb l1_wg l1_w1 l1_b1 "
    "l1_w2 l1_b2").split()}



def _inst_kind(inst):
    ins = inst.ins
    op = str(ins.opcode)
    if "DMA" in op and "Pool" in str(ins.engine):
        return "sw"
    return "hw"


class Sem:

    def __init__(self, ctx, name, kind="hw"):
        self.h = ctx.es.enter_context(ctx.nc.semaphore(name))
        self.n = 0
        self.base = 0
        self.kind = kind
        self.audit = getattr(ctx, "fused", False)

    def rebase(self):
        self.base = self.n

    def inc(self, inst, k=1):
        if self.audit and self.kind != "coll":
            ik = _inst_kind(inst)
            assert ik == self.kind, \
                f"sem kind mismatch: {self.h} declared {self.kind} got {ik} ({inst.ins.opcode})"
        inst.then_inc(self.h, k)
        self.n += k
        return self.n

    def dma(self, inst):
        return self.inc(inst, 16)

    def wait(self, eng, val=None):
        eng.wait_ge(self.h, self.n if val is None else (self.base + val))

    def wait_abs(self, eng, val):
        eng.wait_ge(self.h, val)


def phase_sem(ctx, name, kind="hw"):
    if getattr(ctx, "fused", False):
        if not hasattr(ctx, "sem_pools"):
            ctx.sem_pools = {"hw": [], "sw": [], "coll": []}
        pool = ctx.sem_pools[kind]
        slot = ctx.sem_slots[kind]
        ctx.sem_slots[kind] += 1
        while len(pool) <= slot:
            pool.append(Sem(ctx, f"ps_{kind}{len(pool)}", kind=kind))
        sm = pool[slot]
        sm.rebase()
        return sm
    if not hasattr(ctx, "sem_cache"):
        ctx.sem_cache = {}
    if name not in ctx.sem_cache:
        ctx.sem_cache[name] = Sem(ctx, name)
    sm = ctx.sem_cache[name]
    sm.rebase()
    return sm


class DPipe:

    def __init__(self, ctx, name, depth=2):
        self.depth = depth
        self.sems = [Sem(ctx, f"{name}_{i}") for i in range(depth)]

    def rebase(self):
        for sm in self.sems:
            sm.rebase()

    def mark(self, i, inst):
        self.sems[i % self.depth].inc(inst, 16)

    def wait(self, eng, i):
        sm = self.sems[i % self.depth]
        eng.wait_ge(sm.h, sm.base + 16 * (i // self.depth + 1))


class _SlotPipe:
    def __init__(self, sems, depth):
        self.depth = depth
        self.sems = sems

    def rebase(self):
        for sm in self.sems:
            sm.rebase()

    def mark(self, i, inst):
        self.sems[i % self.depth].inc(inst, 16)

    def wait(self, eng, i):
        sm = self.sems[i % self.depth]
        eng.wait_ge(sm.h, sm.base + 16 * (i // self.depth + 1))


def phase_pipe(ctx, name, depth=2, kind="hw"):
    if getattr(ctx, "fused", False):
        sems = [phase_sem(ctx, f"{name}_{i}", kind=kind) for i in range(depth)]
        return _SlotPipe(sems, depth)
    if not hasattr(ctx, "pipe_cache"):
        ctx.pipe_cache = {}
    key = (name, depth)
    if key not in ctx.pipe_cache:
        ctx.pipe_cache[key] = DPipe(ctx, name, depth)
    pp = ctx.pipe_cache[key]
    pp.rebase()
    return pp


class Ctx:
    pass


_DT_BYTES = {F32: 4, BF16: 2, U32: 4}


def sbt(ctx, name, shape, dtype):
    if not hasattr(ctx, "tile_cache"):
        ctx.tile_cache = {}
    if getattr(ctx, "fused", False) and getattr(ctx, "phase_key", None) is not None:
        key = ("s", ctx.phase_key, name)
        if key not in ctx.tile_cache:
            nbytes = shape[1] * _DT_BYTES[dtype]
            off = (ctx.arena_off + 31) // 32 * 32
            assert off + nbytes <= ctx.arena_size, \
                f"arena overflow: {ctx.phase_key}/{name} needs {nbytes} at {off} of {ctx.arena_size}"
            ctx.arena_off = off + nbytes
            t = ctx.nc.alloc_sbuf_tensor_at(f"{ctx.phase_key}_{name}", shape, dtype,
                                            offset=ctx.arena_base + off)
            ctx.tile_cache[key] = t
        t = ctx.tile_cache[key]
        assert list(t.shape) == list(shape), f"sbt {name} shape mismatch"
        return t
    key = ("s", name)
    if key not in ctx.tile_cache:
        ctx.tile_cache[key] = ctx.es.enter_context(ctx.nc.sbuf_tensor(name, shape, dtype))
    t = ctx.tile_cache[key]
    assert list(t.shape) == list(shape), f"sbt {name} shape mismatch"
    return t


def pst(ctx, name, shape, dtype=F32):
    if not hasattr(ctx, "tile_cache"):
        ctx.tile_cache = {}
    key = ("p", name)
    if key not in ctx.tile_cache:
        ctx.tile_cache[key] = ctx.es.enter_context(ctx.nc.psum_tensor(name, shape, dtype))
    return ctx.tile_cache[key]


def fm3(ap2d):
    return ap2d.rearrange("(c p) n -> p c n", p=128)


def make_ctx(nc, es, block, shapes_weights, uid=""):
    ctx = Ctx()
    ctx.nc, ctx.es, ctx.block = nc, es, block
    ctx.uid = uid
    ctx.win = {}
    for name, arr in shapes_weights.items():
        ctx.win[name] = nc.dram_tensor(name, list(arr.shape), F32, kind="ExternalInput").ap()
    for name in ("ident", "ltm", "qglob", "kglob", "iota32", "exmeta0", "misc"):
        pass
    return ctx


def emit_const_loads(ctx, consts):
    nc, block = ctx.nc, ctx.block
    ctx.cin = {}
    for name, arr in consts.items():
        ctx.cin[name] = nc.dram_tensor(name, list(arr.shape), F32, kind="ExternalInput").ap()
    ctx.ones_bf = sbt(ctx, "c_ones_bf", [128, 128], BF16)
    ctx.ones_row = sbt(ctx, "c_ones_row", [1, TILE], BF16)
    ctx.ones_f32 = sbt(ctx, "c_ones_f32", [128, 128], F32)
    ctx.ident = sbt(ctx, "c_ident", [128, 128], F32)
    ctx.ltm = sbt(ctx, "c_ltm", [128, 128], F32)
    ctx.qglob = sbt(ctx, "c_qglob", [128, 512], F32)
    ctx.kglob = sbt(ctx, "c_kglob", [128, 32], F32)
    ctx.iota32 = sbt(ctx, "c_iota32", [128, 32], F32)
    ctx.misc = sbt(ctx, "c_misc", [128, 8], F32)
    ctx.ps512 = [pst(ctx, f"cps{i}", [128, TILE]) for i in range(6)]
    ctx.s_const = Sem(ctx, "s_const")

    @block.sync
    def _(sync):
        ctx.s_const.dma(sync.dma_start(out=ctx.ident[:, :], in_=ctx.cin["ident"][:, :]))
        ctx.s_const.dma(sync.dma_start(out=ctx.ltm[:, :], in_=ctx.cin["ltm"][:, :]))
        ctx.s_const.dma(sync.dma_start(out=ctx.qglob[:, :], in_=ctx.cin["qglob"][:, :]))
        ctx.s_const.dma(sync.dma_start(out=ctx.kglob[:, :], in_=ctx.cin["kglob"][:, :]))
        ctx.s_const.dma(sync.dma_start(out=ctx.iota32[:, :], in_=ctx.cin["iota32"][:, :]))
        ctx.s_const.dma(sync.dma_start(out=ctx.misc[:, :], in_=ctx.cin["misc"][:, :]))

    @block.vector
    def _(vector):
        vector.memset(ctx.ones_bf[:, :], 1.0)
        vector.memset(ctx.ones_row[:, :], 1.0)
        ctx.s_const.inc(vector.memset(ctx.ones_f32[:, :], 1.0))

    ctx.const_ready = ctx.s_const.n


def barrier(ctx):
    block = ctx.block
    if not hasattr(ctx, "bar_sem"):
        ctx.bar_sem = Sem(ctx, "bar_sem")
    s = ctx.bar_sem
    base = s.n
    engs = ["sync", "vector", "scalar", "tensor", "gpsimd"]

    for e in engs:
        def mk(e=e):
            def f(eng):
                s.inc(eng.engine_nop() if e in ("vector", "gpsimd") else eng.nop())
            return f
        getattr(block, e)(mk())
    for e in engs:
        def mk2(e=e):
            def f(eng):
                eng.wait_ge(s.h, base + 5)
            return f
        getattr(block, e)(mk2())


def begin_phase(ctx, key):
    ctx.phase_key = key
    ctx.sem_slots = {"hw": 0, "sw": 0, "coll": 0}
    ctx.arena_off = 0


def end_phase(ctx):
    sphase, n = ctx.phase_done
    block = ctx.block

    @block.sync
    def _(sync):
        sphase.wait_abs(sync, n)
    barrier(ctx)
    ctx.phase_key = None





LTILE = 256
LNT = NT // LTILE


def emit_ln(ctx, x_src, h_dst, gates=False):
    nc, block = ctx.nc, ctx.block
    u = ctx.uid
    p_lb = phase_pipe(ctx, "lnlb", kind="sw")
    s_sq = phase_sem(ctx, "lnsq")
    s_pe = phase_sem(ctx, "lnpe")
    s_vs = phase_sem(ctx, "lnvs")
    s_sd = phase_sem(ctx, "lnsd")
    s_ap = phase_sem(ctx, "lnap")
    p_st = phase_pipe(ctx, "lnst")
    s_ag2 = phase_sem(ctx, "lnag2")
    s_gw = phase_sem(ctx, "lngw", kind="sw")
    s_gp = phase_sem(ctx, "lngp")
    s_gv = phase_sem(ctx, "lngv")

    xb = sbt(ctx, "lnxb", [128, 2 * NCH * LTILE], BF16)
    xq = sbt(ctx, "lnxq", [128, NCH * LTILE], BF16)
    hst = sbt(ctx, "lnhs", [128, 4 * LTILE], F32)
    hout = sbt(ctx, "lnho", [128, 2 * NCH * LTILE], BF16)
    p_s = pst(ctx, "lnps", [128, LTILE])
    p_s2 = pst(ctx, "lnps2", [128, LTILE])
    xb4 = xb.ap().rearrange("p (b c n) -> p b c n", b=2, c=NCH)
    xq3 = xq.ap().rearrange("p (c n) -> p c n", c=NCH)
    ho4 = hout.ap().rearrange("p (b c n) -> p b c n", b=2, c=NCH)
    if gates:
        wg_s = sbt(ctx, "lnwg", [128, NCH * E], BF16)
        p_g = ctx.ps512[0][:, 0:E]
        ctx.gates_tm = sbt(ctx, "gatetm", [128, 32 * E], F32)
        ctx.gsig_tm = sbt(ctx, "gsigtm", [128, 32 * E], F32)

    if gates:
        @block.gpsimd
        def _(gpsimd):
            s_gw.dma(gpsimd.dma_start(out=wg_s.ap().rearrange("p (c e) -> p c e", c=NCH),
                                      in_=ctx.win["l1_wg"].rearrange("(c p) e -> p c e", p=128)))

    pieces = []
    if getattr(ctx, "m_prefill_pending", False):
        ctx.m_prefill_pending = False
        _, _, _, _, _, w1c, w2c, b12c = ctx.m_dram
        for e in range(E):
            for hh in range(2):
                pieces.append((w1c[e][hh * 512:(hh + 1) * 512, :], ctx.win["l1_w1"][e][hh * 512:(hh + 1) * 512, :]))
                pieces.append((w2c[e][hh * 2048:(hh + 1) * 2048, :], ctx.win["l1_w2"][e][hh * 2048:(hh + 1) * 2048, :]))
            pieces.append((b12c[e:e + 1, 0:FF], ctx.win["l1_b1"][e:e + 1, :]))
            pieces.append((b12c[e:e + 1, FF:FF + D], ctx.win["l1_b2"][e:e + 1, :]))

    @block.gpsimd
    def _(gpsimd):
        ctx.s_const.wait(gpsimd, ctx.const_ready)
        for t in range(LNT):
            b = t % 2
            if t >= 2:
                s_ap.wait(gpsimd, t - 1)
                p_st.wait(gpsimd, t - 2)
                if gates:
                    s_gp.wait(gpsimd, 2 * (t - 1))
            p_lb.mark(t, gpsimd.dma_start(out=xb4[:, b], in_=fm3(x_src)[:, :, t * LTILE:(t + 1) * LTILE]))
            if t >= 1:
                while pieces and len(pieces) > (LNT - 1 - t) * 4:
                    dst, src = pieces.pop(0)
                    ctx.s_mwc.dma(gpsimd.dma_start(out=dst, in_=src))

    @block.scalar
    def _(scalar):
        var = hst[:, 2 * LTILE:3 * LTILE]
        sd = hst[:, 3 * LTILE:4 * LTILE]
        for t in range(LNT):
            b = t % 2
            p_lb.wait(scalar, t)
            if t >= 1:
                s_pe.wait(scalar, t)
            s_sq.inc(scalar.activation(xq[:, :], xb4[:, b].rearrange("p c n -> p (c n)"), AF.Square))
            s_vs.wait(scalar, t + 1)
            s_sd.inc(scalar.activation(sd[:, :], var[:, :], AF.Sqrt, bias=ctx.misc[:, 1:2]))
            if gates:
                for tc in range(LTILE // 128):
                    a = t * (LTILE // 128) + tc
                    s_gp.wait(scalar, a + 1)
                    s_gv.inc(scalar.copy(ctx.gates_tm[:, a * E:(a + 1) * E], p_g))

    if gates:
        wg3 = wg_s.ap().rearrange("p (c e) -> p c e", c=NCH)

    @block.tensor
    def _(tensor):
        if gates:
            s_gw.wait(tensor, 16)
        for t in range(LNT):
            b = t % 2
            s_sq.wait(tensor, t + 1)
            if t >= 1:
                s_vs.wait(tensor, t)
            for c in range(NCH):
                tensor.matmul(p_s[:, :], ctx.ones_bf[:, :], xb4[:, b, c], start=(c == 0), stop=(c == NCH - 1))
            for c in range(NCH):
                i = tensor.matmul(p_s2[:, :], ctx.ones_bf[:, :], xq3[:, c], start=(c == 0), stop=(c == NCH - 1))
            s_pe.inc(i)
            if gates:
                s_ap.wait(tensor, t + 1)
                for tc in range(LTILE // 128):
                    a = t * (LTILE // 128) + tc
                    if a >= 1:
                        s_gv.wait(tensor, a)
                    for c in range(NCH):
                        i = tensor.matmul(p_g, ho4[:, b, c, tc * 128:(tc + 1) * 128],
                                          wg3[:, c], start=(c == 0), stop=(c == NCH - 1))
                    s_gp.inc(i)

    @block.vector
    def _(vector):
        mu = hst[:, 0:LTILE]
        r = hst[:, LTILE:2 * LTILE]
        var = hst[:, 2 * LTILE:3 * LTILE]
        sd = hst[:, 3 * LTILE:4 * LTILE]
        for t in range(LNT):
            b = t % 2
            s_pe.wait(vector, t + 1)
            vector.tensor_scalar_mul(mu[:, :], p_s[:, :], 1.0 / D)
            vector.tensor_scalar_mul(var[:, :], p_s2[:, :], 1.0 / D)
            vector.tensor_tensor(r[:, :], mu[:, :], mu[:, :], ALU.mult)
            s_vs.inc(vector.tensor_sub(var[:, :], var[:, :], r[:, :]))
            s_sd.wait(vector, t + 1)
            vector.reciprocal(r[:, :], sd[:, :])
            if t >= 2:
                p_st.wait(vector, t - 2)
                if gates:
                    s_gp.wait(vector, 2 * (t - 1))
            for c in range(NCH):
                vector.tensor_sub(var[:, :], xb4[:, b, c], mu[:, :])
                i = vector.tensor_tensor(ho4[:, b, c], var[:, :], r[:, :], ALU.mult)
            s_ap.inc(i)

    @block.sync
    def _(sync):
        ctx.ln_pipe = p_st
        for t in range(LNT):
            b = t % 2
            s_ap.wait(sync, t + 1)
            p_st.mark(t, sync.dma_start(out=fm3(h_dst)[:, :, t * LTILE:(t + 1) * LTILE], in_=ho4[:, b]))

    if gates:
        g3 = ctx.gates_tm.ap().rearrange("p (a e) -> p a e", e=E)

        @block.scalar
        def _(scalar):
            s_gv.wait(scalar, 32)
            s_gv.inc(scalar.activation(ctx.gsig_tm[:, :], ctx.gates_tm[:, :], AF.Sigmoid))

        @block.sync
        def _(sync):
            s_gv.wait(sync)
            s_ag2.dma(sync.dma_start(out=ctx.ag_gin.rearrange("(a p) e -> p a e", p=128), in_=g3))
            ctx.gates_mark = (s_ag2, s_ag2.n)


def ln_wait_cols(ctx, eng, hi_col):
    t = (hi_col + LTILE - 1) // LTILE - 1
    ctx.ln_pipe.wait(eng, t)
    if t >= 1:
        ctx.ln_pipe.wait(eng, t - 1)



def emit_l0ffn(ctx, h_src, x_src, x_dst, nsteps=NB):
    nc, block = ctx.nc, ctx.block
    u = ctx.uid
    s_w = phase_sem(ctx, "f0w", kind="sw")
    p_h = phase_pipe(ctx, "f0h")
    s_cb = phase_sem(ctx, "f0cb")
    s_pg = phase_sem(ctx, "f0pg")
    s_ag = phase_sem(ctx, "f0ag")
    s_cu = phase_sem(ctx, "f0cu")
    s_uo = phase_sem(ctx, "f0uo")
    s_x = phase_sem(ctx, "f0x")
    s_o = phase_sem(ctx, "f0o")
    s_oa = phase_sem(ctx, "f0oa")
    s_ra = phase_sem(ctx, "f0ra")
    s_r = phase_sem(ctx, "f0r", kind="sw")
    p_h2 = phase_pipe(ctx, "f0h2x")

    wnames = ["wfT", "ufT", "wiT", "uiT", "wcT", "ucT"]
    wt = {n: sbt(ctx, f"f0{n}", [128, NCH * D], BF16) for n in wnames}
    wt["woT"] = wt["wfT"]
    wt["uoT"] = wt["ufT"]
    bias = sbt(ctx, "f0bias", [128, 4 * NCH], F32)
    incs = sbt(ctx, "f0incs", [128, NCH], F32)
    cst = sbt(ctx, "f0c", [128, NCH * TILE], F32)
    cbf = sbt(ctx, "f0cbf", [128, NCH * TILE], BF16)
    hbuf = sbt(ctx, "f0h", [128, 2 * NCH * TILE], BF16)
    hbuf2 = hbuf
    gsc = sbt(ctx, "f0g", [128, 3 * TILE], F32)
    uout = sbt(ctx, "f0uo", [128, NCH * TILE], F32)
    oscr = sbt(ctx, "f0os", [128, 2 * TILE], F32)
    xbuf = sbt(ctx, "f0x", [128, NCH * TILE], F32)
    p_gA = ctx.ps512[0:3]
    p_gB = ctx.ps512[3:6]
    p_out = [p_gB[0], p_gB[1]]
    c3 = cst.ap().rearrange("p (c n) -> p c n", c=NCH)
    cb3 = cbf.ap().rearrange("p (c n) -> p c n", c=NCH)
    h4 = hbuf.ap().rearrange("p (b c n) -> p b c n", b=2, c=NCH)
    h4b = hbuf2.ap().rearrange("p (b c n) -> p b c n", b=2, c=NCH)
    uo3 = uout.ap().rearrange("p (c n) -> p c n", c=NCH)
    x3 = xbuf.ap().rearrange("p (c n) -> p c n", c=NCH)
    w3 = {n: wt[n].ap().rearrange("p (c m) -> p c m", c=NCH) for n in wt}

    s_w2 = phase_sem(ctx, "f0w2", kind="sw")

    @block.gpsimd
    def _(gpsimd):
        for n in wnames:
            s_w.dma(gpsimd.dma_start(out=w3[n], in_=ctx.win[f"l0_{n}"].rearrange("(c p) m -> p c m", p=128)))

    s_wb = phase_sem(ctx, "f0wb")
    s_ic = phase_sem(ctx, "f0ic")

    @block.sync
    def _(sync):
        s_wb.dma(sync.dma_start(out=bias[:, 0:NCH], in_=ctx.win["l0_bf"][:, :]))
        s_wb.dma(sync.dma_start(out=bias[:, NCH:2 * NCH], in_=ctx.win["l0_bi"][:, :]))
        s_wb.dma(sync.dma_start(out=bias[:, 2 * NCH:3 * NCH], in_=ctx.win["l0_bc"][:, :]))
        s_wb.dma(sync.dma_start(out=bias[:, 3 * NCH:4 * NCH], in_=ctx.win["l0_bo"][:, :]))
        s_ic.dma(sync.dma_start(out=incs[:, :], in_=ctx.win["l0_inc"][:, :]))
        for t in range(nsteps):
            b = t % 2
            ln_wait_cols(ctx, sync, (t + 1) * TILE)
            if t >= 2:
                s_pg.wait(sync, 3 * NCH * (t - 1))
            p_h.mark(t, sync.dma_start(out=h4[:, b], in_=fm3(h_src)[:, :, t * TILE:(t + 1) * TILE]))

    @block.vector
    def _(vector):
        s_ic.wait(vector, 16)
        for c in range(NCH):
            vector.memset(c3[:, c], 0.0)
        i = None
        for c in range(NCH):
            i = vector.tensor_scalar(c3[:, c], c3[:, c], incs[:, c:c + 1], None, ALU.add)
        s_cu.inc(i)

    ncu = 1
    npgW = 0

    for t in range(nsteps):
        bh = t % 2

        @block.vector
        def _(vector, ncu=ncu):
            s_cu.wait(vector, ncu)
            s_cb.inc(vector.tensor_copy(cbf[:, :], cst[:, :]))

        @block.tensor
        def _(tensor, t=t, bh=bh):
            s_w.wait(tensor, 16 * 6)
            p_h.wait(tensor, t)
            s_cb.wait(tensor, t + 1)
            for m in range(NCH):
                g = t * NCH + m
                pg = p_gA if m % 2 == 0 else p_gB
                if g >= 2:
                    s_ag.wait(tensor, 3 * (g - 2) + 3)
                for gi, (wn, un) in enumerate((("wfT", "ufT"), ("wiT", "uiT"), ("wcT", "ucT"))):
                    for c in range(NCH):
                        tensor.matmul(pg[gi][:, :], w3[wn][:, c, m * 128:(m + 1) * 128],
                                      h4[:, bh, c], start=(c == 0), stop=False)
                    for c in range(NCH):
                        i = tensor.matmul(pg[gi][:, :], w3[un][:, c, m * 128:(m + 1) * 128],
                                          cb3[:, c], start=False, stop=(c == NCH - 1))
                    s_pg.inc(i)

        @block.scalar
        def _(scalar, t=t):
            s_wb.wait(scalar, 64)
            for m in range(NCH):
                g = t * NCH + m
                pg = p_gA if m % 2 == 0 else p_gB
                s_pg.wait(scalar, 3 * g + 3)
                if g >= 1:
                    s_cu.wait(scalar, 1 + 3 * g)
                scalar.activation(gsc[:, 0:TILE], pg[0][:, :], AF.Sigmoid, bias=bias[:, m:m + 1])
                scalar.activation(gsc[:, TILE:2 * TILE], pg[1][:, :], AF.Sigmoid, bias=bias[:, NCH + m:NCH + m + 1])
                i = scalar.activation(gsc[:, 2 * TILE:3 * TILE], pg[2][:, :], AF.Tanh, bias=bias[:, 2 * NCH + m:2 * NCH + m + 1])
                s_ag.inc(i, 3)

        @block.vector
        def _(vector, t=t):
            for m in range(NCH):
                g = t * NCH + m
                s_ag.wait(vector, 3 * g + 3)
                vector.tensor_tensor(gsc[:, TILE:2 * TILE], gsc[:, TILE:2 * TILE], gsc[:, 2 * TILE:3 * TILE], ALU.mult)
                vector.tensor_tensor(c3[:, m], c3[:, m], gsc[:, 0:TILE], ALU.mult)
                i = vector.tensor_add(c3[:, m], c3[:, m], gsc[:, TILE:2 * TILE])
                s_cu.inc(i, 3)

        ncu += 3 * NCH
        npgW += 3 * NCH

    @block.gpsimd
    def _(gpsimd):
        s_pg.wait(gpsimd, npgW)
        s_w2.dma(gpsimd.dma_start(out=w3["woT"], in_=ctx.win["l0_woT"].rearrange("(c p) m -> p c m", p=128)))
        s_w2.dma(gpsimd.dma_start(out=w3["uoT"], in_=ctx.win["l0_uoT"].rearrange("(c p) m -> p c m", p=128)))

    @block.vector
    def _(vector):
        s_cu.wait(vector, ncu)
        s_cb.inc(vector.tensor_copy(cbf[:, :], cst[:, :]))

    @block.tensor
    def _(tensor):
        s_cb.wait(tensor, nsteps + 1)
        s_w2.wait(tensor, 32)
        for m in range(NCH):
            pox = p_gA[m % 3]
            if m >= 3:
                s_uo.wait(tensor, m - 2)
            for c in range(NCH):
                i = tensor.matmul(pox[:, :], w3["uoT"][:, c, m * 128:(m + 1) * 128],
                                  cb3[:, c], start=(c == 0), stop=(c == NCH - 1))
            s_pg.inc(i)

    @block.vector
    def _(vector):
        base = npgW
        for m in range(NCH):
            s_pg.wait(vector, base + m + 1)
            s_uo.inc(vector.tensor_copy(uo3[:, m], p_gA[m % 3][:, :]))

    @block.sync
    def _(sync):
        s_pg.wait(sync, npgW)
        for t in range(NB):
            b = t % 2
            if t >= 1:
                s_r.wait(sync, 16 * t)
            s_x.dma(sync.dma_start(out=x3, in_=fm3(x_src)[:, :, t * TILE:(t + 1) * TILE]))
            if t >= 2:
                s_o.wait(sync, NCH * (t - 1))
            p_h2.mark(t, sync.dma_start(out=h4b[:, b], in_=fm3(h_src)[:, :, t * TILE:(t + 1) * TILE]))

    npg2 = npgW + NCH

    @block.tensor
    def _(tensor):
        s_uo.wait(tensor, NCH)
        for t in range(NB):
            bh = t % 2
            p_h2.wait(tensor, t)
            for m in range(NCH):
                k = t * NCH + m
                po = p_out[k % 2]
                if k >= 2:
                    s_o.wait(tensor, k - 1)
                for c in range(NCH):
                    i = tensor.matmul(po[:, :], w3["woT"][:, c, m * 128:(m + 1) * 128],
                                      h4b[:, bh, c], start=(c == 0), stop=(c == NCH - 1))
                s_pg.inc(i)

    def _apply(vector, k):
        t, m = divmod(k, NCH)
        s_oa.wait(vector, k + 1)
        s_x.wait(vector, 16 * (t + 1))
        i = vector.scalar_tensor_tensor(x3[:, m],
                                        oscr[:, (k % 2) * TILE:(k % 2 + 1) * TILE],
                                        0.5, x3[:, m], ALU.mult, ALU.add)
        s_ra.inc(i)

    @block.vector
    def _(vector):
        for k in range(NB * NCH):
            m = k % NCH
            po = p_out[k % 2]
            if k >= 1:
                _apply(vector, k - 1)
            s_pg.wait(vector, npg2 + k + 1)
            s_o.inc(vector.tensor_add(oscr[:, (k % 2) * TILE:(k % 2 + 1) * TILE], po[:, :], uo3[:, m]))
        _apply(vector, NB * NCH - 1)

    @block.scalar
    def _(scalar):
        for t in range(NB):
            for m in range(NCH):
                k = t * NCH + m
                s_o.wait(scalar, k + 1)
                i = scalar.activation(oscr[:, (k % 2) * TILE:(k % 2 + 1) * TILE],
                                      oscr[:, (k % 2) * TILE:(k % 2 + 1) * TILE],
                                      AF.Tanh, bias=bias[:, 3 * NCH + m:3 * NCH + m + 1])
                s_oa.inc(i)

    @block.gpsimd
    def _(gpsimd):
        for t in range(NB):
            s_ra.wait(gpsimd, (t + 1) * NCH)
            s_r.dma(gpsimd.dma_start(out=fm3(x_dst)[:, :, t * TILE:(t + 1) * TILE], in_=x3))

    ctx.phase_done = (s_r, s_r.n)



def l0_chunklist(qt):
    out = []
    for r in range(NCORE):
        for jj in range(4):
            if 16 * jj + r >= 32 * (qt + 1):
                continue
            masked = not (16 * jj + r + 8 < 32 * qt)
            out.append((r * 4 + jj, masked))
    return out


def emit_l0attn(ctx, h_src, x_src, x_dst):
    nc, block = ctx.nc, ctx.block
    s_w = phase_sem(ctx, "a0w", kind="sw")
    p_h = phase_pipe(ctx, "a0h")
    s_qp = phase_sem(ctx, "a0qp")
    s_qe = phase_sem(ctx, "a0qe")
    p_kst = phase_pipe(ctx, "a0kst", 4, kind="sw")
    p_vst = phase_pipe(ctx, "a0vst", 2, kind="sw")
    s_cc = phase_sem(ctx, "a0cc", kind="coll")
    p_kld = phase_pipe(ctx, "a0kld")
    p_vld = phase_pipe(ctx, "a0vld")
    p_vld2 = phase_pipe(ctx, "a0vld2")
    s_sp = phase_sem(ctx, "a0sp")
    s_pb = phase_sem(ctx, "a0pb")
    s_pm = phase_sem(ctx, "a0pm")
    s_om = phase_sem(ctx, "a0om")
    s_on = phase_sem(ctx, "a0on")
    p_x = phase_pipe(ctx, "a0x")
    s_r = phase_sem(ctx, "a0r", kind="sw")

    wq = sbt(ctx, "a0wq", [128, NCH * D], BF16)
    wk = sbt(ctx, "a0wk", [128, NCH * D], BF16)
    wv = sbt(ctx, "a0wv", [128, NCH * D], BF16)
    hbuf = sbt(ctx, "a0hb", [128, 2 * NCH * TILE], BF16)
    qsb = sbt(ctx, "a0q", [128, NCH * NT], BF16)
    arena = sbt(ctx, "a0arena", [128, 8192], BF16)
    kbuf = sbt(ctx, "a0kb", [128, 3 * NCH * 128], BF16)
    vbuf = sbt(ctx, "a0vb", [128, 3 * D], BF16)
    mtile = sbt(ctx, "a0mt", [128, 256], F32)
    nsc = sbt(ctx, "a0ns", [128, 2 * 256], F32)
    xb = sbt(ctx, "a0xb", [128, NCH * 256], F32)

    w3 = {n: t.ap().rearrange("p (c m) -> p c m", c=NCH) for n, t in
          (("wq", wq), ("wk", wk), ("wv", wv))}
    h4 = hbuf.ap().rearrange("p (b c n) -> p b c n", b=2, c=NCH)
    q3 = qsb.ap().rearrange("p (c n) -> p c n", c=NCH)
    k3s = arena.ap()[:, 0:NCH * TILE].rearrange("p (c n) -> p c n", c=NCH)
    v3s = arena.ap()[:, NCH * TILE:NCH * TILE + 4 * D].rearrange("p (a d) -> p a d", a=4)
    kb4 = kbuf.ap().rearrange("p (b c n) -> p b c n", b=3, c=NCH)
    vb3 = vbuf.ap().rearrange("p (b d) -> p b d", b=3)
    x3 = xb.ap().rearrange("p (c n) -> p c n", c=NCH)

    if not hasattr(ctx, "a0_dram"):
        kv_k_in = nc.dram_tensor("a0kvk_in", [4 * NCH * 128 * NB, 128], BF16, kind="Internal").ap()
        kv_v_in = nc.dram_tensor("a0kvv_in", [NT, D], BF16, kind="Internal").ap()
        k_oAB = [nc.dram_tensor(f"a0ko{h}", [NCORE * 16384, 128], BF16, kind="Internal").ap() for h in "AB"]
        v_oAB = [nc.dram_tensor(f"a0vo{h}", [NCORE * 2048, D], BF16, kind="Internal").ap() for h in "AB"]
        ctx.a0_dram = (kv_k_in, kv_v_in, k_oAB, v_oAB)
    kv_k_in, kv_v_in, k_oAB, v_oAB = ctx.a0_dram

    pQ = [ctx.ps512[i] for i in range(2)]
    NG = 24

    @block.gpsimd
    def _(gpsimd):
        for nm, t3 in (("l0_wq", "wq"), ("l0_wk", "wk"), ("l0_wv", "wv")):
            s_w.dma(gpsimd.dma_start(out=w3[t3], in_=ctx.win[nm].rearrange("(c p) m -> p c m", p=128)))

    @block.sync
    def _(sync):
        for t in range(NTILES):
            b = t % 2
            ln_wait_cols(ctx, sync, (t + 1) * TILE)
            if t >= 2:
                s_qp.wait(sync, NG * (t - 1))
            p_h.mark(t, sync.dma_start(out=h4[:, b], in_=fm3(h_src)[:, :, t * TILE:(t + 1) * TILE]))

    @block.tensor
    def _(tensor):
        s_w.wait(tensor, 48)
        for t in range(NTILES):
            b = t % 2
            p_h.wait(tensor, t)
            for j in range(NG):
                g = NG * t + j
                pp = pQ[g % 2]
                if g >= 2:
                    s_qe.wait(tensor, g - 1)
                if j < 16:
                    wsel, co = ("wq", j) if j < 8 else ("wk", j - 8)
                    for c in range(NCH):
                        i = tensor.matmul(pp[:, :], w3[wsel][:, c, co * 128:(co + 1) * 128],
                                          h4[:, b, c], start=(c == 0), stop=(c == NCH - 1))
                else:
                    a, no = divmod(j - 16, 2)
                    for c in range(NCH):
                        i = tensor.matmul(pp[:, :], h4[:, b, c, a * 128:(a + 1) * 128],
                                          w3["wv"][:, c, no * TILE:(no + 1) * TILE],
                                          start=(c == 0), stop=(c == NCH - 1))
                s_qp.inc(i)

    @block.scalar
    def _(scalar):
        for t in range(NTILES):
            for j in range(NG):
                g = NG * t + j
                pp = pQ[g % 2]
                s_qp.wait(scalar, g + 1)
                if j < 8:
                    dst = q3[:, j, t * TILE:(t + 1) * TILE]
                elif j < 16:
                    if j == 8:
                        for jj in range(4):
                            if t >= 1:
                                p_kst.wait(scalar, 4 * (t - 1) + jj)
                    dst = k3s[:, j - 8]
                else:
                    a, no = divmod(j - 16, 2)
                    if j == 16 and t >= 1:
                        p_vst.wait(scalar, t - 1)
                    dst = v3s[:, a, no * TILE:(no + 1) * TILE]
                s_qe.inc(scalar.copy(dst, pp[:, :]))

    @block.gpsimd
    def _(gpsimd):
        for t in range(NTILES):
            s_qe.wait(gpsimd, NG * t + 16)
            for jj in range(4):
                p_kst.mark(4 * t + jj, gpsimd.dma_start(
                    out=kv_k_in[t * 4096 + jj * 1024:t * 4096 + (jj + 1) * 1024, :]
                    .rearrange("(c p) n -> p c n", p=128),
                    in_=k3s[:, :, jj * 128:(jj + 1) * 128]))
            s_qe.wait(gpsimd, NG * t + 24)
            p_vst.mark(t, gpsimd.dma_start(
                out=kv_v_in[t * TILE:(t + 1) * TILE, :].rearrange("(a p) d -> p a d", p=128),
                in_=v3s))
            if t == 3 or t == 7:
                hh = t // 4
                for jj in range(4):
                    p_kst.wait(gpsimd, 4 * t + jj)
                p_vst.wait(gpsimd, t)
                if FAKE_COLL:
                    for r in range(NCORE):
                        gpsimd.dma_start(out=k_oAB[hh][r * 16384:(r + 1) * 16384, :],
                                         in_=kv_k_in[hh * 16384:(hh + 1) * 16384, :]).then_inc(s_cc.h, 16)
                        gpsimd.dma_start(out=v_oAB[hh][r * 2048:(r + 1) * 2048, :],
                                         in_=kv_v_in[hh * 2048:(hh + 1) * 2048, :]).then_inc(s_cc.h, 16)
                    s_cc.n += 256
                else:
                    gpsimd.collective_compute("AllGather", ALU.bypass,
                                              replica_groups=[list(range(NCORE))],
                                              ins=[kv_k_in[hh * 16384:(hh + 1) * 16384, :].opt()],
                                              outs=[k_oAB[hh].opt()]).then_inc(s_cc.h, 1)
                    s_cc.n += 1
                    gpsimd.collective_compute("AllGather", ALU.bypass,
                                              replica_groups=[list(range(NCORE))],
                                              ins=[kv_v_in[hh * 2048:(hh + 1) * 2048, :].opt()],
                                              outs=[v_oAB[hh].opt()]).then_inc(s_cc.h, 1)
                    s_cc.n += 1
    cc_done = s_cc.n

    groups = []
    for b in range(NB):
        for qt in range(2):
            groups.append((b, qt, l0_chunklist(qt)))

    pal3 = arena.ap().rearrange("p (a n) -> p a n", n=256)
    vbufB = sbt(ctx, "a0vbB", [128, 3 * D], BF16)
    vbB3 = vbufB.ap().rearrange("p (b d) -> p b d", b=3)

    pO = ctx.ps512[0:4]
    pSab = [ctx.ps512[4], ctx.ps512[5], pst(ctx, "lnps2", [128, LTILE])]
    pC = pst(ctx, "lnps", [128, LTILE])

    sched = []
    si = 0
    om = 0
    vA = 0
    vB = 0
    for gi, (b, qt, lst) in enumerate(groups):
        g = {"b": b, "qt": qt, "lst": lst, "si0": si, "vA0": vA, "vB0": vB,
             "om0": om, "gi": gi}
        si += len(lst)
        vA += len(lst)
        g["omA"] = om = om + len(lst)
        g["omB"] = om = om + len(lst)
        vB += len(lst)
        sched.append(g)

    @block.sync
    def _(sync):
        s_cc.wait(sync, cc_done)
        for gi, g in enumerate(sched):
            b, qt, lst = g["b"], g["qt"], g["lst"]
            if gi >= 1:
                s_r.wait(sync, 16 * gi)
            p_x.mark(gi, sync.dma_start(
                out=x3, in_=fm3(x_src)[:, :, b * TILE + qt * 256:b * TILE + qt * 256 + 256]))
            for ci, (cc, masked) in enumerate(lst):
                r, jj = divmod(cc, 4)
                ksi = g["si0"] + ci
                kvi = g["vA0"] + ci
                if ksi >= 3:
                    s_sp.wait(sync, ksi - 2)
                p_kld.mark(ksi, sync.dma_start(
                    out=kb4[:, ksi % 3],
                    in_=k_oAB[b // 4][r * 16384 + (b % 4) * 4096 + jj * 1024:
                                      r * 16384 + (b % 4) * 4096 + (jj + 1) * 1024, :]
                    .rearrange("(c p) n -> p c n", p=128)))
                if ci >= 3:
                    s_om.wait(sync, g["om0"] + ci - 2)
                p_vld.mark(kvi, sync.dma_start(
                    out=vb3[:, kvi % 3],
                    in_=v_oAB[b // 4][r * 2048 + (b % 4) * 512 + jj * 128:
                                      r * 2048 + (b % 4) * 512 + (jj + 1) * 128, :]))
            for ci, (cc, masked) in enumerate(lst):
                r, jj = divmod(cc, 4)
                kvi = g["vB0"] + ci
                if ci >= 3:
                    s_om.wait(sync, g["omA"] + ci - 2)
                p_vld2.mark(kvi, sync.dma_start(
                    out=vbB3[:, kvi % 3],
                    in_=v_oAB[b // 4][r * 2048 + (b % 4) * 512 + jj * 128:
                                      r * 2048 + (b % 4) * 512 + (jj + 1) * 128, :]))

    @block.tensor
    def _(tensor):
        s_qe.wait(tensor, NG * NTILES)

        def _oA(g, ci):
            nA = len(g["lst"])
            ksi = g["si0"] + ci
            s_pm.wait(tensor, ksi + 1)
            p_vld.wait(tensor, g["vA0"] + ci)
            if ci == 0:
                s_on.wait(tensor, 2 * g["gi"])
            for c in range(4):
                tensor.matmul(pO[c][:, 0:256], vb3[:, (g["vA0"] + ci) % 3, c * 128:(c + 1) * 128],
                              pal3[:, ci], start=(ci == 0), stop=(ci == nA - 1))
            i = tensor.matmul(pC[:, 0:256], ctx.ones_bf[:, :], pal3[:, ci],
                              start=(ci == 0), stop=(ci == nA - 1))
            s_om.inc(i)

        def _oB(g, ci):
            nA = len(g["lst"])
            p_vld2.wait(tensor, g["vB0"] + ci)
            if ci == 0:
                s_on.wait(tensor, 2 * g["gi"] + 1)
            for c in range(4):
                i = tensor.matmul(pO[c][:, 256:512], vbB3[:, (g["vB0"] + ci) % 3, (4 + c) * 128:(5 + c) * 128],
                                  pal3[:, ci], start=(ci == 0), stop=(ci == nA - 1))
            s_om.inc(i)

        for gi, g in enumerate(sched):
            b, qt, lst = g["b"], g["qt"], g["lst"]
            for ci, (cc, masked) in enumerate(lst):
                ksi = g["si0"] + ci
                p_kld.wait(tensor, ksi)
                if ksi >= 3:
                    s_pb.wait(tensor, ksi - 2)
                for c in range(NCH):
                    i = tensor.matmul(pSab[ksi % 3][:, 0:256],
                                      kb4[:, ksi % 3, c],
                                      q3[:, c, b * TILE + qt * 256:b * TILE + qt * 256 + 256],
                                      start=(c == 0), stop=(c == NCH - 1))
                s_sp.inc(i)
                if ci >= 1:
                    _oA(g, ci - 1)
            _oA(g, len(lst) - 1)
            for ci in range(len(lst)):
                _oB(g, ci)

    @block.scalar
    def _(scalar):
        for gi, g in enumerate(sched):
            for ci, (cc, masked) in enumerate(g["lst"]):
                ksi = g["si0"] + ci
                s_sp.wait(scalar, ksi + 1)
                if gi >= 1:
                    s_om.wait(scalar, sched[gi - 1]["omB"])
                s_pb.inc(scalar.activation(pal3[:, ci], pSab[ksi % 3][:, 0:256],
                                           AF.Exp, bias=ctx.misc[:, 0:1], scale=1.0 / 32.0))

    @block.vector
    def _(vector):
        for gi, g in enumerate(sched):
            b, qt, lst = g["b"], g["qt"], g["lst"]
            for ci, (cc, masked) in enumerate(lst):
                ksi = g["si0"] + ci
                s_pb.wait(vector, ksi + 1)
                if masked:
                    vector.tensor_scalar(mtile[:, :], ctx.qglob[:, qt * 256:(qt + 1) * 256],
                                         ctx.kglob[:, cc:cc + 1], None, ALU.is_ge)
                    i = vector.tensor_tensor(pal3[:, ci], pal3[:, ci], mtile[:, :], ALU.mult)
                else:
                    i = vector.engine_nop()
                s_pm.inc(i)
            s_om.wait(vector, g["omA"])
            p_x.wait(vector, gi)
            vector.reciprocal(nsc[:, 0:256], pC[:, 0:256])
            for c in range(4):
                vector.tensor_tensor(nsc[:, 256:512], pO[c][:, 0:256], nsc[:, 0:256], ALU.mult)
                i = vector.tensor_add(x3[:, c], x3[:, c], nsc[:, 256:512])
            s_on.inc(i)
            s_om.wait(vector, g["omB"])
            for c in range(4):
                vector.tensor_tensor(nsc[:, 256:512], pO[c][:, 256:512], nsc[:, 0:256], ALU.mult)
                i = vector.tensor_add(x3[:, 4 + c], x3[:, 4 + c], nsc[:, 256:512])
            s_on.inc(i)

    @block.gpsimd
    def _(gpsimd):
        for gi, g in enumerate(sched):
            b, qt = g["b"], g["qt"]
            s_on.wait(gpsimd, 2 * (gi + 1))
            s_r.dma(gpsimd.dma_start(
                out=fm3(x_dst)[:, :, b * TILE + qt * 256:b * TILE + qt * 256 + 256], in_=x3))

    ctx.phase_done = (s_r, s_r.n)



def emit_l1attn(ctx, h_src, x_src, x_dst):
    nc, block = ctx.nc, ctx.block
    s_w = phase_sem(ctx, "a1w", kind="sw")
    s_w2 = phase_sem(ctx, "a1w2", kind="sw")
    p_h = phase_pipe(ctx, "a1h")
    s_qp = phase_sem(ctx, "a1qp")
    s_qe = phase_sem(ctx, "a1qe")
    s_qs = phase_sem(ctx, "a1qs")
    s_qv = phase_sem(ctx, "a1qv")
    s_qb = phase_sem(ctx, "a1qb")
    s_vv = phase_sem(ctx, "a1vv")
    s_sd = phase_sem(ctx, "a1sd")
    s_ql = phase_sem(ctx, "a1ql")
    s_vp = phase_sem(ctx, "a1vp")
    s_ve = phase_sem(ctx, "a1ve")
    p_qst = phase_pipe(ctx, "a1qst", 2, kind="sw")
    p_kst = phase_pipe(ctx, "a1kst", 4, kind="sw")
    p_vst = phase_pipe(ctx, "a1vst", 4, kind="sw")
    s_cc = phase_sem(ctx, "a1cc", kind="coll")
    p_qld = phase_pipe(ctx, "a1qld")
    p_kld = phase_pipe(ctx, "a1kld")
    p_vld = phase_pipe(ctx, "a1vld")
    s_sp = phase_sem(ctx, "a1sp")
    s_pb = phase_sem(ctx, "a1pb")
    s_om = phase_sem(ctx, "a1om")
    s_oe = phase_sem(ctx, "a1oe")
    s_wp = phase_sem(ctx, "a1wp")
    s_wv = phase_sem(ctx, "a1wv")
    p_x = phase_pipe(ctx, "a1x")
    s_r = phase_sem(ctx, "a1r", kind="sw")

    wq = sbt(ctx, "a1wq", [128, NCH * D], BF16)
    wk = sbt(ctx, "a1wk", [128, NCH * D], BF16)
    wv = sbt(ctx, "a1wv", [128, NCH * D], BF16)
    bqr = sbt(ctx, "a1bq", [1, 4 * D], BF16)
    selc = sbt(ctx, "a1sel", [128, 2], BF16)
    selb = sbt(ctx, "a1selb", [128, 128], F32)
    qkp = sbt(ctx, "a1qkp", [128, 4], F32)
    inib = sbt(ctx, "a1inib", [128, 1], F32)
    gam = sbt(ctx, "a1gam", [128, NCH], F32)
    hbuf = sbt(ctx, "a1hb", [128, NCH * TILE], BF16)
    qkarena = sbt(ctx, "a1qka", [128, 2 * 2 * NCH * TILE], BF16)
    vst_t = sbt(ctx, "a1vst", [128, 4 * D], BF16)
    stage = sbt(ctx, "a1stg", [128, 4 * 1024], BF16)
    srow = sbt(ctx, "a1srow", [128, 2 * 512], F32)
    vsc = sbt(ctx, "a1vsc", [128, 2 * TILE], F32)
    sdr = sbt(ctx, "a1sdr", [128, TILE], F32)
    qb_t = sbt(ctx, "a1qb2", [128, 2 * NCH * TILE], BF16)
    kbuf = sbt(ctx, "a1kb", [128, 4 * 128], BF16)
    vbuf = sbt(ctx, "a1vb", [128, 4 * 128], BF16)
    pbuf = sbt(ctx, "a1pbf", [128, 4 * TILE], BF16)
    osb = sbt(ctx, "a1osb", [128, 2 * NCH * TILE], BF16)
    xb = sbt(ctx, "a1xb", [128, NCH * TILE], F32)

    w3 = {"wq": wq.ap().rearrange("p (c m) -> p c m", c=NCH),
          "wk": wk.ap().rearrange("p (c m) -> p c m", c=NCH),
          "wv": wv.ap().rearrange("p (c m) -> p c m", c=NCH)}
    w3["wo"] = w3["wq"]
    h4 = hbuf.ap().rearrange("p (o c n) -> p o c n", o=1, c=NCH)
    qk5 = qkarena.ap().rearrange("p (s x c n) -> p s x c n", s=2, x=2, c=NCH)
    v3s = vst_t.ap().rearrange("p (a d) -> p a d", a=4)
    st4 = stage.ap().rearrange("p (r n) -> p r n", r=4)
    sr3 = srow.ap().rearrange("p (r n) -> p r n", r=2)
    qb4 = qb_t.ap().rearrange("p (s c n) -> p s c n", s=2, c=NCH)
    kb3 = kbuf.ap().rearrange("p (b n) -> p b n", b=4)
    vb3 = vbuf.ap().rearrange("p (b n) -> p b n", b=4)
    pb3 = pbuf.ap().rearrange("p (b n) -> p b n", b=4)
    ob4 = osb.ap().rearrange("p (b c n) -> p b c n", b=2, c=NCH)
    x3 = xb.ap().rearrange("p (c n) -> p c n", c=NCH)

    if not hasattr(ctx, "a1_dram"):
        k_in = nc.dram_tensor("a1kvk_in", [NB * 4 * NCH * 128, 128], BF16, kind="Internal").ap()
        v_in = nc.dram_tensor("a1kvv_in", [NCH * NT, 128], BF16, kind="Internal").ap()
        k_oAB = [nc.dram_tensor(f"a1ko{h}", [NCORE * 16384, 128], BF16, kind="Internal").ap() for h in "AB"]
        v_oAB = [nc.dram_tensor(f"a1vo{h}", [NCORE * 16384, 128], BF16, kind="Internal").ap() for h in "AB"]
        q_dram = nc.dram_tensor("a1q_dram", [D, NT], BF16, kind="Internal").ap()
        ctx.a1_dram = (k_in, v_in, k_oAB, v_oAB, q_dram)
    k_in, v_in, k_oAB, v_oAB, q_dram = ctx.a1_dram

    pQ = ctx.ps512[0:2]
    pstat = ctx.ps512[2]
    pbc = ctx.ps512[3:5]
    pV = ctx.ps512[5]
    NU = 16

    @block.gpsimd
    def _(gpsimd):
        for nm, t3 in (("l1_wq", "wq"), ("l1_wk", "wk"), ("l1_wv", "wv")):
            s_w.dma(gpsimd.dma_start(out=w3[t3], in_=ctx.win[nm].rearrange("(c p) m -> p c m", p=128)))
        for i, nm in enumerate(("l1_bq", "l1_bk", "l1_bv", "l1_bo")):
            s_w.dma(gpsimd.dma_start(out=bqr[0:1, i * D:(i + 1) * D], in_=ctx.win[nm][:, :]))
        s_w.dma(gpsimd.dma_start(out=selc[:, :], in_=ctx.cin["sel2"][:, :]))

    s_wc = phase_sem(ctx, "a1wc")

    @block.sync
    def _(sync):
        s_wc.dma(sync.dma_start(out=selb[:, :], in_=ctx.cin["selB2"][:, :]))
        s_wc.dma(sync.dma_start(out=qkp[:, 0:1], in_=ctx.win["l1_qns"][:, :]))
        s_wc.dma(sync.dma_start(out=qkp[:, 1:2], in_=ctx.win["l1_qnb"][:, :]))
        s_wc.dma(sync.dma_start(out=qkp[:, 2:3], in_=ctx.win["l1_kns"][:, :]))
        s_wc.dma(sync.dma_start(out=qkp[:, 3:4], in_=ctx.win["l1_knb"][:, :]))
        s_wc.dma(sync.dma_start(out=inib[:, :], in_=ctx.win["l1_initb"][:, :]))
        s_wc.dma(sync.dma_start(out=gam[:, :], in_=ctx.win["l1_gamma"][:, :]))
        for t in range(NTILES):
            ln_wait_cols(ctx, sync, (t + 1) * TILE)
            if t >= 1:
                s_qp.wait(sync, NU * t)
                s_vp.wait(sync, 8 * t)
            p_h.mark(t, sync.dma_start(out=h4[:, 0], in_=fm3(h_src)[:, :, t * TILE:(t + 1) * TILE]))

    WALL = 16 * 8

    def _stats(tensor, U):
        s_qe.wait(tensor, U + 1)
        if U >= 1:
            s_qv.wait(tensor, U)
        tensor.matmul(pstat[0:2, 0:512], selc[:, :], st4[:, U % 4, 0:512], start=True, stop=True)
        i = tensor.matmul(pstat[64:66, 0:512], selc[:, :], st4[:, U % 4, 512:1024], start=True, stop=True)
        s_qs.inc(i)

    def _bcast(tensor, U):
        s_qv.wait(tensor, U + 1)
        if U >= 1:
            s_vv.wait(tensor, U)
        tensor.matmul(pbc[0][:, :], selb[0:2, :], sr3[0:2, U % 2], start=True, stop=True)
        i = tensor.matmul(pbc[1][:, :], selb[64:66, :], sr3[64:66, U % 2], start=True, stop=True)
        s_qb.inc(i)

    @block.tensor
    def _(tensor):
        s_w.wait(tensor, WALL)
        s_wc.wait(tensor, 16 * 7)
        for t in range(NTILES):
            bh = t % 2
            p_h.wait(tensor, t)
            for u in range(NU):
                U = NU * t + u
                X, co = u % 2, u // 2
                pp = pQ[U % 2]
                if U >= 2:
                    s_qe.wait(tensor, U - 1)
                for c in range(NCH):
                    tensor.matmul(pp[:, :], w3["wq" if X == 0 else "wk"][:, c, co * 128:(co + 1) * 128],
                                  h4[:, 0, c], start=(c == 0), stop=False)
                i = tensor.matmul(pp[:, :], bqr[0:1, X * D + co * 128:X * D + (co + 1) * 128],
                                  ctx.ones_row[0:1, :], start=False, stop=True)
                s_qp.inc(i)
                if U >= 1:
                    _stats(tensor, U - 1)
                if U >= 2:
                    _bcast(tensor, U - 2)
            for vg in range(8):
                a, no = divmod(vg, 2)
                G = 8 * t + vg
                if G >= 1:
                    s_ve.wait(tensor, G)
                for c in range(NCH):
                    tensor.matmul(pV[:, :], h4[:, 0, c, a * 128:(a + 1) * 128],
                                  w3["wv"][:, c, no * 512:(no + 1) * 512],
                                  start=(c == 0), stop=False)
                i = tensor.matmul(pV[:, :], ctx.ones_row[0:1, 0:128],
                                  bqr[0:1, 2 * D + no * 512:2 * D + (no + 1) * 512],
                                  start=False, stop=True)
                s_vp.inc(i)
        NUU = NU * NTILES
        _stats(tensor, NUU - 1)
        _bcast(tensor, NUU - 2)
        _bcast(tensor, NUU - 1)

    @block.scalar
    def _(scalar):
        for t in range(NTILES):
            for u in range(NU):
                U = NU * t + u
                s_qp.wait(scalar, U + 1)
                if U >= 3:
                    s_ql.wait(scalar, U - 3)
                scalar.copy(st4[:, U % 4, 0:512], pQ[U % 2][:, :])
                s_qe.inc(scalar.activation(st4[:, U % 4, 512:1024], pQ[U % 2][:, :], AF.Square))
                if U >= 1:
                    s_qs.wait(scalar, U)
                    scalar.copy(sr3[0:2, (U - 1) % 2], pstat[0:2, 0:512])
                    s_qv.inc(scalar.copy(sr3[64:66, (U - 1) % 2], pstat[64:66, 0:512]))
                if U >= 2:
                    s_vv.wait(scalar, U - 1)
                    s_sd.inc(scalar.activation(sdr[:, :], vsc[:, TILE:2 * TILE], AF.Sqrt, bias=ctx.misc[:, 2:3]))
            for vg in range(8):
                a, no = divmod(vg, 2)
                G = 8 * t + vg
                s_vp.wait(scalar, G + 1)
                if t >= 1:
                    for cslab in range(8):
                        p_vst.wait(scalar, 8 * (t - 1) + cslab)
                s_ve.inc(scalar.copy(v3s[:, a, no * 512:(no + 1) * 512], pV[:, :]))
        NUU = NU * NTILES
        for U in (NUU - 1,):
            s_qs.wait(scalar, U + 1)
            scalar.copy(sr3[0:2, U % 2], pstat[0:2, 0:512])
            s_qv.inc(scalar.copy(sr3[64:66, U % 2], pstat[64:66, 0:512]))
        for U in (NUU - 2, NUU - 1):
            s_vv.wait(scalar, U + 1)
            s_sd.inc(scalar.activation(sdr[:, :], vsc[:, TILE:2 * TILE], AF.Sqrt, bias=ctx.misc[:, 2:3]))

    @block.vector
    def _(vector):
        s_wc.wait(vector, 16 * 7)
        for U in range(NU * NTILES):
            t, u = divmod(U, NU)
            X, co = u % 2, u // 2
            s_qb.wait(vector, U + 1)
            m = vsc[:, 0:TILE]
            var = vsc[:, TILE:2 * TILE]
            vector.tensor_scalar_mul(m[:, :], pbc[0][:, :], 1.0 / 64.0)
            vector.tensor_scalar_mul(var[:, :], pbc[1][:, :], 1.0 / 64.0)
            vector.tensor_tensor(sdr[:, :], m[:, :], m[:, :], ALU.mult)
            s_vv.inc(vector.tensor_sub(var[:, :], var[:, :], sdr[:, :]))
            s_sd.wait(vector, U + 1)
            vector.reciprocal(var[:, :], sdr[:, :])
            if X == 1 and t >= 2:
                for jj in range(4):
                    p_kst.wait(vector, 4 * (t - 2) + jj)
            if X == 0 and t >= 2:
                p_qst.wait(vector, t - 2)
            dst = qk5[:, t % 2, X, co]
            vector.tensor_sub(sdr[:, :], st4[:, U % 4, 0:512], m[:, :])
            vector.tensor_tensor(sdr[:, :], sdr[:, :], var[:, :], ALU.mult)
            i = vector.tensor_scalar(dst, sdr[:, :], qkp[:, 2 * X:2 * X + 1], qkp[:, 2 * X + 1:2 * X + 2],
                                     ALU.mult, ALU.add)
            s_ql.inc(i)

    @block.gpsimd
    def _(gpsimd):
        for t in range(NTILES):
            s_ql.wait(gpsimd, NU * (t + 1))
            p_qst.mark(t, gpsimd.dma_start(
                out=fm3(q_dram)[:, :, t * TILE:(t + 1) * TILE], in_=qk5[:, t % 2, 0]))
            for jj in range(4):
                p_kst.mark(4 * t + jj, gpsimd.dma_start(
                    out=k_in[t * 4096 + jj * 1024:t * 4096 + (jj + 1) * 1024, :]
                    .rearrange("(c p) n -> p c n", p=128),
                    in_=qk5[:, t % 2, 1, :, jj * 128:(jj + 1) * 128]))
            s_ve.wait(gpsimd, 8 * (t + 1))
            for cslab in range(8):
                p_vst.mark(8 * t + cslab, gpsimd.dma_start(
                    out=v_in[t * 4096 + cslab * 512:t * 4096 + (cslab + 1) * 512, :]
                    .rearrange("(a p) n -> p a n", p=128),
                    in_=v3s[:, :, cslab * 128:(cslab + 1) * 128]))
            if t == 3 or t == 7:
                hh = t // 4
                for jj in range(4):
                    p_kst.wait(gpsimd, 4 * t + jj)
                for cs in range(8):
                    p_vst.wait(gpsimd, 8 * t + cs)
                if FAKE_COLL:
                    for r in range(NCORE):
                        gpsimd.dma_start(out=k_oAB[hh][r * 16384:(r + 1) * 16384, :],
                                         in_=k_in[hh * 16384:(hh + 1) * 16384, :]).then_inc(s_cc.h, 16)
                        gpsimd.dma_start(out=v_oAB[hh][r * 16384:(r + 1) * 16384, :],
                                         in_=v_in[hh * 16384:(hh + 1) * 16384, :]).then_inc(s_cc.h, 16)
                    s_cc.n += 256
                else:
                    gpsimd.collective_compute("AllGather", ALU.bypass,
                                              replica_groups=[list(range(NCORE))],
                                              ins=[k_in[hh * 16384:(hh + 1) * 16384, :].opt()],
                                              outs=[k_oAB[hh].opt()]).then_inc(s_cc.h, 1)
                    s_cc.n += 1
                    gpsimd.collective_compute("AllGather", ALU.bypass,
                                              replica_groups=[list(range(NCORE))],
                                              ins=[v_in[hh * 16384:(hh + 1) * 16384, :].opt()],
                                              outs=[v_oAB[hh].opt()]).then_inc(s_cc.h, 1)
                    s_cc.n += 1
        s_qp.wait(gpsimd, NU * NTILES)
        s_w2.dma(gpsimd.dma_start(out=w3["wo"], in_=ctx.win["l1_wo"].rearrange("(c p) m -> p c m", p=128)))
    cc_half = (s_cc.n - s_cc.base) // 2

    pO = ctx.ps512[0:2]
    pS2 = ctx.ps512[2:6]
    pW = ctx.ps512[0:2]

    @block.sync
    def _(sync):
        s_cc.wait(sync, cc_half * 2)
        for b in range(NB):
            hh = b // 4
            if b >= 1:
                s_r.wait(sync, 16 * b)
            p_x.mark(b, sync.dma_start(out=x3, in_=fm3(x_src)[:, :, b * TILE:(b + 1) * TILE]))
            if b >= 2:
                s_oe.wait(sync, 16 * (b - 1))
            p_qst.wait(sync, b)
            p_qld.mark(b, sync.dma_start(out=qb4[:, b % 2], in_=fm3(q_dram)[:, :, b * TILE:(b + 1) * TILE]))
            for cp in range(NCH):
                for kc in range(32):
                    r, jj = divmod(kc, 4)
                    LI = (b * NCH + cp) * 32 + kc
                    if LI >= 4:
                        s_sp.wait(sync, 2 * (LI - 3))
                        s_om.wait(sync, 2 * (LI - 3))
                    p_kld.mark(LI, sync.dma_start(
                        out=kb3[:, LI % 4],
                        in_=k_oAB[hh][r * 16384 + (b % 4) * 4096 + jj * 1024 + cp * 128:
                                      r * 16384 + (b % 4) * 4096 + jj * 1024 + (cp + 1) * 128, :]))
                    p_vld.mark(LI, sync.dma_start(
                        out=vb3[:, LI % 4],
                        in_=v_oAB[hh][r * 16384 + (b % 4) * 4096 + cp * 512 + jj * 128:
                                      r * 16384 + (b % 4) * 4096 + cp * 512 + (jj + 1) * 128, :]))

    @block.tensor
    def _(tensor):
        s_qe.wait(tensor, NU * NTILES)
        s_ve.wait(tensor, 8 * NTILES)
        s_w2.wait(tensor, 16)
        s_qv.wait(tensor)
        s_vv.wait(tensor)
        s_ql.wait(tensor)

        def _o_unit(SU):
            LI, h2 = divmod(SU, 2)
            b = LI // (NCH * 32)
            cp = (LI // 32) % NCH
            kc = LI % 32
            bank = pO[cp % 2]
            s_pb.wait(tensor, SU + 1)
            p_vld.wait(tensor, LI)
            if kc == 0 and b >= 1:
                s_wv.wait(tensor, 8 * b)
            i = tensor.matmul(bank[64 * h2:64 * h2 + 64, :],
                              vb3[:, LI % 4, 64 * h2:64 * h2 + 64],
                              pb3[:, SU % 4], start=(kc == 0), stop=(kc == 31))
            s_om.inc(i)

        def _wo(b):
            s_oe.wait(tensor, 16 * (b + 1))
            for dc in range(NCH):
                pp = pW[dc % 2]
                if b > 0 or dc >= 2:
                    s_wv.wait(tensor, 8 * b + dc - 1)
                for c in range(NCH):
                    tensor.matmul(pp[:, :], w3["wo"][:, c, dc * 128:(dc + 1) * 128],
                                  ob4[:, b % 2, c], start=(c == 0), stop=False)
                i = tensor.matmul(pp[:, :], bqr[0:1, 3 * D + dc * 128:3 * D + (dc + 1) * 128],
                                  ctx.ones_row[0:1, :], start=False, stop=True)
                s_wp.inc(i)

        for b in range(NB):
            if b >= 1:
                _wo(b - 1)
            p_qld.wait(tensor, b)
            SU0 = 2 * b * NCH * 32
            for cp in range(NCH):
                for kc in range(32):
                    LI = (b * NCH + cp) * 32 + kc
                    p_kld.wait(tensor, LI)
                    for h2 in range(2):
                        SU = 2 * LI + h2
                        if SU >= 4:
                            s_pb.wait(tensor, SU - 3)
                        i = tensor.matmul(pS2[SU % 4][:, :],
                                          kb3[64 * h2:64 * h2 + 64, LI % 4],
                                          qb4[64 * h2:64 * h2 + 64, b % 2, cp],
                                          start=True, stop=True)
                        s_sp.inc(i)
                        if SU > SU0:
                            _o_unit(SU - 1)
            _o_unit(2 * (b + 1) * NCH * 32 - 1)
        _wo(NB - 1)

    @block.scalar
    def _(scalar):
        for b in range(NB):
            for cp in range(NCH):
                for kc in range(32):
                    LI = (b * NCH + cp) * 32 + kc
                    for h2 in range(2):
                        SU = 2 * LI + h2
                        s_sp.wait(scalar, SU + 1)
                        if SU >= 4:
                            s_om.wait(scalar, SU - 3)
                        s_pb.inc(scalar.activation(pb3[:, SU % 4], pS2[SU % 4][:, :],
                                                   AF.Sigmoid, bias=inib[:, 0:1]))
                SUlast = 2 * ((b * NCH + cp) * 32 + 31) + 1
                s_om.wait(scalar, SUlast + 1)
                if b >= 2:
                    s_wp.wait(scalar, 8 * (b - 1))
                s_oe.inc(scalar.copy(ob4[:, b % 2, cp], pO[cp % 2][:, :]))
                s_oe.inc(scalar.nop())

    @block.vector
    def _(vector):
        for b in range(NB):
            p_x.wait(vector, b)
            for dc in range(NCH):
                s_wp.wait(vector, 8 * b + dc + 1)
                vector.tensor_scalar(sdr[:, :], pW[dc % 2][:, :], gam[:, dc:dc + 1], None, ALU.mult)
                i = vector.tensor_add(x3[:, dc], x3[:, dc], sdr[:, :])
                s_wv.inc(i)

    @block.gpsimd
    def _(gpsimd):
        for b in range(NB):
            s_wv.wait(gpsimd, 8 * (b + 1))
            s_r.dma(gpsimd.dma_start(out=fm3(x_dst)[:, :, b * TILE:(b + 1) * TILE], in_=x3))

    ctx.phase_done = (s_r, s_r.n)



NT3 = [(0, 512), (512, 1024), (1024, CAP)]
NT3_G = [2 * ((c1 - c0 + 127) // 128) for c0, c1 in NT3]


def emit_moe(ctx, h_src, x_src, x_dst):
    nc, block = ctx.nc, ctx.block
    s_ag = phase_sem(ctx, "m_ag", kind="coll")
    s_sl = phase_sem(ctx, "m_sl")
    s_bi = phase_sem(ctx, "m_bi")
    s_bp = phase_sem(ctx, "m_bp")
    s_ms = phase_sem(ctx, "m_ms")
    s_of = phase_sem(ctx, "m_of")
    p_ht = phase_pipe(ctx, "m_ht", 4)
    s_pf = phase_sem(ctx, "m_pf", kind="sw")
    s_em = phase_sem(ctx, "m_em")
    s_ez = phase_sem(ctx, "m_ez")
    s_sc = phase_sem(ctx, "m_sc", kind="sw")
    s_sm = phase_sem(ctx, "m_sm", kind="sw")
    p_ex = phase_pipe(ctx, "m_ex", 2)
    s_wc = phase_sem(ctx, "m_wcc", kind="sw")
    s_w2 = phase_sem(ctx, "m_w2")
    p_w1 = phase_pipe(ctx, "m_w1", 2)
    s_b12 = phase_sem(ctx, "m_b12")
    s_h1 = phase_sem(ctx, "m_h1")
    s_ge = phase_sem(ctx, "m_ge")
    p_hs = phase_pipe(ctx, "m_hs", 2)
    p_hl = phase_pipe(ctx, "m_hl", 2)
    p_mt = phase_pipe(ctx, "m_mt", 2)
    s_mc = phase_sem(ctx, "m_mc")
    s_y = phase_sem(ctx, "m_y")
    s_ye = phase_sem(ctx, "m_ye")
    s_ya = phase_sem(ctx, "m_ya", kind="sw")
    s_z = phase_sem(ctx, "m_z")
    p_yl = phase_pipe(ctx, "m_yl", 2)
    s_tr = phase_sem(ctx, "m_tr")
    s_rx = phase_sem(ctx, "m_rx")
    p_x = phase_pipe(ctx, "m_x")
    s_r = phase_sem(ctx, "m_r", kind="sw")

    S_all = sbt(ctx, "m_sall", [128, 256 * E], F32)
    bis = sbt(ctx, "m_bis", [128, 4 * E], F32)
    cnt = sbt(ctx, "m_cnt", [128, E], F32)
    cscr = sbt(ctx, "m_cscr", [128, 256], F32)
    mask = sbt(ctx, "m_mask", [128, 32], F32)
    pref = sbt(ctx, "m_pref", [128, 32], F32)
    cntc = sbt(ctx, "m_cntc", [128, 1], F32)
    offs = sbt(ctx, "m_offs", [128, 1], F32)
    dest = sbt(ctx, "m_dest", [128, 32], F32)
    destu = sbt(ctx, "m_destu", [128, 2 * 32], U32)
    pairs = sbt(ctx, "m_pairs", [128, 2 * 64], F32)
    htm = sbt(ctx, "m_htm", [128, 4 * D], BF16)
    em0 = sbt(ctx, "m_em0", [128, 18], F32)
    w2s = sbt(ctx, "m_w2s", [128, 32 * D], BF16)
    w1s = sbt(ctx, "m_w1s", [128, 2 * NCH * 128], BF16)
    b1e = sbt(ctx, "m_b1e", [1, FF], BF16)
    b2e = sbt(ctx, "m_b2e", [1, D], BF16)
    exf = sbt(ctx, "m_exf", [128, NCH * CAP], BF16)
    hstage = sbt(ctx, "m_hst", [128, 2 * CAP], BF16)
    hnt = sbt(ctx, "m_hnt", [128, 32 * 512], BF16)
    meta = sbt(ctx, "m_meta", [128, 2 * 2], F32)
    sctu = sbt(ctx, "m_sctu", [128, 9], U32)
    gbuf = sbt(ctx, "m_gbuf", [128, 9], F32)
    ysb = sbt(ctx, "m_ysb", [128, 2 * D], F32)
    zeros = sbt(ctx, "m_zero", [128, D], F32)
    zerosb = sbt(ctx, "m_zerob", [128, D], BF16)
    ytl = sbt(ctx, "m_ytl", [128, D], F32)
    xb = sbt(ctx, "m_xb", [128, NCH * 128], F32)

    sall3 = S_all.ap().rearrange("p (a e) -> p a e", e=E)
    thr = bis.ap()[:, 0:E]
    lo = bis.ap()[:, E:2 * E]
    hi = bis.ap()[:, 2 * E:3 * E]
    bsc = bis.ap()[:, 3 * E:4 * E]
    du3 = destu.ap().rearrange("p (s a) -> p s a", s=2)
    pr3 = pairs.ap().rearrange("p (s a t) -> p s a t", s=2, t=2)
    htm3 = htm.ap().rearrange("p (s d) -> p s d", s=4)
    w2v = w2s.ap().rearrange("p (f m) -> p f m", f=32)
    w1v = w1s.ap().rearrange("p (s c m) -> p s c m", s=2, c=NCH)
    exv = exf.ap().rearrange("p (c n) -> p c n", c=NCH)
    hsv = hstage.ap().rearrange("p (s n) -> p s n", s=2)
    hnv = hnt.ap().rearrange("p (f n) -> p f n", f=32)
    mv = meta.ap().rearrange("p (s t) -> p s t", s=2)
    yv = ysb.ap().rearrange("p (s d) -> p s d", s=2)
    ylv = ytl.ap().rearrange("p (s d) -> p s d", s=1)
    x3 = xb.ap().rearrange("p (c n) -> p c n", c=NCH)
    gloc = ctx.gates_tm.ap().rearrange("p (a e) -> p e a", e=E)
    gsg = ctx.gsig_tm.ap().rearrange("p (a e) -> p e a", e=E)

    if not hasattr(ctx, "m_dram"):
        ag_gout = nc.dram_tensor("m_aggout", [NCORE * NT, E], F32, kind="Internal").ap()
        ex_all = nc.dram_tensor("m_exall", [E * CAP, D], BF16, kind="Internal").ap()
        exmeta = nc.dram_tensor("m_exmeta", [E * CAP, 2], F32, kind="Internal").ap()
        h_big = nc.dram_tensor("m_hbig", [FF, CAP], BF16, kind="Internal").ap()
        y_moe = nc.dram_tensor("m_ymoe", [NT + 128, D], F32, kind="Internal").ap()
        w1c = nc.dram_tensor("m_w1c", [E, D, FF], BF16, kind="Internal").ap()
        w2c = nc.dram_tensor("m_w2c", [E, FF, D], BF16, kind="Internal").ap()
        b12c = nc.dram_tensor("m_b12c", [E, FF + D], BF16, kind="Internal").ap()
        ctx.m_dram = (ag_gout, ex_all, exmeta, h_big, y_moe, w1c, w2c, b12c)
        ctx.m_wcached = False
    ag_gout, ex_all, exmeta, h_big, y_moe, w1c, w2c, b12c = ctx.m_dram
    first_fill = not ctx.m_wcached

    pW = ctx.ps512[0:2]
    pTot = ctx.ps512[2]
    pOf = ctx.ps512[3]
    pTr = ctx.ps512[4:6]

    gm_sem, gm_n = ctx.gates_mark

    @block.gpsimd
    def _(gpsimd):
        if first_fill:
            for e in range(E):
                s_wc.dma(gpsimd.dma_start(out=w1c[e], in_=ctx.win["l1_w1"][e]))
                s_wc.dma(gpsimd.dma_start(out=w2c[e], in_=ctx.win["l1_w2"][e]))
                s_wc.dma(gpsimd.dma_start(out=b12c[e:e + 1, 0:FF], in_=ctx.win["l1_b1"][e:e + 1, :]))
                s_wc.dma(gpsimd.dma_start(out=b12c[e:e + 1, FF:FF + D], in_=ctx.win["l1_b2"][e:e + 1, :]))
        gm_sem.wait_abs(gpsimd, gm_n)
        if FAKE_COLL:
            for r in range(NCORE):
                gpsimd.dma_start(out=ag_gout[r * NT:(r + 1) * NT, :], in_=ctx.ag_gin[:, :]).then_inc(s_ag.h, 16)
            s_ag.n += 128
        else:
            gpsimd.collective_compute("AllGather", ALU.bypass,
                                      replica_groups=[list(range(NCORE))],
                                      ins=[ctx.ag_gin.opt()], outs=[ag_gout.opt()]).then_inc(s_ag.h, 1)
            s_ag.n += 1
    ctx.m_wcached = True

    @block.vector
    def _(vector):
        vector.memset(zeros[:, :], 0.0)
        vector.memset(zerosb[:, :], 0.0)
        vector.memset(lo, -64.0)
        vector.memset(hi, 64.0)
        s_bi.inc(vector.memset(thr, 0.0))

    @block.sync
    def _(sync):
        s_ag.wait(sync)
        s_sl.dma(sync.dma_start(out=sall3, in_=ag_gout.rearrange("(p a) e -> p a e", p=128)))
        s_bi.wait(sync, 1)
        for zc in range(33):
            s_z.dma(sync.dma_start(out=y_moe[zc * 128:(zc + 1) * 128, :], in_=zeros[:, :]))
        for zc in range(E * CAP // 128):
            s_ez.dma(sync.dma_start(out=ex_all[zc * 128:(zc + 1) * 128, :], in_=zerosb[:, :]))
        s_em.dma(sync.dma_start(out=em0[:, :], in_=ctx.cin["exmeta0s"][:, :]))

    for it in range(N_BISECT):
        @block.vector
        def _(vector, it=it):
            if it == 0:
                s_sl.wait(vector, 16)
            vector.wait_ge(s_bi.h, s_bi.n)
            i = None
            for e in range(E):
                i = vector.tensor_scalar(cscr[:, :], sall3[:, :, e], thr[:, e:e + 1], None,
                                         ALU.is_ge, ALU.add, accum_out=cnt[:, e:e + 1])
            s_bi.inc(i)

        @block.tensor
        def _(tensor, it=it):
            s_bi.wait(tensor)
            i = tensor.matmul(pTot[:, 0:E], ctx.ones_f32[:, :], cnt[:, :], start=True, stop=True)
            s_bp.inc(i)

        @block.vector
        def _(vector, it=it):
            s_bp.wait(vector, it + 1)
            w = 64.0 / (2.0 ** it)

            def step(i):
                s_bi.inc(i)
                vector.wait_ge(s_bi.h, s_bi.n)
            step(vector.tensor_scalar(bsc, pTot[:, 0:E], float(KSEL), None, ALU.is_ge))
            step(vector.tensor_scalar(bsc, bsc, 2.0 * w, -w, ALU.mult, ALU.add))
            step(vector.tensor_add(thr, thr, bsc))

    for e in range(E):
        @block.vector
        def _(vector, e=e):
            if e >= 2:
                s_sc.wait(vector, 16 * 32 * (e - 1))
                s_sm.wait(vector, 16 * 32 * (e - 1))

            def step(i):
                s_ms.inc(i)
                vector.wait_ge(s_ms.h, s_ms.n)
            step(vector.tensor_scalar(mask[:, :], gloc[:, e], thr[:, e:e + 1], None, ALU.is_ge))
            step(vector.tensor_tensor_scan(pref[:, :], mask[:, :], mask[:, :], 0.0, ALU.add, ALU.bypass))
            step(vector.tensor_copy(cntc[:, 0:1], pref[:, 31:32]))

        @block.tensor
        def _(tensor, e=e):
            s_ms.wait(tensor)
            i = tensor.matmul(pOf[:, 0:1], ctx.ltm[:, :], cntc[:, 0:1], start=True, stop=True)
            s_of.inc(i)

        @block.vector
        def _(vector, e=e):
            s_of.wait(vector, e + 1)

            def step(i):
                s_ms.inc(i)
                vector.wait_ge(s_ms.h, s_ms.n)
            step(vector.tensor_copy(offs[:, :], pOf[:, 0:1]))
            step(vector.tensor_scalar(dest[:, :], pref[:, :], offs[:, 0:1], -1.0 + e * CAP, ALU.add, ALU.add))
            step(vector.scalar_tensor_tensor(dest[:, :], dest[:, :], -BIG, mask[:, :], ALU.add, ALU.mult))
            step(vector.tensor_scalar(dest[:, :], dest[:, :], BIG, None, ALU.add))
            step(vector.tensor_copy(du3[:, e % 2], dest[:, :]))
            step(vector.tensor_copy(pr3[:, e % 2, :, 0], ctx.iota32[:, :]))
            step(vector.tensor_copy(pr3[:, e % 2, :, 1], gsg[:, e]))

        @block.scalar
        def _(scalar, e=e):
            for a in range(32):
                ai = 32 * e + a
                if ai >= 4:
                    s_sc.wait(scalar, 16 * (ai - 3))
                p_ht.mark(ai, scalar.dma_start(out=htm3[:, ai % 4],
                                               in_=h_src[:, a * 128:(a + 1) * 128], transpose=True))

        @block.gpsimd
        def _(gpsimd, e=e):
            if not hasattr(ctx, "m_bcreg"):
                ctx.m_bcreg = nc.alloc_register(name="m_bcreg", engine=mybir.EngineType.Pool)
                gpsimd.reg_mov(ctx.m_bcreg, E * CAP - 1)
                ctx.m_bcreg2 = nc.alloc_register(name="m_bcreg2", engine=mybir.EngineType.Pool)
                gpsimd.reg_mov(ctx.m_bcreg2, NT + 128 - 1)
            s_em.wait(gpsimd, 16)
            gpsimd.dma_start(out=exmeta[e * CAP:(e + 1) * CAP, :]
                             .rearrange("(p f) t -> p f t", p=128),
                             in_=em0.ap().rearrange("p (f t) -> p f t", t=2)).then_inc(s_pf.h, 16)
            s_pf.n += 16
            s_ms.wait(gpsimd)
            s_pf.wait(gpsimd)
            s_ez.wait(gpsimd, 16 * (E * CAP // 128))
            for a in range(32):
                ai = 32 * e + a
                p_ht.wait(gpsimd, ai)
                s_sc.dma(gpsimd.indirect_dma_start(
                    out=ex_all[:, :],
                    out_offset=IndirectOffsetOnAxis(ap=destu[:, (e % 2) * 32 + a:(e % 2) * 32 + a + 1], axis=0),
                    in_=htm3[:, ai % 4], in_offset=None,
                    bounds_check=ctx.m_bcreg, oob_is_err=False))
                s_sm.dma(gpsimd.indirect_dma_start(
                    out=exmeta[:, :],
                    out_offset=IndirectOffsetOnAxis(ap=destu[:, (e % 2) * 32 + a:(e % 2) * 32 + a + 1], axis=0),
                    in_=pairs[:, (e % 2) * 64 + 2 * a:(e % 2) * 64 + 2 * a + 2], in_offset=None,
                    bounds_check=ctx.m_bcreg, oob_is_err=False))

    ny = 0
    nya = 0
    for e in range(E):
        @block.scalar
        def _(scalar, e=e):
            s_sc.wait(scalar, 16 * 32 * (e + 1))
            if e >= 1:
                s_h1.wait(scalar, 96 * e)
            for c in range(NCH):
                p_ex.mark(NCH * e + c, scalar.dma_start(
                    out=exv[:, c], in_=ex_all[e * CAP:(e + 1) * CAP, c * 128:(c + 1) * 128],
                    transpose=True))

        @block.sync
        def _(sync, e=e, ny=ny, nge=96 * e):
            if hasattr(ctx, "s_mwc"):
                ctx.s_mwc.wait_abs(sync, ctx.s_mwc.n)
            else:
                s_wc.wait(sync)
            if e >= 1:
                s_y.wait(sync, ny)
            s_w2.dma(sync.dma_start(out=w2v, in_=w2c[e].rearrange("(f p) m -> p f m", p=128)))
            s_b12.dma(sync.dma_start(out=b1e[0:1, :], in_=b12c[e:e + 1, 0:FF]))
            s_b12.dma(sync.dma_start(out=b2e[0:1, :], in_=b12c[e:e + 1, FF:FF + D]))
            for f in range(32):
                fi = 32 * e + f
                if fi >= 2:
                    s_ge.wait(sync, 3 * (fi - 1))
                p_w1.mark(fi, sync.dma_start(
                    out=w1v[:, fi % 2],
                    in_=w1c[e, :, f * 128:(f + 1) * 128].rearrange("(c p) m -> p c m", p=128)))
                if f >= 1:
                    s_ge.wait(sync, nge + 3 * f)
                    p_hs.mark(32 * e + f - 1, sync.dma_start(
                        out=h_big[(f - 1) * 128:f * 128, :], in_=hsv[:, (f - 1) % 2]))
            s_ge.wait(sync, nge + 96)
            p_hs.mark(32 * e + 31, sync.dma_start(
                out=h_big[31 * 128:32 * 128, :], in_=hsv[:, 31 % 2]))

        @block.tensor
        def _(tensor, e=e, nh1=96 * e):
            for c in range(NCH):
                p_ex.wait(tensor, NCH * e + c)
            s_b12.wait(tensor)
            for f in range(32):
                fi = 32 * e + f
                p_w1.wait(tensor, fi)
                for nt, (c0, c1) in enumerate(NT3):
                    g = nh1 + 3 * f + nt
                    pp = pW[g % 2]
                    if g >= 2:
                        s_ge.wait(tensor, g - 1)
                    for c in range(NCH):
                        tensor.matmul(pp[:, 0:c1 - c0], w1v[:, fi % 2, c],
                                      exv[:, c, c0:c1], start=(c == 0), stop=False)
                    i = tensor.matmul(pp[:, 0:c1 - c0],
                                      b1e[0:1, f * 128:(f + 1) * 128],
                                      ctx.ones_row[0:1, 0:c1 - c0], start=False, stop=True)
                    s_h1.inc(i)

        @block.scalar
        def _(scalar, e=e, nh1=96 * e):
            for f in range(32):
                if f >= 2:
                    p_hs.wait(scalar, 32 * e + f - 2)
                for nt, (c0, c1) in enumerate(NT3):
                    g = nh1 + 3 * f + nt
                    s_h1.wait(scalar, g + 1)
                    i = scalar.activation(hsv[:, f % 2, c0:c1], pW[g % 2][:, 0:c1 - c0], AF.Gelu)
                    s_ge.inc(i)

        @block.sync
        def _(sync, e=e):
            s_sm.wait(sync, 16 * 32 * (e + 1))
            for q in range(9):
                qi = 9 * e + q
                if qi >= 2:
                    s_mc.wait(sync, qi - 1)
                p_mt.mark(qi, sync.dma_start(out=mv[:, qi % 2],
                                             in_=exmeta[e * CAP + q * 128:e * CAP + (q + 1) * 128, :]))

        @block.vector
        def _(vector, e=e):
            for q in range(9):
                qi = 9 * e + q
                p_mt.wait(vector, qi)
                if e >= 1:
                    s_ya.wait(vector, 16 * (9 * (e - 1) + q + 1))
                vector.tensor_copy(sctu[:, q:q + 1], mv[:, qi % 2, 0:1])
                i = vector.tensor_copy(gbuf[:, q:q + 1], mv[:, qi % 2, 1:2])
                s_mc.inc(i)

        @block.sync
        def _(sync, e=e, ny=ny):
            gcum = ny
            for nt, (c0, c1) in enumerate(NT3):
                li = 3 * e + nt
                if li >= 1:
                    s_y.wait(sync, gcum)
                p_hl.mark(li, sync.dma_start(
                    out=hnv[:, :, 0:c1 - c0],
                    in_=h_big[:, c0:c1].rearrange("(f p) n -> p f n", p=128)))
                gcum += NT3_G[nt]

        @block.tensor
        def _(tensor, e=e, ny=ny):
            s_w2.wait(tensor, 16 * (e + 1))
            k = ny
            for nt, (c0, c1) in enumerate(NT3):
                li = 3 * e + nt
                p_hl.wait(tensor, li)
                for cc in range((c1 - c0 + 127) // 128):
                    for dh in range(2):
                        pp = pW[k % 2]
                        if k >= 2:
                            s_ye.wait(tensor, k - 1)
                        for f in range(32):
                            tensor.matmul(pp[:, :], hnv[:, f, cc * 128:(cc + 1) * 128],
                                          w2v[:, f, dh * 512:(dh + 1) * 512],
                                          start=(f == 0), stop=False)
                        i = tensor.matmul(pp[:, :], ctx.ones_row[0:1, 0:128],
                                          b2e[0:1, dh * 512:(dh + 1) * 512],
                                          start=False, stop=True)
                        s_y.inc(i)
                        k += 1

        @block.vector
        def _(vector, e=e, ny=ny, nya=nya):
            k = ny
            kk = 0
            for nt, (c0, c1) in enumerate(NT3):
                for cc in range((c1 - c0 + 127) // 128):
                    q = c0 // 128 + cc
                    for dh in range(2):
                        s_y.wait(vector, k + 1)
                        s_mc.wait(vector, 9 * e + q + 1)
                        if kk >= 4:
                            s_ya.wait(vector, nya + 16 * (kk // 2 - 1))
                        i = vector.tensor_scalar(yv[:, (k // 2) % 2, dh * 512:(dh + 1) * 512],
                                                 pW[k % 2][:, :], gbuf[:, q:q + 1], None, ALU.mult)
                        s_ye.inc(i)
                        k += 1
                        kk += 1

        @block.gpsimd
        def _(gpsimd, e=e, ny=ny, nya=nya):
            if e == 0:
                s_z.wait(gpsimd, 16 * 33)
            else:
                s_ya.wait(gpsimd, nya)
            k = ny
            for nt, (c0, c1) in enumerate(NT3):
                for cc in range((c1 - c0 + 127) // 128):
                    q = c0 // 128 + cc
                    s_ye.wait(gpsimd, k + 2)
                    s_mc.wait(gpsimd, 9 * e + q + 1)
                    s_ya.dma(gpsimd.indirect_dma_start(
                        out=y_moe[:, :],
                        out_offset=IndirectOffsetOnAxis(ap=sctu[:, q:q + 1], axis=0),
                        in_=yv[:, (k // 2) % 2], in_offset=None,
                        compute_op=ALU.add, oob_is_err=True, bounds_check=ctx.m_bcreg2))
                    k += 2
        ny += 18
        nya += 16 * 9

    @block.sync
    def _(sync):
        s_ya.wait(sync, nya)
        for rt in range(32):
            if rt >= 1:
                s_tr.wait(sync, 8 * rt)
            p_yl.mark(rt, sync.dma_start(out=ylv[:, 0], in_=y_moe[rt * 128:(rt + 1) * 128, :]))
            if rt >= 1:
                s_r.wait(sync, 16 * rt)
            p_x.mark(rt, sync.dma_start(out=x3, in_=fm3(x_src)[:, :, rt * 128:(rt + 1) * 128]))

    @block.tensor
    def _(tensor):
        for rt in range(32):
            p_yl.wait(tensor, rt)
            for c in range(NCH):
                g = 8 * rt + c
                if g >= 2:
                    s_rx.wait(tensor, g - 1)
                i = tensor.transpose(pTr[g % 2][:, 0:128], ylv[:, 0, c * 128:(c + 1) * 128],
                                     ctx.ident[:, :])
                s_tr.inc(i)

    @block.vector
    def _(vector):
        for rt in range(32):
            p_x.wait(vector, rt)
            for c in range(NCH):
                g = 8 * rt + c
                s_tr.wait(vector, g + 1)
                i = vector.scalar_tensor_tensor(x3[:, c], pTr[g % 2][:, 0:128], 0.5,
                                                x3[:, c], ALU.mult, ALU.add)
                s_rx.inc(i)

    @block.gpsimd
    def _(gpsimd):
        for rt in range(32):
            s_rx.wait(gpsimd, 8 * (rt + 1))
            s_r.dma(gpsimd.dma_start(out=fm3(x_dst)[:, :, rt * 128:(rt + 1) * 128], in_=x3))

    ctx.phase_done = (s_r, s_r.n)



def finish(ctx):
    s, n = ctx.phase_done
    block = ctx.block

    @block.sync
    def _(sync):
        s.wait_abs(sync, n)


def build_ffn0(weights, consts, gates=False, ln_only=False, nsteps=NB):
    nc = _mknc()
    x_in = nc.dram_tensor("x_in", [D, NT], F32, kind="ExternalInput").ap()
    if ln_only:
        h_dram = nc.dram_tensor("h_out", [D, NT], BF16, kind="ExternalOutput").ap()
    else:
        x_out = nc.dram_tensor("x_out", [D, NT], F32, kind="ExternalOutput").ap()
        h_dram = nc.dram_tensor("h_dram", [D, NT], BF16, kind="Internal").ap()
    es = contextlib.ExitStack()
    with es:
        block = es.enter_context(nc.Block())
        ctx = Ctx()
        ctx.nc, ctx.es, ctx.block, ctx.uid, ctx.nbar = nc, es, block, "", 0
        ctx.win = {}
        wneed = [] if ln_only else [k for k in weights if k.startswith("l0_") and not k.endswith(("wq", "wk", "wv"))]
        for name in wneed:
            ctx.win[name] = nc.dram_tensor(name, list(weights[name].shape), F32, kind="ExternalInput").ap()
        emit_const_loads(ctx, consts)
        emit_ln(ctx, x_in, h_dram, gates=False)
        if not ln_only:
            emit_l0ffn(ctx, h_dram, x_in, x_out, nsteps=nsteps)
            finish(ctx)
        else:
            @block.sync
            def _(sync):
                ln_wait_cols(ctx, sync, NT)
    return nc, wneed


def build_attn0(weights, consts):
    nc = _mknc()
    x_in = nc.dram_tensor("x_in", [D, NT], F32, kind="ExternalInput").ap()
    x_out = nc.dram_tensor("x_out", [D, NT], F32, kind="ExternalOutput").ap()
    h_dram = nc.dram_tensor("h_dram", [D, NT], BF16, kind="Internal").ap()
    es = contextlib.ExitStack()
    with es:
        block = es.enter_context(nc.Block())
        ctx = Ctx()
        ctx.nc, ctx.es, ctx.block, ctx.uid, ctx.nbar = nc, es, block, "", 0
        ctx.win = {}
        wneed = ["l0_wq", "l0_wk", "l0_wv"]
        for name in wneed:
            ctx.win[name] = nc.dram_tensor(name, list(weights[name].shape), F32, kind="ExternalInput").ap()
        emit_const_loads(ctx, consts)
        emit_ln(ctx, x_in, h_dram, gates=False)
        emit_l0attn(ctx, h_dram, x_in, x_out)
        finish(ctx)
    return nc, wneed


def build_attn1(weights, consts):
    nc = _mknc()
    x_in = nc.dram_tensor("x_in", [D, NT], F32, kind="ExternalInput").ap()
    x_out = nc.dram_tensor("x_out", [D, NT], F32, kind="ExternalOutput").ap()
    h_dram = nc.dram_tensor("h_dram", [D, NT], BF16, kind="Internal").ap()
    es = contextlib.ExitStack()
    with es:
        block = es.enter_context(nc.Block())
        ctx = Ctx()
        ctx.nc, ctx.es, ctx.block, ctx.uid, ctx.nbar = nc, es, block, "", 0
        ctx.win = {}
        wneed = ["l1_wq", "l1_wk", "l1_wv", "l1_wo", "l1_bq", "l1_bk", "l1_bv", "l1_bo",
                 "l1_qns", "l1_qnb", "l1_kns", "l1_knb", "l1_initb", "l1_gamma"]
        for name in wneed:
            ctx.win[name] = nc.dram_tensor(name, list(weights[name].shape), F32, kind="ExternalInput").ap()
        emit_const_loads(ctx, consts)
        emit_ln(ctx, x_in, h_dram, gates=False)
        emit_l1attn(ctx, h_dram, x_in, x_out)
        finish(ctx)
    return nc, wneed


def build_moe(weights, consts):
    nc = _mknc()
    x_in = nc.dram_tensor("x_in", [D, NT], F32, kind="ExternalInput").ap()
    x_out = nc.dram_tensor("x_out", [D, NT], F32, kind="ExternalOutput").ap()
    h_dram = nc.dram_tensor("h_dram", [D, NT], BF16, kind="Internal").ap()
    es = contextlib.ExitStack()
    with es:
        block = es.enter_context(nc.Block())
        ctx = Ctx()
        ctx.nc, ctx.es, ctx.block, ctx.uid, ctx.nbar = nc, es, block, "", 0
        ctx.win = {}
        wneed = ["l1_wg", "l1_w1", "l1_b1", "l1_w2", "l1_b2"]
        for name in wneed:
            ctx.win[name] = nc.dram_tensor(name, list(weights[name].shape), F32, kind="ExternalInput").ap()
        ctx.ag_gin = nc.dram_tensor("m_aggin", [NT, E], F32, kind="Internal").ap()
        emit_const_loads(ctx, consts)
        emit_ln(ctx, x_in, h_dram, gates=True)
        emit_moe(ctx, h_dram, x_in, x_out)
        finish(ctx)
    return nc, wneed



_PROG_CACHE = {}


def _get_prog(kind, weights, consts0):
    if kind in _PROG_CACHE:
        return _PROG_CACHE[kind]
    if kind == "ffn0":
        nc, wneed = build_ffn0(weights, consts0)
    elif kind == "attn0":
        nc, wneed = build_attn0(weights, consts0)
    elif kind == "attn1":
        nc, wneed = build_attn1(weights, consts0)
    elif kind == "moe":
        nc, wneed = build_moe(weights, consts0)
    _PROG_CACHE[kind] = (nc, wneed)
    return _PROG_CACHE[kind]


def kernel_unfused(**inputs):
    w = host_weights(inputs)
    consts = [host_constants(c) for c in range(NCORE)]
    xs = shard_x(np.asarray(inputs["L0_attn_x"], np.float32))
    for kind in ("ffn0", "attn0", "ffn0", "moe", "attn1", "moe"):
        nc, wneed = _get_prog(kind, w, consts[0])
        in_maps = []
        for c in range(NCORE):
            m = {"x_in": xs[c]}
            for n in wneed:
                m[n] = w[n]
            m.update(consts[c])
            in_maps.append(m)
        res = run_bass_kernel_spmd(nc, in_maps, core_ids=list(range(NCORE)))
        xs = [np.asarray(res.results[c]["x_out"], np.float32) for c in range(NCORE)]
    return unshard_x(xs).astype(np.float32)


def kernel(**inputs):
    w = host_weights(inputs)
    consts = [host_constants(c) for c in range(NCORE)]
    xs = shard_x(np.asarray(inputs["L0_attn_x"], np.float32))
    if "fused" not in _PROG_CACHE:
        _PROG_CACHE["fused"] = build_fused(w, consts[0])
    nc = _PROG_CACHE["fused"]
    in_maps = []
    for c in range(NCORE):
        m = {"x_in": xs[c]}
        for n in w:
            m[n] = w[n]
        m.update(consts[c])
        in_maps.append(m)
    res = run_bass_kernel_spmd(nc, in_maps, core_ids=list(range(NCORE)))
    xs = [np.asarray(res.results[c]["x_out"], np.float32) for c in range(NCORE)]
    return unshard_x(xs).astype(np.float32)



def emit_moe_prefill(ctx):
    nc, block = ctx.nc, ctx.block
    ag_gout = nc.dram_tensor("m_aggout", [NCORE * NT, E], F32, kind="Internal").ap()
    ex_all = nc.dram_tensor("m_exall", [E * CAP, D], BF16, kind="Internal").ap()
    exmeta = nc.dram_tensor("m_exmeta", [E * CAP, 2], F32, kind="Internal").ap()
    h_big = nc.dram_tensor("m_hbig", [FF, CAP], BF16, kind="Internal").ap()
    y_moe = nc.dram_tensor("m_ymoe", [NT + 128, D], F32, kind="Internal").ap()
    w1c = nc.dram_tensor("m_w1c", [E, D, FF], BF16, kind="Internal").ap()
    w2c = nc.dram_tensor("m_w2c", [E, FF, D], BF16, kind="Internal").ap()
    b12c = nc.dram_tensor("m_b12c", [E, FF + D], BF16, kind="Internal").ap()
    ctx.m_dram = (ag_gout, ex_all, exmeta, h_big, y_moe, w1c, w2c, b12c)
    ctx.m_wcached = True
    ctx.s_mwc = Sem(ctx, "s_mwc", kind="sw")
    ctx.m_prefill_pending = True


def build_fused(weights, consts):
    nc = _mknc()
    x_in = nc.dram_tensor("x_in", [D, NT], F32, kind="ExternalInput").ap()
    x_out = nc.dram_tensor("x_out", [D, NT], F32, kind="ExternalOutput").ap()
    xa = nc.dram_tensor("xa", [D, NT], F32, kind="Internal").ap()
    xbd = nc.dram_tensor("xbd", [D, NT], F32, kind="Internal").ap()
    h_dram = nc.dram_tensor("h_dram", [D, NT], BF16, kind="Internal").ap()
    es = contextlib.ExitStack()
    with es:
        block = es.enter_context(nc.Block())
        ctx = Ctx()
        ctx.nc, ctx.es, ctx.block, ctx.uid, ctx.nbar = nc, es, block, "", 0
        ctx.fused = True
        ctx.phase_key = None
        ctx.win = {}
        for name in weights:
            ctx.win[name] = nc.dram_tensor(name, list(weights[name].shape), F32, kind="ExternalInput").ap()
        ctx.ag_gin = nc.dram_tensor("m_aggin", [NT, E], F32, kind="Internal").ap()
        emit_const_loads(ctx, consts)
        emit_moe_prefill(ctx)
        rem = nc.sbuf_bytes_remaining
        ctx.arena_size = rem - 512
        arena_t = ctx.es.enter_context(nc.sbuf_tensor("ARENA", [128, ctx.arena_size], mybir.dt.uint8))
        mloc = nc.lookup_mloc(arena_t)
        ctx.arena_base = mloc.addr

        begin_phase(ctx, "ffn0")
        emit_ln(ctx, x_in, h_dram, gates=False)
        emit_l0ffn(ctx, h_dram, x_in, xa)
        end_phase(ctx)

        begin_phase(ctx, "attn0")
        emit_ln(ctx, xa, h_dram, gates=False)
        emit_l0attn(ctx, h_dram, xa, xbd)
        end_phase(ctx)

        begin_phase(ctx, "ffn0")
        emit_ln(ctx, xbd, h_dram, gates=False)
        emit_l0ffn(ctx, h_dram, xbd, xa)
        end_phase(ctx)

        begin_phase(ctx, "moe")
        emit_ln(ctx, xa, h_dram, gates=True)
        emit_moe(ctx, h_dram, xa, xbd)
        end_phase(ctx)

        begin_phase(ctx, "attn1")
        emit_ln(ctx, xbd, h_dram, gates=False)
        emit_l1attn(ctx, h_dram, xbd, xa)
        end_phase(ctx)

        begin_phase(ctx, "moe")
        emit_ln(ctx, xa, h_dram, gates=True)
        emit_moe(ctx, h_dram, xa, x_out)
        end_phase(ctx)
    return nc
